# version 02
# baseline (speedup 1.0000x reference)
import contextlib
import numpy as np
import ml_dtypes
import concourse.bass as bass
import concourse.mybir as mybir
from concourse.bass_utils import run_bass_kernel_spmd

F32 = mybir.dt.float32
BF16 = mybir.dt.bfloat16
AF = mybir.ActivationFunctionType
ALU = mybir.AluOpType
AX = mybir.AxisListType
BFNP = ml_dtypes.bfloat16

NCORES = 8
R = 2
TS = 2048
D = 1024
NT = R * TS
NTILES = NT // 128
H = 16
HD = 64
EPS = 1e-5



class Sem:
    __slots__ = ("h", "v", "name")

    def __init__(self, h, name):
        self.h = h
        self.v = 0
        self.name = name


STREAMS = ("tensor", "vector", "scalar", "gpsimd", "sync")


class Ctx:
    def __init__(self, nc, es):
        self.nc = nc
        self.es = es
        self.streams = {k: [] for k in STREAMS}
        self._sems = {}
        self._lastwait = {}
        self.serial = False
        self._last = {}

    def sem(self, name):
        s = self._sems.get(name)
        if s is None:
            h = self.es.enter_context(self.nc.semaphore(name))
            s = Sem(h, name)
            self._sems[name] = s
        return s

    def op(self, stream, fn, waits=(), inc=None, inc_k=1):
        if self.serial:
            waits = list(waits) + [sv for st2, sv in self._last.items()
                                   if st2 != stream]
        w = []
        for s, v in waits:
            key = (stream, s.name)
            if self._lastwait.get(key, -1) >= v:
                continue
            self._lastwait[key] = v
            w.append((s.h, v))
        tgt = None
        if inc is not None:
            inc.v += inc_k
            tgt = (inc.h, inc_k)
            if self.serial:
                self._last[stream] = (inc, inc.v)
        ret = inc.v if inc is not None else None

        def run(eng, w=w, fn=fn, tgt=tgt):
            for hh, vv in w:
                eng.wait_ge(hh, vv)
            r = fn(eng)
            if tgt is not None:
                if r is None:
                    r = eng.nop()
                r.then_inc(tgt[0], tgt[1])
        self.streams[stream].append(run)
        return ret

    def chain(self, stream, fns, waits=(), inc=None, inc_k=1):
        s = self.sem("chain_" + stream)
        base = s.v
        s.v += len(fns) - 1

        def run_all(e, fns=list(fns), base=base, h=s.h):
            r = None
            for i, fn in enumerate(fns):
                if i:
                    e.wait_ge(h, base + i)
                r = fn(e)
                if i < len(fns) - 1:
                    r.then_inc(h, 1)
            return r
        return self.op(stream, run_all, waits=waits, inc=inc, inc_k=inc_k)

    def barrier(self, tag):
        sems = [self.sem(f"bar_{st}") for st in STREAMS]
        vals = []
        for st, s in zip(STREAMS, sems):
            vals.append(self.op(st, lambda e: e.nop(), inc=s))
        for st in STREAMS:
            self.op(st, lambda e: None,
                    waits=[(s, v) for s, v in zip(sems, vals)])


class Ring:

    def __init__(self, ctx, name, nslots, prod_step=1, cons_step=1):
        self.ctx = ctx
        self.n = nslots
        self.filled = ctx.sem(name + "_f")
        self.freed = ctx.sem(name + "_e")
        self.prod_step = prod_step
        self.cons_step = cons_step
        self.fill_tok = []
        self.free_tok = []
        self.i_prod = 0
        self.i_cons = 0

    def prod_waits(self):
        i = self.i_prod
        if i >= self.n:
            return [(self.freed, self.free_tok[i - self.n])]
        return []

    def prod_done(self):
        self.fill_tok.append(self.filled.v)
        self.i_prod += 1

    def cons_waits(self):
        return [(self.filled, self.fill_tok[self.i_cons])]

    def cons_done(self):
        self.free_tok.append(self.freed.v)
        self.i_cons += 1


class CastLoader:

    def __init__(self, ctx, es, name, cols32, nh=2):
        self.ctx = ctx
        self.stage = es.enter_context(
            ctx.nc.sbuf_tensor(name + "_stg", [128, nh * cols32], F32))
        self.cols32 = cols32
        self.nh = nh
        self.fill = ctx.sem(name + "_lf")
        self.done = ctx.sem(name + "_lc")
        self.i = 0
        self.cast_toks = []

    def load(self, dst, src, cols, ring_waits=(), nparts=128):
        assert cols <= self.cols32
        sh = (self.i % self.nh) * self.cols32
        w = []
        if self.i >= self.nh:
            w.append((self.done, self.cast_toks[self.i - self.nh]))
        stg = self.stage[:nparts, sh:sh + cols]
        nd = len(src.shape)
        if nd > 2:
            names = [f"d{j}" for j in range(nd - 1)]
            spec = "p (" + " ".join(names) + ") -> p " + " ".join(names)
            kw = {names[j]: src.shape[1 + j] for j in range(nd - 2)}
            stg_in = stg.rearrange(spec, **kw)
        else:
            stg_in = stg
        t_l = self.ctx.op("sync", lambda e: e.dma_start(out=stg_in, in_=src),
                          waits=w, inc=self.fill, inc_k=16)
        t_c = self.ctx.op("scalar", lambda e: e.copy(dst, stg),
                          waits=[(self.fill, t_l)] + list(ring_waits),
                          inc=self.done)
        self.cast_toks.append(t_c)
        self.i += 1
        return t_c


def bcast_pp(ctx, ps_ap, ones128, src_ap, waits=(), inc=None):
    return ctx.op("tensor",
                  lambda e: e.matmul(ps_ap, ones128, src_ap, start=True, stop=True),
                  waits=waits, inc=inc)




def _T(a):
    return np.ascontiguousarray(np.swapaxes(np.asarray(a), -1, -2))


def _C(a):
    return np.ascontiguousarray(np.asarray(a))


def stage_weights(i):
    W = {}
    W["c_ident"] = np.eye(128, dtype=BFNP)
    W["c_identf"] = np.eye(128, dtype=np.float32)
    W["c_eps"] = np.full((128, 1), EPS, dtype=np.float32)
    W["c_negbig"] = np.full((128, 1), -1e30, dtype=np.float32)
    W["c_ones1"] = np.ones((1, 128), dtype=BFNP)
    W["c_ones128"] = np.ones((128, 1), dtype=BFNP)
    W["c_triu"] = np.triu(np.ones((128, 128), dtype=np.float32)).astype(BFNP)
    cm = np.zeros((4, 128, 512), dtype=np.float32)
    for oi, off in enumerate((0, 128, 256, 384)):
        cm[oi] = (np.arange(128)[:, None] + off <= np.arange(512)[None, :])
    W["c_causal"] = cm.astype(BFNP)
    import math
    pos = np.arange(TS, dtype=np.float64)
    freqs = np.exp(np.arange(0, HD, 2, dtype=np.float64) * (-math.log(10000.0) / HD))
    ang = pos[:, None] * freqs[None, :]
    cosm = np.cos(ang).astype(np.float32)
    sinm = np.sin(ang).astype(np.float32)
    W["c_rope_cos"] = np.tile(cosm, (1, H))
    W["c_rope_sin"] = np.tile(-sinm, (1, H))

    W["l0a_wqT"] = _T(i["L0_attn_Wq"])
    W["l0a_wkT"] = _T(i["L0_attn_Wk"])
    W["l0a_wvT"] = _T(i["L0_attn_Wv"])
    W["l0a_woT"] = _T(i["L0_attn_Wo"])
    W["l0a_qg_full"] = np.ascontiguousarray(np.tile(np.asarray(i["L0_attn_q_gamma"]), H)).reshape(1, D)
    W["l0a_qb_full"] = np.ascontiguousarray(np.tile(np.asarray(i["L0_attn_q_beta"]), H)).reshape(1, D)
    W["l0a_kg_full"] = np.ascontiguousarray(np.tile(np.asarray(i["L0_attn_k_gamma"]), H)).reshape(1, D)
    W["l0a_kb_full"] = np.ascontiguousarray(np.tile(np.asarray(i["L0_attn_k_beta"]), H)).reshape(1, D)
    W["l0f_w12T"] = _T(i["L0_ffn_w12"])
    W["l0f_w3T"] = _T(i["L0_ffn_w3"])
    W["l0f_sw1T"] = _T(i["L0_ffn_shared_w1"])
    W["l0f_sw2T"] = _T(i["L0_ffn_shared_w2"])
    W["l0f_sw3T"] = _T(i["L0_ffn_shared_w3"])
    W["l0f_rwT"] = _T(i["L0_ffn_router_w"])
    W["l0f_eb"] = _C(i["L0_ffn_expert_bias"]).reshape(1, 8)
    W["l1a_wq"] = _C(i["L1_attn_Wq"])
    W["l1a_wk"] = _C(i["L1_attn_Wk"])
    W["l1a_wv"] = _C(i["L1_attn_Wv"])
    W["l1a_wo"] = _C(i["L1_attn_Wo"])
    W["l1a_bq"] = _C(i["L1_attn_bq"]).reshape(1, D)
    W["l1a_bk"] = _C(i["L1_attn_bk"]).reshape(1, D)
    W["l1a_bv"] = _C(i["L1_attn_bv"]).reshape(1, D)
    W["l1a_bo"] = _C(i["L1_attn_bo"]).reshape(1, D)
    rel = np.asarray(i["L1_attn_rel_table"])
    ext = rel[np.minimum(np.arange(4095), rel.shape[0] - 1)]
    idx = np.clip(np.arange(4096)[None, :] + 127 - np.arange(128)[:, None], 0, 4094)
    W["l1a_toep"] = np.ascontiguousarray(ext[idx].transpose(2, 0, 1))
    W["l1f_rw"] = _C(i["L1_ffn_router_w"])
    W["l1f_rb"] = _C(i["L1_ffn_router_b"]).reshape(1, 64)
    w1 = np.asarray(i["L1_ffn_w1"])
    W["l1f_w1p"] = np.ascontiguousarray(
        w1.reshape(32, 2, 1024, 256).transpose(0, 2, 1, 3).reshape(32, 1024, 512))
    b1 = np.asarray(i["L1_ffn_b1"])
    W["l1f_b1p"] = np.ascontiguousarray(b1.reshape(32, 1, 512))
    W["l1f_w2"] = _C(i["L1_ffn_w2"])
    W["l1f_b2m"] = _C(i["L1_ffn_b2"])
    W["l1f_sw1"] = _C(i["L1_ffn_sw1"])
    W["l1f_sb1"] = _C(i["L1_ffn_sb1"]).reshape(2, 1, 2048)
    W["l1f_sw2"] = _C(i["L1_ffn_sw2"])
    W["l1f_sb2"] = _C(i["L1_ffn_sb2"]).reshape(2, 1, 1024)
    W["l2a_wqT"] = _T(i["L2_attn_Wq"])
    W["l2a_wkT"] = _T(i["L2_attn_Wk"])
    W["l2a_wvT"] = _T(i["L2_attn_Wv"])
    W["l2a_woT"] = _T(i["L2_attn_Wo"])
    W["l2f_rwT"] = _T(i["L2_ffn_router_w"])
    W["l2f_rb"] = _C(i["L2_ffn_router_b"]).reshape(2, 1, 8)
    W["l2f_ewT"] = _T(i["L2_ffn_expert_w"])
    W["l2f_eb"] = _C(i["L2_ffn_expert_b"])
    W["l3a_wq"] = _C(i["L3_attn_Wq"])
    W["l3a_wk"] = _C(i["L3_attn_Wk"])
    W["l3a_wv"] = _C(i["L3_attn_Wv"])
    W["l3f_g1"] = _C(i["L3_ffn_ln1_g"]).reshape(1, D)
    W["l3f_b1"] = _C(i["L3_ffn_ln1_b"]).reshape(1, D)
    W["l3f_g2"] = _C(i["L3_ffn_ln2_g"]).reshape(1, D)
    W["l3f_b2"] = _C(i["L3_ffn_ln2_b"]).reshape(1, D)
    W["l3f_wq"] = _C(i["L3_ffn_Wq"])
    W["l3f_wk"] = _C(i["L3_ffn_Wk"])
    W["l3f_wv"] = _C(i["L3_ffn_Wv"])
    W["l3f_wo"] = _C(i["L3_ffn_Wo"])
    W["l3f_bq"] = _C(i["L3_ffn_bq"]).reshape(1, D)
    W["l3f_bk"] = _C(i["L3_ffn_bk"]).reshape(1, D)
    W["l3f_bv"] = _C(i["L3_ffn_bv"]).reshape(1, D)
    W["l3f_bo"] = _C(i["L3_ffn_bo"]).reshape(1, D)
    W["l3f_wr"] = _C(i["L3_ffn_Wr"])
    W["l3f_br"] = _C(i["L3_ffn_br"]).reshape(1, 8)
    W["l3f_ew1"] = _C(i["L3_ffn_e_w1"])
    eb1 = np.asarray(i["L3_ffn_e_b1"])
    W["l3f_eb1T"] = np.ascontiguousarray(eb1.reshape(8, 32, 128).transpose(0, 2, 1))
    W["l3f_ew2"] = _C(i["L3_ffn_e_w2"])
    W["l3f_eb2"] = _C(i["L3_ffn_e_b2"])
    return W



class Common:

    def __init__(self, ctx, es, wap):
        nc = ctx.nc
        self.ident = es.enter_context(nc.sbuf_tensor("cc_ident", [128, 128], BF16))
        self.ones1 = es.enter_context(nc.sbuf_tensor("cc_ones1", [1, 128], BF16))
        self.ones128 = es.enter_context(nc.sbuf_tensor("cc_ones128", [128, 1], BF16))
        self.junk = es.enter_context(nc.sbuf_tensor("cc_junk", [128, 1024], BF16))
        self.identf = es.enter_context(nc.sbuf_tensor("cc_identf", [128, 128], F32))
        self.causal = es.enter_context(nc.sbuf_tensor("cc_causal", [128, 4 * 512], BF16))
        self.eps = es.enter_context(nc.sbuf_tensor("cc_eps", [128, 1], F32))
        self.negbig = es.enter_context(nc.sbuf_tensor("cc_negbig", [128, 1], F32))
        self.ldc = ctx.sem("ldc")
        for dst, src in ((self.ident, wap["c_ident"]), (self.ones1, wap["c_ones1"]),
                         (self.ones128, wap["c_ones128"]), (self.eps, wap["c_eps"]),
                         (self.negbig, wap["c_negbig"]), (self.identf, wap["c_identf"])):
            ctx.op("sync", lambda e, d=dst, s=src: e.dma_start(out=d[:], in_=s),
                   inc=self.ldc, inc_k=16)
        ctx.op("sync", lambda e: e.dma_start(
            out=self.causal[:].rearrange("p (a n) -> p a n", a=4),
            in_=wap["c_causal"].rearrange("a p n -> p a n")),
            inc=self.ldc, inc_k=16)
        self.ld_tok = self.ldc.v


def emit_ln_tile(ctx, cm, x_ap, h_dst, sv, dve_waits, sc_sem, dve_sem,
                 junk=None):
    s_col = sv[:, 0:1]
    mean_col = sv[:, 1:2]
    nmean_col = sv[:, 2:3]
    ssq_col = sv[:, 3:4]
    rstd_col = sv[:, 4:5]
    dcols = x_ap.shape[-1]
    junk = junk if junk is not None else cm.junk

    t1 = ctx.chain("vector", [
        lambda e: e.tensor_reduce(s_col, x_ap, AX.X, ALU.add),
        lambda e: e.tensor_scalar_mul(mean_col, s_col, 1.0 / dcols),
        lambda e: e.tensor_scalar_mul(nmean_col, s_col, -1.0 / dcols),
    ], waits=dve_waits, inc=dve_sem)

    t2 = ctx.chain("scalar", [
        lambda e: e.activation(junk[:, 0:dcols], x_ap, AF.Square, bias=nmean_col,
                               accum_out=ssq_col),
        lambda e: e.activation(ssq_col, ssq_col, AF.Ln, bias=cm.eps[:, 0:1],
                               scale=1.0 / dcols),
        lambda e: e.activation(rstd_col, ssq_col, AF.Exp, scale=-0.5),
    ], waits=[(dve_sem, t1)], inc=sc_sem)

    def dve2(e):
        return e.tensor_scalar(h_dst, x_ap, mean_col, rstd_col,
                               ALU.subtract, ALU.mult)
    t3 = ctx.op("vector", dve2, waits=[(sc_sem, t2)], inc=dve_sem)
    return t2, t3


def emit_transpose_h(ctx, cm, h_src, ps_tr, hT_dst, nchunk, pe_waits, pe_sem,
                     dve_sem, evac_stream="vector"):
    def pe(e):
        r = None
        for c in range(nchunk):
            r = e.transpose(ps_tr[:, c * 128:(c + 1) * 128],
                            h_src[:, c * 128:(c + 1) * 128], cm.ident[:])
        return r
    t = ctx.op("tensor", pe, waits=pe_waits, inc=pe_sem)
    tok = ctx.op(evac_stream, lambda e: e.tensor_copy(hT_dst, ps_tr[:, 0:nchunk * 128]),
                 waits=[(pe_sem, t)], inc=dve_sem)
    return tok


def emit_gemm_acc(ctx, psum_ap, hT_slices, w_slices, extra=(), waits=()):
    def pe(e):
        r = None
        n = len(hT_slices)
        for c in range(n):
            r = e.matmul(psum_ap, hT_slices[c], w_slices[c],
                         start=(c == 0), stop=(c == n - 1 and not extra))
        for j, (lh, rh) in enumerate(extra):
            r = e.matmul(psum_ap, lh, rh, start=False, stop=(j == len(extra) - 1))
        return r
    return pe, waits



def emit_l2_ffn(ctx, cm, wap, x_src, x_dst):
    nc = ctx.nc
    STT = 8
    NST = NTILES // STT

    with contextlib.ExitStack() as es:
        x_st = es.enter_context(nc.sbuf_tensor("p5_x", [128, STT * D], F32))
        acc0 = es.enter_context(nc.sbuf_tensor("p5_acc0", [128, STT * D], F32))
        hT = es.enter_context(nc.sbuf_tensor("p5_hT", [128, STT * D], BF16))
        hb = es.enter_context(nc.sbuf_tensor("p5_hb", [128, D], BF16))
        lg = es.enter_context(nc.sbuf_tensor("p5_lg", [128, STT * 8], F32))
        msk = es.enter_context(nc.sbuf_tensor("p5_msk", [128, STT * 8], F32))
        mskb = es.enter_context(nc.sbuf_tensor("p5_mskb", [128, 8], BF16))
        mskT = es.enter_context(nc.sbuf_tensor("p5_mskT", [8, STT * 128], BF16))
        sv = es.enter_context(nc.sbuf_tensor("p5_sv", [128, STT * 8], F32))
        tmp8 = es.enter_context(nc.sbuf_tensor("p5_tmp8", [128, 16], F32))
        rw_sb = es.enter_context(nc.sbuf_tensor("p5_rw", [128, 2 * 64], BF16))
        rb_sb = es.enter_context(nc.sbuf_tensor("p5_rb", [1, 16], BF16))
        eb_sb = es.enter_context(nc.sbuf_tensor("p5_eb", [8, 2 * D], BF16))
        wslot = es.enter_context(nc.sbuf_tensor("p5_w", [128, 3 * 8 * D], BF16))
        ps_tr = es.enter_context(nc.psum_tensor("p5_pstr", [128, 1024], BF16))
        ps_mt = es.enter_context(nc.psum_tensor("p5_psmt", [128, 512], BF16))
        ps_r = es.enter_context(nc.psum_tensor("p5_psr", [128, 8], F32))
        ps_o = [es.enter_context(nc.psum_tensor(f"p5_pso{i}", [128, D], F32))
                for i in range(2)]

        ld = ctx.sem("p5_ld")
        pe = ctx.sem("p5_pe")
        ve = ctx.sem("p5_ve")
        se = ctx.sem("p5_se")
        st_s = ctx.sem("p5_st")
        loader = CastLoader(ctx, es, "p5", 4096, 2)
        gld = loader.done
        loader.load(rw_sb[:], wap["l2f_rwT"].rearrange("l (c p) n -> p l c n", p=128), 128)
        loader.load(rb_sb[:], wap["l2f_rb"].rearrange("l o n -> o l n"), 16, nparts=1)
        tok_static = loader.load(
            eb_sb[:], wap["l2f_eb"].rearrange("l p n -> p l n"), 2048, nparts=8)

        ewT = wap["l2f_ewT"]
        efill = []
        efree = []
        NSLOT = 3

        def emit_expert_load(l, e_):
            i = len(efill)
            slot = i % NSLOT
            pw = [efree[i - NSLOT]] if i >= NSLOT else []
            t = None
            for half in range(2):
                src = ewT[l, e_, half * 512:(half + 1) * 512, :]
                src = src.rearrange("(c p) n -> p c n", p=128)
                t = loader.load(wslot[:, slot * 8 * D + half * 4 * D:
                                      slot * 8 * D + (half + 1) * 4 * D],
                                src, 4096,
                                ring_waits=pw if half == 0 else ())
            efill.append(t)

        nload = 0
        total_loads = NST * 2 * 8
        for _ in range(3):
            emit_expert_load(nload // 8 % 2, nload % 8)
            nload += 1

        def router_and_mask(l, ti, hT_base, lgs, msks, pe_src_tok):
            rw_off = l * 64
            def mm(e):
                r = None
                for c in range(8):
                    r = e.matmul(ps_r[:], hT[:, hT_base + c * 128: hT_base + (c + 1) * 128],
                                 rw_sb[:, rw_off + c * 8: rw_off + (c + 1) * 8],
                                 start=(c == 0), stop=False)
                return e.matmul(ps_r[:], cm.ones1[:], rb_sb[:, l * 8:(l + 1) * 8],
                                start=False, stop=True)
            t_mm = ctx.op("tensor", mm,
                          waits=[(ve, pe_src_tok), (gld, tok_static)],
                          inc=pe)
            lgc = lg[:, lgs:lgs + 8]
            mskc = msk[:, msks:msks + 8]
            m1 = tmp8[:, 0:1]
            m2 = tmp8[:, 1:2]
            eqm = tmp8[:, 8:16]

            t_d = ctx.chain("vector", [
                lambda e: e.tensor_copy(lgc, ps_r[:]),
                lambda e: e.tensor_reduce(m1, lgc, AX.X, ALU.max),
                lambda e: e.tensor_scalar(eqm, lgc, m1, None, ALU.is_ge),
                lambda e: e.scalar_tensor_tensor(eqm, eqm, cm.negbig[:], lgc,
                                                 ALU.mult, ALU.add),
                lambda e: e.tensor_reduce(m2, eqm, AX.X, ALU.max),
                lambda e: e.tensor_scalar(mskc, lgc, m2, None, ALU.is_ge),
                lambda e: e.tensor_copy(mskb[:], mskc),
            ], waits=[(pe, t_mm), (cm.ldc, cm.ld_tok)], inc=ve)
            t_p = ctx.op("tensor",
                         lambda e: e.transpose(ps_mt[0:8, 0:128], mskb[:], cm.ident[:]),
                         waits=[(ve, t_d)], inc=pe)
            t_e = ctx.op("vector",
                         lambda e: e.tensor_copy(mskT[:8, ti * 128:(ti + 1) * 128], ps_mt[:8, 0:128]),
                         waits=[(pe, t_p)], inc=ve)
            return t_e

        ps_free_tok = [0, 0]
        tr_free_tok = 0
        store_toks = {}

        import os
        _dbg_bar = os.environ.get("K_DEBUG_BARRIER", "0") == "1"
        for s in range(NST):
            if _dbg_bar and s:
                ctx.barrier(f"dbg{s}")
            xw = []
            for ti in range(STT):
                g = NTILES // NST * s + ti
                tok = ctx.op("sync", lambda e, ti=ti, g=g: e.dma_start(
                    out=x_st[:, ti * D:(ti + 1) * D], in_=x_src[g * 128:(g + 1) * 128, :]),
                    waits=[(st_s, store_toks[(s - 1, ti)])] if s else [],
                    inc=ld, inc_k=16)
                xw.append(tok)
            mskA_toks = []
            mskB_toks = []
            for ti in range(STT):
                sc_t, h_t = emit_ln_tile(
                    ctx, cm, x_st[:, ti * D:(ti + 1) * D], hb[:],
                    sv[:, ti * 8: ti * 8 + 8], [(ld, xw[ti])], se, ve)
                tr_t = emit_transpose_h(
                    ctx, cm, hb[:], ps_tr, hT[:, ti * D:(ti + 1) * D], 8,
                    [(ve, h_t), (ve, tr_free_tok), (ldc := cm.ldc, cm.ld_tok)], pe, ve)
                tr_free_tok = tr_t
                mskA_toks.append(router_and_mask(0, ti, ti * D, ti * 8, ti * 8, tr_t))
            hT_ready = tr_free_tok
            for e_ in range(8):
                icons = len(efree)
                slot = icons % NSLOT
                cw = [(gld, efill[icons])]
                for ti in range(STT):
                    ps = ps_o[ti % 2]
                    def mm(e, ti=ti, slot=slot, ps=ps):
                        r = None
                        for n in range(2):
                            for c in range(8):
                                r = e.matmul(ps[:, n * 512:(n + 1) * 512],
                                             hT[:, ti * D + c * 128: ti * D + (c + 1) * 128],
                                             wslot[:, slot * 8 * D + c * D + n * 512:
                                                   slot * 8 * D + c * D + (n + 1) * 512],
                                             start=(c == 0), stop=(c == 7))
                        return r
                    t_mm = ctx.op("tensor", mm,
                                  waits=list(cw) + [(ve, mskA_toks[ti]), (ve, ps_free_tok[ti % 2])],
                                  inc=pe)
                    mcol = msk[:, ti * 8 + e_: ti * 8 + e_ + 1]
                    if e_ == 0:
                        fn_d = (lambda e, ps=ps, ti=ti, mcol=mcol:
                                e.tensor_scalar(acc0[:, ti * D:(ti + 1) * D], ps[:], mcol, None, ALU.mult))
                    else:
                        fn_d = (lambda e, ps=ps, ti=ti, mcol=mcol:
                                e.scalar_tensor_tensor(acc0[:, ti * D:(ti + 1) * D], ps[:], mcol,
                                                       acc0[:, ti * D:(ti + 1) * D], ALU.mult, ALU.add))
                    t_d = ctx.op("vector", fn_d, waits=[(pe, t_mm)], inc=ve)
                    ps_free_tok[ti % 2] = t_d
                efree.append((ve, t_d))
                if nload < total_loads:
                    emit_expert_load(nload // 8 % 2, nload % 8)
                    nload += 1
            for ti in range(STT):
                ps = ps_o[ti % 2]
                def mmb(e, ti=ti, ps=ps):
                    r = None
                    for n in range(2):
                        r = e.matmul(ps[:, n * 512:(n + 1) * 512],
                                     mskT[:8, ti * 128:(ti + 1) * 128],
                                     eb_sb[:8, n * 512:(n + 1) * 512],
                                     start=True, stop=True)
                    return r
                t_mm = ctx.op("tensor", mmb,
                              waits=[(ve, mskA_toks[ti]), (ve, ps_free_tok[ti % 2]), (gld, tok_static)],
                              inc=pe)
                t_d = ctx.op("vector",
                             lambda e, ti=ti, ps=ps: e.tensor_add(
                                 acc0[:, ti * D:(ti + 1) * D], acc0[:, ti * D:(ti + 1) * D], ps[:]),
                             waits=[(pe, t_mm)], inc=ve)
                ps_free_tok[ti % 2] = t_d
                t_c = ctx.op("vector",
                             lambda e, ti=ti: e.tensor_copy(hb[:], acc0[:, ti * D:(ti + 1) * D]),
                             waits=[(ve, t_d)], inc=ve)
                tr_t = emit_transpose_h(
                    ctx, cm, hb[:], ps_tr, hT[:, ti * D:(ti + 1) * D], 8,
                    [(ve, t_c), (ve, tr_free_tok)], pe, ve)
                tr_free_tok = tr_t
                mskB_toks.append(router_and_mask(1, ti, ti * D, ti * 8, ti * 8, tr_t))
            for e_ in range(8):
                icons = len(efree)
                slot = icons % NSLOT
                cw = [(gld, efill[icons])]
                for ti in range(STT):
                    ps = ps_o[ti % 2]
                    def mm(e, ti=ti, slot=slot, ps=ps):
                        r = None
                        for n in range(2):
                            for c in range(8):
                                r = e.matmul(ps[:, n * 512:(n + 1) * 512],
                                             hT[:, ti * D + c * 128: ti * D + (c + 1) * 128],
                                             wslot[:, slot * 8 * D + c * D + n * 512:
                                                   slot * 8 * D + c * D + (n + 1) * 512],
                                             start=(c == 0), stop=(c == 7))
                        return r
                    t_mm = ctx.op("tensor", mm,
                                  waits=list(cw) + [(ve, mskB_toks[ti]), (ve, ps_free_tok[ti % 2])],
                                  inc=pe)
                    mcol = msk[:, ti * 8 + e_: ti * 8 + e_ + 1]
                    t_d = ctx.op("vector",
                                 lambda e, ps=ps, ti=ti, mcol=mcol:
                                 e.scalar_tensor_tensor(x_st[:, ti * D:(ti + 1) * D], ps[:], mcol,
                                                        x_st[:, ti * D:(ti + 1) * D], ALU.mult, ALU.add),
                                 waits=[(pe, t_mm)], inc=ve)
                    ps_free_tok[ti % 2] = t_d
                efree.append((ve, t_d))
                if nload < total_loads:
                    emit_expert_load(nload // 8 % 2, nload % 8)
                    nload += 1
            for ti in range(STT):
                g = NTILES // NST * s + ti
                ps = ps_o[ti % 2]
                def mmb(e, ti=ti, ps=ps):
                    r = None
                    for n in range(2):
                        r = e.matmul(ps[:, n * 512:(n + 1) * 512],
                                     mskT[:8, ti * 128:(ti + 1) * 128],
                                     eb_sb[:8, D + n * 512: D + (n + 1) * 512],
                                     start=True, stop=True)
                    return r
                t_mm = ctx.op("tensor", mmb,
                              waits=[(ve, mskB_toks[ti]), (ve, ps_free_tok[ti % 2])],
                              inc=pe)
                t_d = ctx.op("vector",
                             lambda e, ti=ti, ps=ps: e.tensor_add(
                                 x_st[:, ti * D:(ti + 1) * D], x_st[:, ti * D:(ti + 1) * D], ps[:]),
                             waits=[(pe, t_mm)], inc=ve)
                ps_free_tok[ti % 2] = t_d
                store_toks[(s, ti)] = ctx.op(
                    "sync", lambda e, ti=ti, g=g: e.dma_start(
                        out=x_dst[g * 128:(g + 1) * 128, :], in_=x_st[:, ti * D:(ti + 1) * D]),
                    waits=[(ve, t_d)], inc=st_s, inc_k=16)
        ctx.op("sync", lambda e: None, waits=[(st_s, st_s.v)])



def emit_l0_ffn(ctx, cm, wap, x_src, x_dst):
    nc = ctx.nc
    STT, NST = 8, 4
    UC = 8 * D
    NU = 6

    with contextlib.ExitStack() as es:
        x_st = es.enter_context(nc.sbuf_tensor("p1_x", [128, STT * D], F32))
        hT = es.enter_context(nc.sbuf_tensor("p1_hT", [128, STT * D], BF16))
        hb = es.enter_context(nc.sbuf_tensor("p1_hb", [128, D], BF16))
        s1b = es.enter_context(nc.sbuf_tensor("p1_s1", [128, D], BF16))
        gb = es.enter_context(nc.sbuf_tensor("p1_g", [128, D], BF16))
        gT_sb = es.enter_context(nc.sbuf_tensor("p1_gT", [128, D], BF16))
        sc_sb = es.enter_context(nc.sbuf_tensor("p1_sc", [128, STT * 8], F32))
        gates = es.enter_context(nc.sbuf_tensor("p1_gt", [128, STT * 8], F32))
        s8 = es.enter_context(nc.sbuf_tensor("p1_s8", [128, 8], F32))
        tmp8 = es.enter_context(nc.sbuf_tensor("p1_t8", [128, 16], F32))
        ebB = es.enter_context(nc.sbuf_tensor("p1_ebB", [128, 8], F32))
        sv = es.enter_context(nc.sbuf_tensor("p1_sv", [128, STT * 8], F32))
        rw_sb = es.enter_context(nc.sbuf_tensor("p1_rw", [128, 64], BF16))
        row32 = es.enter_context(nc.sbuf_tensor("p1_r32", [1, 16], F32))
        rowbf = es.enter_context(nc.sbuf_tensor("p1_rbf", [1, 16], BF16))
        wslot = es.enter_context(nc.sbuf_tensor("p1_w", [128, NU * UC], BF16))
        psA = es.enter_context(nc.psum_tensor("p1_psA", [128, D], F32))
        psB = es.enter_context(nc.psum_tensor("p1_psB", [128, D], F32))
        psC = es.enter_context(nc.psum_tensor("p1_psC", [128, D], F32))
        ps_tr = es.enter_context(nc.psum_tensor("p1_pstr", [128, 1024], BF16))
        ps_r = es.enter_context(nc.psum_tensor("p1_psr", [128, 8], F32))

        import os as _os0
        _dbg = _os0.environ.get("K_P1_DBG", "0") == "1"
        if _dbg:
            dbg_g = nc.dram_tensor("dbg_gates", [4, 128, 64], F32, kind="ExternalOutput").ap()
            dbg_s = nc.dram_tensor("dbg_sc", [4, 128, 64], F32, kind="ExternalOutput").ap()
        loader = CastLoader(ctx, es, "p1", 4096, 2)
        pe = ctx.sem("p1_pe")
        ve = ctx.sem("p1_ve")
        se = ctx.sem("p1_se")
        ld = ctx.sem("p1_ld")
        st_s = ctx.sem("p1_st")

        rw_tok = loader.load(rw_sb[:], wap["l0f_rwT"].rearrange("(c p) n -> p c n", p=128), 64)
        t_r = ctx.op("sync", lambda e: e.dma_start(out=row32[:1, 0:8], in_=wap["l0f_eb"]),
                     inc=ld, inc_k=16)
        t_rc = ctx.op("scalar", lambda e: e.copy(rowbf[:1, 0:8], row32[:1, 0:8]),
                      waits=[(ld, t_r)], inc=se)
        t_bb = ctx.op("tensor", lambda e: e.matmul(ps_r[:], cm.ones1[:], rowbf[:1, 0:8],
                                                   start=True, stop=True),
                      waits=[(se, t_rc), (cm.ldc, cm.ld_tok)], inc=pe)
        ebB_tok = ctx.op("vector", lambda e: e.tensor_copy(ebB[:], ps_r[:]),
                         waits=[(pe, t_bb)], inc=ve)

        unit_fill = []
        unit_free = []

        def load_unit(srcs):
            i = len(unit_fill)
            rw_ = [unit_free[i - NU]] if i >= NU else []
            col0 = (i % NU) * UC
            off = 0
            t = None
            for j, (src, cols) in enumerate(srcs):
                t = loader.load(wslot[:, col0 + off:col0 + off + cols], src, cols,
                                ring_waits=rw_ if j == 0 else ())
                off += cols
            unit_fill.append(t)
            return i

        def unit_ap(i, c, n0, n1):
            col0 = (i % NU) * UC
            return wslot[:, col0 + c * 1024 + n0: col0 + c * 1024 + n1]

        def half_srcs(w2d):
            return [(w2d[h * 512:(h + 1) * 512, :].rearrange("(c p) n -> p c n", p=128), 4096)
                    for h in range(2)]

        import os
        _no_sh = os.environ.get("K_P1_NO_SHARED", "0") == "1"
        _no_ex = os.environ.get("K_P1_NO_EXPERTS", "0") == "1"

        def gen_loads():
            for s in range(NST):
                if not _no_sh:
                    for hc in range(4):
                        yield half_srcs(wap["l0f_sw1T"][:, hc * 1024:(hc + 1) * 1024])
                        yield half_srcs(wap["l0f_sw2T"][:, hc * 1024:(hc + 1) * 1024])
                        yield half_srcs(wap["l0f_sw3T"][hc * 1024:(hc + 1) * 1024, :])
                if not _no_ex:
                    for e_ in range(8):
                        yield half_srcs(wap["l0f_w12T"][e_][:, 0:1024])
                        yield half_srcs(wap["l0f_w12T"][e_][:, 1024:2048])
                        yield half_srcs(wap["l0f_w3T"][e_])
        loads = gen_loads()
        for _ in range(NU):
            load_unit(next(loads))
        iu = 0

        store_toks = {}
        psA_free = 0
        psB_free = 0
        psC_free = 0
        tr_free = 0
        hb_free = 0
        s1_free = 0
        psr_free = 0

        for s in range(NST):
            gate_toks = []
            for ti in range(STT):
                xt = ctx.op("sync", lambda e, ti=ti, g=NTILES // NST * s + ti: e.dma_start(
                    out=x_st[:, ti * D:(ti + 1) * D], in_=x_src[g * 128:(g + 1) * 128, :]),
                    waits=[(st_s, store_toks[(s - 1, ti)])] if s else [],
                    inc=ld, inc_k=16)
                sc_t, h_t = emit_ln_tile(ctx, cm, x_st[:, ti * D:(ti + 1) * D], hb[:],
                                         sv[:, ti * 8:ti * 8 + 8], [(ld, xt)], se, ve)
                tr_t = emit_transpose_h(ctx, cm, hb[:], ps_tr,
                                        hT[:, ti * D:(ti + 1) * D], 8,
                                        [(ve, h_t), (ve, tr_free)], pe, ve)
                tr_free = tr_t
                def mmr(e, ti=ti):
                    r = None
                    for c in range(8):
                        r = e.matmul(ps_r[:], hT[:, ti * D + c * 128:ti * D + (c + 1) * 128],
                                     rw_sb[:, c * 8:(c + 1) * 8],
                                     start=(c == 0), stop=(c == 7))
                    return r
                t_mm = ctx.op("tensor", mmr,
                              waits=[(ve, tr_t), (loader.done, rw_tok), (ve, ebB_tok),
                                     (se, psr_free)],
                              inc=pe)
                t_sg = ctx.op("scalar", lambda e, ti=ti: e.activation(
                    sc_sb[:, ti * 8:(ti + 1) * 8], ps_r[:], AF.Sigmoid),
                    waits=[(pe, t_mm)], inc=se)
                psr_free = t_sg

                scc = sc_sb[:, ti * 8:(ti + 1) * 8]
                gts_ = gates[:, ti * 8:(ti + 1) * 8]
                gate_toks.append(ctx.chain("vector", [
                    lambda e, scc=scc: e.tensor_add(s8[:], scc, ebB[:]),
                    lambda e: e.tensor_reduce(tmp8[:, 0:1], s8[:], AX.X, ALU.max),
                    lambda e: e.tensor_scalar(tmp8[:, 8:16], s8[:], tmp8[:, 0:1],
                                              None, ALU.is_ge),
                    lambda e: e.scalar_tensor_tensor(tmp8[:, 8:16], tmp8[:, 8:16],
                                                     cm.negbig[:], s8[:], ALU.mult, ALU.add),
                    lambda e: e.tensor_reduce(tmp8[:, 1:2], tmp8[:, 8:16], AX.X, ALU.max),
                    lambda e: e.tensor_scalar(tmp8[:, 8:16], s8[:], tmp8[:, 1:2],
                                              None, ALU.is_ge),
                    lambda e, scc=scc, gts_=gts_: e.tensor_mul(gts_, scc, tmp8[:, 8:16]),
                ], waits=[(se, t_sg), (cm.ldc, cm.ld_tok)], inc=ve))

            if _dbg:
                ctx.op("sync", lambda e, s=s: e.dma_start(out=dbg_g[s], in_=gates[:]),
                       waits=[(ve, gate_toks[-1])], inc=st_s, inc_k=16)
                ctx.op("sync", lambda e, s=s: e.dma_start(out=dbg_s[s], in_=sc_sb[:]),
                       waits=[(ve, gate_toks[-1])], inc=st_s, inc_k=16)
            for hc in ([] if _no_sh else range(4)):
                u1, u2, u3 = iu, iu + 1, iu + 2
                for ti in range(STT):
                    def mm1(e, ti=ti, u1=u1):
                        r = None
                        for n in range(2):
                            for c in range(8):
                                r = e.matmul(psA[:, n * 512:(n + 1) * 512],
                                             hT[:, ti * D + c * 128:ti * D + (c + 1) * 128],
                                             unit_ap(u1, c, n * 512, (n + 1) * 512),
                                             start=(c == 0), stop=(c == 7))
                        return r
                    tA = ctx.op("tensor", mm1,
                                waits=[(loader.done, unit_fill[u1]), (ve, gate_toks[ti]),
                                       (se, psA_free), (ve, psC_free)],
                                inc=pe)
                    t_s1 = ctx.op("scalar", lambda e: e.activation(s1b[:], psA[:], AF.Silu),
                                  waits=[(pe, tA), (ve, s1_free)], inc=se)
                    psA_free = t_s1
                    def mm2(e, ti=ti, u2=u2):
                        r = None
                        for n in range(2):
                            for c in range(8):
                                r = e.matmul(psB[:, n * 512:(n + 1) * 512],
                                             hT[:, ti * D + c * 128:ti * D + (c + 1) * 128],
                                             unit_ap(u2, c, n * 512, (n + 1) * 512),
                                             start=(c == 0), stop=(c == 7))
                        return r
                    tB = ctx.op("tensor", mm2,
                                waits=[(loader.done, unit_fill[u2]), (ve, psB_free)],
                                inc=pe)
                    t_g = ctx.op("vector", lambda e: e.tensor_mul(gb[:], s1b[:], psB[:]),
                                 waits=[(pe, tB), (se, t_s1), (ve, hb_free)], inc=ve)
                    psB_free = t_g
                    s1_free = t_g
                    tr_t = emit_transpose_h(ctx, cm, gb[:], ps_tr, gT_sb[:], 8,
                                            [(ve, t_g), (ve, tr_free)], pe, ve)
                    tr_free = tr_t
                    hb_free = tr_t
                    def mm3(e, u3=u3):
                        r = None
                        for n in range(2):
                            for c in range(8):
                                r = e.matmul(psC[:, n * 512:(n + 1) * 512],
                                             gT_sb[:, c * 128:(c + 1) * 128],
                                             unit_ap(u3, c, n * 512, (n + 1) * 512),
                                             start=(c == 0), stop=(c == 7))
                        return r
                    tC = ctx.op("tensor", mm3,
                                waits=[(loader.done, unit_fill[u3]), (ve, tr_t), (ve, psC_free)],
                                inc=pe)
                    t_acc = ctx.op("vector", lambda e, ti=ti: e.tensor_add(
                        x_st[:, ti * D:(ti + 1) * D], x_st[:, ti * D:(ti + 1) * D], psC[:]),
                        waits=[(pe, tC)], inc=ve)
                    psC_free = t_acc
                    store_toks[("acc", s, ti)] = t_acc
                for u in (u1, u2, u3):
                    unit_free.append((ve, psC_free))
                    if True:
                        try:
                            load_unit(next(loads))
                        except StopIteration:
                            pass
                iu += 3
            for e_ in ([] if _no_ex else range(8)):
                ua, ub, uw = iu, iu + 1, iu + 2
                for ti in range(STT):
                    def mm1(e, ti=ti, ua=ua):
                        r = None
                        for n in range(2):
                            for c in range(8):
                                r = e.matmul(psA[:, n * 512:(n + 1) * 512],
                                             hT[:, ti * D + c * 128:ti * D + (c + 1) * 128],
                                             unit_ap(ua, c, n * 512, (n + 1) * 512),
                                             start=(c == 0), stop=(c == 7))
                        return r
                    tA = ctx.op("tensor", mm1,
                                waits=[(loader.done, unit_fill[ua]), (ve, gate_toks[ti]),
                                       (se, psA_free), (ve, psC_free)],
                                inc=pe)
                    t_s1 = ctx.op("scalar", lambda e: e.activation(s1b[:], psA[:], AF.Silu),
                                  waits=[(pe, tA), (ve, s1_free)], inc=se)
                    psA_free = t_s1
                    def mm2(e, ti=ti, ub=ub):
                        r = None
                        for n in range(2):
                            for c in range(8):
                                r = e.matmul(psB[:, n * 512:(n + 1) * 512],
                                             hT[:, ti * D + c * 128:ti * D + (c + 1) * 128],
                                             unit_ap(ub, c, n * 512, (n + 1) * 512),
                                             start=(c == 0), stop=(c == 7))
                        return r
                    tB = ctx.op("tensor", mm2,
                                waits=[(loader.done, unit_fill[ub]), (ve, psB_free)],
                                inc=pe)
                    gcol = gates[:, ti * 8 + e_:ti * 8 + e_ + 1]
                    t_g = ctx.op("vector", lambda e, gcol=gcol: e.scalar_tensor_tensor(
                        gb[:], psB[:], gcol, s1b[:], ALU.mult, ALU.mult),
                        waits=[(pe, tB), (se, t_s1), (ve, hb_free)], inc=ve)
                    psB_free = t_g
                    s1_free = t_g
                    tr_t = emit_transpose_h(ctx, cm, gb[:], ps_tr, gT_sb[:], 8,
                                            [(ve, t_g), (ve, tr_free)], pe, ve)
                    tr_free = tr_t
                    hb_free = tr_t
                    def mm3(e, uw=uw):
                        r = None
                        for n in range(2):
                            for c in range(8):
                                r = e.matmul(psC[:, n * 512:(n + 1) * 512],
                                             gT_sb[:, c * 128:(c + 1) * 128],
                                             unit_ap(uw, c, n * 512, (n + 1) * 512),
                                             start=(c == 0), stop=(c == 7))
                        return r
                    tC = ctx.op("tensor", mm3,
                                waits=[(loader.done, unit_fill[uw]), (ve, tr_t), (ve, psC_free)],
                                inc=pe)
                    t_acc = ctx.op("vector", lambda e, ti=ti: e.tensor_add(
                        x_st[:, ti * D:(ti + 1) * D], x_st[:, ti * D:(ti + 1) * D], psC[:]),
                        waits=[(pe, tC)], inc=ve)
                    psC_free = t_acc
                    store_toks[("acc", s, ti)] = t_acc
                for u in (ua, ub, uw):
                    unit_free.append((ve, psC_free))
                    try:
                        load_unit(next(loads))
                    except StopIteration:
                        pass
                iu += 3
            for ti in range(STT):
                store_toks[(s, ti)] = ctx.op(
                    "sync", lambda e, ti=ti, g=NTILES // NST * s + ti: e.dma_start(
                        out=x_dst[g * 128:(g + 1) * 128, :],
                        in_=x_st[:, ti * D:(ti + 1) * D]),
                    waits=[(ve, store_toks[("acc", s, ti)])], inc=st_s, inc_k=16)
        ctx.op("sync", lambda e: None, waits=[(st_s, st_s.v)])



def emit_l1_ffn(ctx, cm, wap, x_src, x_dst):
    nc = ctx.nc
    STT, NST = 8, 4
    UC = 8704
    NU = 3

    with contextlib.ExitStack() as es:
        x_st = es.enter_context(nc.sbuf_tensor("p3_x", [128, STT * D], F32))
        hT = es.enter_context(nc.sbuf_tensor("p3_hT", [128, STT * D], BF16))
        hb = es.enter_context(nc.sbuf_tensor("p3_hb", [128, D], BF16))
        s1h = es.enter_context(nc.sbuf_tensor("p3_s1", [128, 512], BF16))
        s1g = es.enter_context(nc.sbuf_tensor("p3_sg", [128, 512], BF16))
        s1T = es.enter_context(nc.sbuf_tensor("p3_s1T", [128, 512], BF16))
        sv = es.enter_context(nc.sbuf_tensor("p3_sv", [128, STT * 8], F32))
        eexp = es.enter_context(nc.sbuf_tensor("p3_ee", [128, 64], F32))
        itb = es.enter_context(nc.sbuf_tensor("p3_it", [128, 64], F32))
        mskb = es.enter_context(nc.sbuf_tensor("p3_mk", [128, 64], F32))
        cw_st = es.enter_context(nc.sbuf_tensor("p3_cw", [128, STT * 64], F32))
        cwb = es.enter_context(nc.sbuf_tensor("p3_cwb", [128, 64], BF16))
        cwT = es.enter_context(nc.sbuf_tensor("p3_cwT", [64, STT * 128], BF16))
        t8c = es.enter_context(nc.sbuf_tensor("p3_t8", [128, 8], F32))
        rw_sb = es.enter_context(nc.sbuf_tensor("p3_rw", [128, 512], BF16))
        rb_sb = es.enter_context(nc.sbuf_tensor("p3_rb", [1, 64], BF16))
        shb1 = es.enter_context(nc.sbuf_tensor("p3_shb1", [1, 4096], BF16))
        shb2 = es.enter_context(nc.sbuf_tensor("p3_shb2", [1, 2048], BF16))
        b2m = es.enter_context(nc.sbuf_tensor("p3_b2m", [64, 1024], BF16))
        wslot = es.enter_context(nc.sbuf_tensor("p3_w", [128, NU * UC], BF16))
        psA = es.enter_context(nc.psum_tensor("p3_psA", [128, 512], F32))
        psC = es.enter_context(nc.psum_tensor("p3_psC", [128, D], F32))
        ps_tr = es.enter_context(nc.psum_tensor("p3_pstr", [128, 1024], BF16))
        ps_r = es.enter_context(nc.psum_tensor("p3_psr", [128, 64], F32))
        ps_ct = es.enter_context(nc.psum_tensor("p3_psct", [128, 128], BF16))

        loader = CastLoader(ctx, es, "p3", 4096, 2)
        gld = loader.done
        pe = ctx.sem("p3_pe")
        ve = ctx.sem("p3_ve")
        se = ctx.sem("p3_se")
        ld = ctx.sem("p3_ld")
        st_s = ctx.sem("p3_st")

        loader.load(rw_sb[:], wap["l1f_rw"].rearrange("(c p) n -> p c n", p=128), 512)
        loader.load(rb_sb[:], wap["l1f_rb"], 64, nparts=1)
        loader.load(shb1[:], wap["l1f_sb1"].rearrange("l o n -> o l n"), 4096, nparts=1)
        loader.load(shb2[:], wap["l1f_sb2"].rearrange("l o n -> o l n"), 2048, nparts=1)
        tok_static = loader.load(b2m[:64, :], wap["l1f_b2m"], 1024, nparts=64)

        unit_fill = []
        unit_free = []

        def load_unit(srcs):
            i = len(unit_fill)
            rw_ = [unit_free[i - NU]] if i >= NU else []
            col0 = (i % NU) * UC
            off = 0
            t = None
            for j, (src, cols) in enumerate(srcs):
                np_ = src.shape[0] if len(src.shape) == 2 else 128
                t = loader.load(wslot[:np_, col0 + off:col0 + off + cols], src, cols,
                                ring_waits=rw_ if j == 0 else (),
                                nparts=np_)
                off += cols
            unit_fill.append(t)

        def uap(i, off, cols):
            col0 = (i % NU) * UC
            return wslot[:, col0 + off:col0 + off + cols]

        def gen_loads():
            for s in range(NST):
                for sh in range(2):
                    for hc in range(4):
                        sw1p = wap["l1f_sw1"][sh, :, hc * 512:(hc + 1) * 512]
                        sw2p = wap["l1f_sw2"][sh, hc * 512:(hc + 1) * 512, :]
                        yield [(sw1p.rearrange("(c p) n -> p c n", p=128), 4096),
                               (sw2p.rearrange("(c p) n -> p c n", p=128), 4096)]
                for pr in range(32):
                    w1p = wap["l1f_w1p"][pr]
                    yield [(w1p.rearrange("(c p) n -> p c n", p=128), 4096),
                           (wap["l1f_w2"][2 * pr].rearrange("(c p) n -> p c n", p=128), 2048),
                           (wap["l1f_w2"][2 * pr + 1].rearrange("(c p) n -> p c n", p=128), 2048),
                           (wap["l1f_b1p"][pr], 512)]
        loads = gen_loads()
        for _ in range(NU):
            load_unit(next(loads))
        iu = 0

        store_toks = {}
        psA_free = 0
        psC_free = 0
        tr_free = 0
        s1_free = 0
        s1g_free = 0
        psr_free = 0
        ct_free = 0

        for s in range(NST):
            cw_toks = []
            cwT_toks = []
            for ti in range(STT):
                xt = ctx.op("sync", lambda e, ti=ti, g=NTILES // NST * s + ti: e.dma_start(
                    out=x_st[:, ti * D:(ti + 1) * D], in_=x_src[g * 128:(g + 1) * 128, :]),
                    waits=[(st_s, store_toks[(s - 1, ti)])] if s else [],
                    inc=ld, inc_k=16)
                sc_t, h_t = emit_ln_tile(ctx, cm, x_st[:, ti * D:(ti + 1) * D], hb[:],
                                         sv[:, ti * 8:ti * 8 + 8], [(ld, xt)], se, ve)
                tr_t = emit_transpose_h(ctx, cm, hb[:], ps_tr,
                                        hT[:, ti * D:(ti + 1) * D], 8,
                                        [(ve, h_t), (ve, tr_free)], pe, ve)
                tr_free = tr_t
                def mmr(e, ti=ti):
                    r = None
                    for c in range(8):
                        r = e.matmul(ps_r[:], hT[:, ti * D + c * 128:ti * D + (c + 1) * 128],
                                     rw_sb[:, c * 64:(c + 1) * 64],
                                     start=(c == 0), stop=False)
                    return e.matmul(ps_r[:], cm.ones1[:], rb_sb[:],
                                    start=False, stop=True)
                t_mm = ctx.op("tensor", mmr,
                              waits=[(ve, tr_t), (gld, tok_static), (se, psr_free)],
                              inc=pe)
                t_nm = ctx.chain("vector", [
                    lambda e: e.tensor_reduce(t8c[:, 0:1], ps_r[:], AX.X, ALU.max),
                    lambda e: e.tensor_scalar_mul(t8c[:, 1:2], t8c[:, 0:1], -1.0),
                ], waits=[(pe, t_mm)], inc=ve)
                t_ex = ctx.op("scalar", lambda e: e.activation(
                    eexp[:], ps_r[:], AF.Exp, bias=t8c[:, 1:2]),
                    waits=[(ve, t_nm)], inc=se)
                psr_free = t_ex
                steps = [lambda e: e.tensor_copy(itb[:], eexp[:])]
                for k in range(8):
                    steps.append(lambda e: e.tensor_reduce(t8c[:, 2:3], itb[:], AX.X, ALU.max))
                    if k < 7:
                        steps.append(lambda e: e.tensor_scalar(mskb[:], itb[:], t8c[:, 2:3],
                                                               None, ALU.is_ge))
                        steps.append(lambda e: e.scalar_tensor_tensor(
                            itb[:], mskb[:], cm.negbig[:], itb[:], ALU.mult, ALU.add))
                steps.append(lambda e: e.tensor_scalar(mskb[:], eexp[:], t8c[:, 2:3],
                                                       None, ALU.is_ge))
                steps.append(lambda e: e.tensor_mul(itb[:], eexp[:], mskb[:]))
                steps.append(lambda e: e.tensor_reduce(t8c[:, 3:4], itb[:], AX.X, ALU.add))
                t_m8 = ctx.chain("vector", steps, waits=[(se, t_ex)], inc=ve)
                t_rs = ctx.chain("scalar", [
                    lambda e: e.activation(t8c[:, 4:5], t8c[:, 3:4], AF.Ln),
                    lambda e: e.activation(t8c[:, 4:5], t8c[:, 4:5], AF.Exp, scale=-1.0),
                ], waits=[(ve, t_m8)], inc=se)
                cwc = cw_st[:, ti * 64:(ti + 1) * 64]
                t_cw = ctx.chain("vector", [
                    lambda e, cwc=cwc: e.tensor_scalar(cwc, itb[:], t8c[:, 4:5], None, ALU.mult),
                    lambda e, cwc=cwc: e.tensor_copy(cwb[:], cwc),
                ], waits=[(se, t_rs), (ve, ct_free)], inc=ve)
                cw_toks.append(t_cw)
                t_ct = ctx.op("tensor", lambda e: e.transpose(ps_ct[0:64, 0:128], cwb[:], cm.ident[:]),
                              waits=[(ve, t_cw)], inc=pe)
                t_ce = ctx.op("vector", lambda e, ti=ti: e.tensor_copy(
                    cwT[:64, ti * 128:(ti + 1) * 128], ps_ct[:64, 0:128]),
                    waits=[(pe, t_ct)], inc=ve)
                ct_free = t_ce
                cwT_toks.append(t_ce)

            for sh in range(2):
                for hc in range(4):
                    u = iu
                    b1off = sh * 2048 + hc * 512
                    for ti in range(STT):
                        def mm1(e, ti=ti, u=u, b1off=b1off):
                            r = None
                            for c in range(8):
                                r = e.matmul(psA[:], hT[:, ti * D + c * 128:ti * D + (c + 1) * 128],
                                             uap(u, c * 512, 512),
                                             start=(c == 0), stop=False)
                            return e.matmul(psA[:], cm.ones1[:], shb1[:, b1off:b1off + 512],
                                            start=False, stop=True)
                        tA = ctx.op("tensor", mm1,
                                    waits=[(gld, unit_fill[u]), (se, psA_free), (ve, psC_free)],
                                    inc=pe)
                        t_s1 = ctx.op("scalar", lambda e: e.activation(s1h[:], psA[:], AF.Silu),
                                      waits=[(pe, tA), (ve, s1_free)], inc=se)
                        psA_free = t_s1
                        tr_t = emit_transpose_h(ctx, cm, s1h[:], ps_tr, s1T[:], 4,
                                                [(se, t_s1), (ve, tr_free)], pe, ve)
                        tr_free = tr_t
                        s1_free = tr_t
                        def mm2(e, ti=ti, u=u):
                            r = None
                            for n in range(2):
                                for c in range(4):
                                    r = e.matmul(psC[:, n * 512:(n + 1) * 512],
                                                 s1T[:, c * 128:(c + 1) * 128],
                                                 uap(u, 4096 + c * 1024 + n * 512, 512),
                                                 start=(c == 0), stop=(c == 3))
                            return r
                        tC = ctx.op("tensor", mm2, waits=[(ve, tr_t), (ve, psC_free)],
                                    inc=pe)
                        t_acc = ctx.op("vector", lambda e, ti=ti: e.tensor_add(
                            x_st[:, ti * D:(ti + 1) * D], x_st[:, ti * D:(ti + 1) * D], psC[:]),
                            waits=[(pe, tC)], inc=ve)
                        psC_free = t_acc
                        store_toks[("acc", s, ti)] = t_acc
                    unit_free.append((ve, psC_free))
                    try:
                        load_unit(next(loads))
                    except StopIteration:
                        pass
                    iu += 1
            for pr in range(32):
                u = iu
                for ti in range(STT):
                    def mm1(e, ti=ti, u=u):
                        r = None
                        for c in range(8):
                            r = e.matmul(psA[:], hT[:, ti * D + c * 128:ti * D + (c + 1) * 128],
                                         uap(u, c * 512, 512),
                                         start=(c == 0), stop=False)
                        return e.matmul(psA[:], cm.ones1[:], uap(u, 8192, 512)[0:1, :],
                                        start=False, stop=True)
                    tA = ctx.op("tensor", mm1,
                                waits=[(gld, unit_fill[u]), (se, psA_free), (ve, psC_free),
                                       (ve, cw_toks[ti])],
                                inc=pe)
                    t_s1 = ctx.op("scalar", lambda e: e.activation(s1h[:], psA[:], AF.Silu),
                                  waits=[(pe, tA), (ve, s1_free)], inc=se)
                    psA_free = t_s1
                    def scale(e, ti=ti, pr=pr):
                        e.tensor_scalar(s1g[:, 0:256], s1h[:, 0:256],
                                        cw_st[:, ti * 64 + 2 * pr:ti * 64 + 2 * pr + 1],
                                        None, ALU.mult)
                        return e.tensor_scalar(s1g[:, 256:512], s1h[:, 256:512],
                                               cw_st[:, ti * 64 + 2 * pr + 1:ti * 64 + 2 * pr + 2],
                                               None, ALU.mult)
                    t_g = ctx.op("vector", scale,
                                 waits=[(se, t_s1), (ve, s1g_free)], inc=ve)
                    s1_free = t_g
                    tr_t = emit_transpose_h(ctx, cm, s1g[:], ps_tr, s1T[:], 4,
                                            [(ve, t_g), (ve, tr_free)], pe, ve)
                    tr_free = tr_t
                    s1g_free = tr_t
                    def mm2(e, ti=ti, u=u):
                        r = None
                        for eh in range(2):
                            for n in range(2):
                                for c in range(2):
                                    r = e.matmul(psC[:, n * 512:(n + 1) * 512],
                                                 s1T[:, eh * 256 + c * 128:eh * 256 + (c + 1) * 128],
                                                 uap(u, 4096 + eh * 2048 + c * 1024 + n * 512, 512),
                                                 start=(eh == 0 and c == 0),
                                                 stop=(eh == 1 and c == 1))
                        return r
                    tC = ctx.op("tensor", mm2, waits=[(ve, tr_t), (ve, psC_free)],
                                inc=pe)
                    t_acc = ctx.op("vector", lambda e, ti=ti: e.tensor_add(
                        x_st[:, ti * D:(ti + 1) * D], x_st[:, ti * D:(ti + 1) * D], psC[:]),
                        waits=[(pe, tC)], inc=ve)
                    psC_free = t_acc
                    store_toks[("acc", s, ti)] = t_acc
                unit_free.append((ve, psC_free))
                try:
                    load_unit(next(loads))
                except StopIteration:
                    pass
                iu += 1
            for ti in range(STT):
                def mmb(e, ti=ti):
                    r = None
                    for n in range(2):
                        r = e.matmul(psC[:, n * 512:(n + 1) * 512], cm.ones1[:],
                                     shb2[:, n * 512:(n + 1) * 512],
                                     start=True, stop=False)
                        r = e.matmul(psC[:, n * 512:(n + 1) * 512], cm.ones1[:],
                                     shb2[:, 1024 + n * 512:1024 + (n + 1) * 512],
                                     start=False, stop=False)
                        r = e.matmul(psC[:, n * 512:(n + 1) * 512],
                                     cwT[:64, ti * 128:(ti + 1) * 128],
                                     b2m[:64, n * 512:(n + 1) * 512],
                                     start=False, stop=True)
                    return r
                t_mm = ctx.op("tensor", mmb,
                              waits=[(ve, cwT_toks[ti]), (ve, psC_free), (gld, tok_static)],
                              inc=pe)
                t_acc = ctx.op("vector", lambda e, ti=ti: e.tensor_add(
                    x_st[:, ti * D:(ti + 1) * D], x_st[:, ti * D:(ti + 1) * D], psC[:]),
                    waits=[(pe, t_mm)], inc=ve)
                psC_free = t_acc
                store_toks[(s, ti)] = ctx.op(
                    "sync", lambda e, ti=ti, g=NTILES // NST * s + ti: e.dma_start(
                        out=x_dst[g * 128:(g + 1) * 128, :],
                        in_=x_st[:, ti * D:(ti + 1) * D]),
                    waits=[(ve, t_acc)], inc=st_s, inc_k=16)
        ctx.op("sync", lambda e: None, waits=[(st_s, st_s.v)])



def emit_softmax_attention(ctx, cm, wap, cfg, x_src, x_dst):
    nc = ctx.nc
    nm = cfg["name"]
    scale = 0.125

    with contextlib.ExitStack() as es:
        qT = es.enter_context(nc.sbuf_tensor(nm + "_qT", [128, 8 * TS], BF16))
        kTb = es.enter_context(nc.sbuf_tensor(nm + "_kT", [128, 8 * TS], BF16))
        v_row = es.enter_context(nc.sbuf_tensor(nm + "_v", [128, 16 * 1040], BF16))
        hT = es.enter_context(nc.sbuf_tensor(nm + "_hT", [128, 16 * D], BF16))
        oT = hT
        wu = es.enter_context(nc.sbuf_tensor(nm + "_wu", [128, 8 * D], BF16))
        pbuf = es.enter_context(nc.sbuf_tensor(nm + "_pb", [128, 2048], BF16))
        xb = es.enter_context(nc.sbuf_tensor(nm + "_xb", [128, 2 * D], F32))
        hb = es.enter_context(nc.sbuf_tensor(nm + "_hb", [128, D], BF16))
        qtok = es.enter_context(nc.sbuf_tensor(nm + "_qt", [128, D], BF16))
        sqb = es.enter_context(nc.sbuf_tensor(nm + "_sq", [128, D], F32))
        sv = es.enter_context(nc.sbuf_tensor(nm + "_sv", [128, 16 * 8], F32))
        nrm = es.enter_context(nc.sbuf_tensor(nm + "_nrm", [128, 48], F32))
        nsm = es.enter_context(nc.sbuf_tensor(nm + "_nsm", [16, 288], F32))
        mrow = es.enter_context(nc.sbuf_tensor(nm + "_mr", [1, 64], F32))
        Mb = es.enter_context(nc.sbuf_tensor(nm + "_Mb", [128, 32], F32))
        den_sb = es.enter_context(nc.sbuf_tensor(nm + "_den", [1, 512], F32))
        rden_bf = es.enter_context(nc.sbuf_tensor(nm + "_rdb", [1, 512], BF16))
        rdenB = es.enter_context(nc.sbuf_tensor(nm + "_rB", [128, 512], F32))
        bias_sb = es.enter_context(nc.sbuf_tensor(nm + "_bia", [1, 4 * D], BF16))
        if cfg.get("affine") or cfg.get("save_h") is not None:
            hf = es.enter_context(nc.sbuf_tensor(nm + "_hf", [128, D], F32))
            affb = es.enter_context(nc.sbuf_tensor(nm + "_aff", [128, 2 * D], BF16))
            affr = es.enter_context(nc.sbuf_tensor(nm + "_afr", [1, 2 * D], BF16))
        if cfg.get("rope"):
            trg = es.enter_context(nc.sbuf_tensor(nm + "_trg", [128, 1024], F32))
        if cfg.get("toep"):
            expC = es.enter_context(nc.sbuf_tensor(nm + "_eC", [128, 4096], BF16))
            Cstg = es.enter_context(nc.sbuf_tensor(nm + "_Cs", [128, 1024], F32))
        pbig = es.enter_context(nc.psum_tensor(nm + "_pS", [128, 2048], F32))
        pmid = es.enter_context(nc.psum_tensor(nm + "_pM", [128, 1024], F32))
        pone = es.enter_context(nc.psum_tensor(nm + "_p1", [128, 512], F32))
        ptr = es.enter_context(nc.psum_tensor(nm + "_pT", [128, 1024], BF16))

        loader = CastLoader(ctx, es, nm, 1024, 2)
        gld = loader.done
        pe = ctx.sem(nm + "_pe")
        ve = ctx.sem(nm + "_ve")
        se = ctx.sem(nm + "_se")
        ld = ctx.sem(nm + "_ld")
        st_s = ctx.sem(nm + "_st")

        tok_static = 0
        if cfg.get("bias"):
            for j, bn in enumerate(cfg["bias"]):
                if bn is None:
                    continue
                tok_static = loader.load(bias_sb[:1, j * D:(j + 1) * D],
                                         wap[bn], D, nparts=1)
        aff_tok = 0
        pmid_free = 0
        if cfg.get("affine"):
            g_n, b_n = cfg["affine"]
            loader.load(affr[:1, 0:D], wap[g_n], D, nparts=1)
            t2 = loader.load(affr[:1, D:2 * D], wap[b_n], D, nparts=1)
            for j in range(2):
                t_b = ctx.op("tensor", lambda e, j=j: [
                    e.matmul(pmid[:, n2 * 512:(n2 + 1) * 512], cm.ones1[:],
                             affr[:1, j * D + n2 * 512:j * D + (n2 + 1) * 512],
                             start=True, stop=True) for n2 in range(2)][-1],
                    waits=[(gld, t2), (cm.ldc, cm.ld_tok), (ve, pmid_free)],
                    inc=pe)
                aff_tok = ctx.op("vector", lambda e, j=j: e.tensor_copy(
                    affb[:, j * D:(j + 1) * D], pmid[:]),
                    waits=[(pe, t_b)], inc=ve)
                pmid_free = aff_tok

        store_toks = {}
        ldrow = {}
        pbig_free = [0, 0]
        pbuf_free = [0, 0]
        ptr_free = 0
        pone_free = 0
        hb_free = 0
        qtok_free = 0
        eC_free = 0

        def load_w(name):
            t = None
            for hh in range(8):
                src = wap[name][hh * 128:(hh + 1) * 128, :]
                t = loader.load(wu[:, hh * D:(hh + 1) * D], src, 1024,
                                ring_waits=[(pe, load_w.free)] if hh == 0 else ())
            return t
        load_w.free = 0

        for r in range(R):
            tok0 = r * TS
            h_toks = []
            wtok_q = load_w(cfg["wq"])
            for ti in range(16):
                g = tok0 // 128 + ti
                slot = ti % 2
                xw = ctx.op("sync", lambda e, ti=ti, g=g, slot=slot: e.dma_start(
                    out=xb[:, slot * D:(slot + 1) * D],
                    in_=x_src[g * 128:(g + 1) * 128, :]),
                    waits=[(ve, ldrow.get(("xb", slot), 0))],
                    inc=ld, inc_k=16)
                x_ap = xb[:, slot * D:(slot + 1) * D]
                if cfg.get("affine") or cfg.get("save_h") is not None:
                    sc_t, hf_t = emit_ln_tile(ctx, cm, x_ap, hf[:],
                                              sv[:, ti * 8:ti * 8 + 8],
                                              [(ld, xw), (ve, hb_free),
                                               (st_s, store_toks.get(("h", r, ti - 1), 0))],
                                              se, ve)
                    ldrow[("xb", slot)] = hf_t
                    if cfg.get("save_h") is not None:
                        store_toks[("h", r, ti)] = ctx.op(
                            "sync", lambda e, g=g: e.dma_start(
                                out=cfg["save_h"][g * 128:(g + 1) * 128, :], in_=hf[:]),
                            waits=[(ve, hf_t)], inc=st_s, inc_k=16)
                    if cfg.get("affine"):
                        h_t = ctx.chain("vector", [
                            lambda e: e.tensor_mul(sqb[:], hf[:], affb[:, 0:D]),
                            lambda e: e.tensor_add(sqb[:], sqb[:], affb[:, D:2 * D]),
                            lambda e: e.tensor_copy(hb[:], sqb[:]),
                        ], waits=[(ve, hf_t)], inc=ve)
                    else:
                        h_t = ctx.op("vector", lambda e: e.tensor_copy(hb[:], hf[:]),
                                     waits=[(ve, hf_t)], inc=ve)

                else:
                    sc_t, h_t = emit_ln_tile(ctx, cm, x_ap, hb[:],
                                             sv[:, ti * 8:ti * 8 + 8],
                                             [(ld, xw), (ve, hb_free)], se, ve)
                    ldrow[("xb", slot)] = h_t
                tr_t = emit_transpose_h(ctx, cm, hb[:], ptr,
                                        hT[:, ti * D:(ti + 1) * D], 8,
                                        [(ve, h_t), (ve, ptr_free)], pe, ve)
                ptr_free = tr_t
                hb_free = tr_t
                h_toks.append(tr_t)
            t_vo = ctx.op("vector", lambda e: e.memset(
                v_row[:].rearrange("p (t h o) -> p t h o", t=16, h=16)[:, :, :, 64:65],
                1.0), inc=ve)

            nrm_base = {"q": 0, "k": 16}
            proj_toks = {}
            for pi, (pname, wname) in enumerate(
                    (("q", cfg["wq"]), ("k", cfg["wk"]), ("v", cfg["wv"]))):
                wtok = wtok_q if pi == 0 else load_w(wname)
                bias_j = {"q": 0, "k": 1, "v": 2}[pname] if cfg.get("bias") else None
                for ti in range(16):
                    def mmp(e, ti=ti, bias_j=bias_j):
                        rr = None
                        for n2 in range(2):
                            for c in range(8):
                                rr = e.matmul(pmid[:, n2 * 512:(n2 + 1) * 512],
                                              hT[:, ti * D + c * 128:ti * D + (c + 1) * 128],
                                              wu[:, c * D + n2 * 512:c * D + (n2 + 1) * 512],
                                              start=(c == 0),
                                              stop=(c == 7 and bias_j is None))
                            if bias_j is not None:
                                rr = e.matmul(pmid[:, n2 * 512:(n2 + 1) * 512], cm.ones1[:],
                                              bias_sb[:1, bias_j * D + n2 * 512:
                                                      bias_j * D + (n2 + 1) * 512],
                                              start=False, stop=True)
                        return rr
                    t_mm = ctx.op("tensor", mmp,
                                  waits=[(gld, wtok), (ve, h_toks[ti]), (ve, pmid_free),
                                         (gld, tok_static)],
                                  inc=pe)
                    if pname == "v":
                        dst = v_row[:, ti * 1040:(ti + 1) * 1040]
                        dst = dst.rearrange("p (h o) -> p h o", h=16)[:, :, 0:64]
                        t_ev = ctx.op("vector", lambda e, dst=dst: e.tensor_copy(
                            dst, pmid[:].rearrange("p (h o) -> p h o", h=16)),
                            waits=[(pe, t_mm), (ve, t_vo)], inc=ve)
                        pmid_free = t_ev
                        proj_toks[("v", ti)] = t_ev
                        continue
                    if cfg.get("rope"):
                        g = tok0 // 128 + ti
                        ctx.op("sync", lambda e, g=g: e.dma_start(
                            out=trg[:, 0:512],
                            in_=wap["c_rope_cos"][(g % 16) * 128:(g % 16 + 1) * 128, :]),
                            waits=[(ve, ldrow.get("trg", 0))], inc=ld, inc_k=16)
                        tt = ctx.op("sync", lambda e, g=g: e.dma_start(
                            out=trg[:, 512:1024],
                            in_=wap["c_rope_sin"][(g % 16) * 128:(g % 16 + 1) * 128, :]),
                            inc=ld, inc_k=16)
                        def ev(e):
                            pmid3 = pmid[:].rearrange("p (a b) -> p a b", b=2)
                            qt3 = qtok[:].rearrange("p (a b) -> p a b", b=2)
                            e.tensor_mul(qt3[:, :, 0:1], pmid3[:, :, 0:1],
                                         trg[:, 0:512].rearrange("p (a b) -> p a b", b=1))
                            return e.tensor_mul(qt3[:, :, 1:2], pmid3[:, :, 0:1],
                                                trg[:, 512:1024].rearrange("p (a b) -> p a b", b=1))
                        t_ev = ctx.op("vector", ev,
                                      waits=[(pe, t_mm), (ld, tt), (ve, qtok_free)],
                                      inc=ve)
                        ldrow["trg"] = t_ev
                    else:
                        t_ev = ctx.op("vector", lambda e: e.tensor_copy(qtok[:], pmid[:]),
                                      waits=[(pe, t_mm), (ve, qtok_free)], inc=ve)
                    pmid_free = t_ev
                    t_sq = ctx.op("vector", lambda e: e.tensor_mul(sqb[:], qtok[:], qtok[:]),
                                  waits=[(ve, t_ev)], inc=ve)
                    nb = nrm_base[pname]
                    if ti > 0:
                        upd = (lambda e, nb=nb: e.tensor_max(
                            nrm[:, nb:nb + 16], nrm[:, nb:nb + 16], nrm[:, 32:48]))
                    else:
                        upd = (lambda e, nb=nb: e.tensor_copy(
                            nrm[:, nb:nb + 16], nrm[:, 32:48]))
                    t_nr = ctx.chain("vector", [
                        lambda e: e.tensor_reduce(
                            nrm[:, 32:48], sqb[:].rearrange("p (h d) -> p h d", h=16),
                            AX.X, ALU.add),
                        upd,
                    ], waits=[(ve, t_sq)], inc=ve)
                    dstT = qT if pname == "q" else kTb
                    def pe_tr(e):
                        rr = None
                        for c in range(8):
                            rr = e.transpose(ptr[:, c * 128:(c + 1) * 128],
                                             qtok[:, c * 128:(c + 1) * 128], cm.ident[:])
                        return rr
                    t_tr = ctx.op("tensor", pe_tr,
                                  waits=[(ve, t_nr), (ve, ptr_free)], inc=pe)
                    dst = dstT[:].rearrange("p (c t) -> p c t", c=8)[:, :, ti * 128:(ti + 1) * 128]
                    t_te = ctx.op("vector", lambda e, dst=dst: e.tensor_copy(
                        dst, ptr[:].rearrange("p (c t) -> p c t", c=8)),
                        waits=[(pe, t_tr)], inc=ve)
                    ptr_free = t_te
                    qtok_free = t_te
                    proj_toks[(pname, ti)] = t_te
                load_w.free = pe.v

            t_trn = ctx.op("tensor", lambda e: [
                e.transpose(pmid[0:16, 0:128], nrm[:, 0:16], cm.identf[:]),
                e.transpose(pmid[0:16, 128:256], nrm[:, 16:32], cm.identf[:])][-1],
                waits=[(ve, ve.v), (ve, pmid_free)], inc=pe)
            t_nsm = ctx.chain("vector", [
                lambda e: e.tensor_copy(nsm[:16, 0:128], pmid[0:16, 0:128]),
                lambda e: e.tensor_copy(nsm[:16, 128:256], pmid[0:16, 128:256]),
                lambda e: e.tensor_reduce(nsm[:16, 256:257], nsm[:16, 0:128], AX.X, ALU.max),
                lambda e: e.tensor_reduce(nsm[:16, 257:258], nsm[:16, 128:256], AX.X, ALU.max),
            ], waits=[(pe, t_trn)], inc=ve)
            pmid_free = t_nsm
            t_lnm = ctx.chain("scalar", [
                lambda e: e.activation(nsm[:16, 258:259], nsm[:16, 256:257], AF.Ln),
                lambda e: e.activation(nsm[:16, 259:260], nsm[:16, 257:258], AF.Ln),
            ], waits=[(ve, t_nsm)], inc=se)
            t_m = ctx.op("vector", lambda e: e.tensor_add(
                nsm[:16, 260:261], nsm[:16, 258:259], nsm[:16, 259:260]),
                waits=[(se, t_lnm)], inc=ve)
            t_me = ctx.op("scalar", lambda e: e.activation(
                nsm[:16, 261:262], nsm[:16, 260:261], AF.Exp, scale=0.5),
                waits=[(ve, t_m)], inc=se)
            t_mt = ctx.op("tensor", lambda e: e.transpose(
                pone[0:1, 0:16], nsm[:16, 261:262], cm.identf[:16, :16]),
                waits=[(se, t_me), (ve, pone_free)], inc=pe)
            t_mr = ctx.op("vector", lambda e: e.tensor_copy(mrow[:1, 0:16], pone[0:1, 0:16]),
                          waits=[(pe, t_mt)], inc=ve)
            t_mrb = ctx.op("scalar", lambda e: e.copy(rden_bf[:1, 0:16], mrow[:1, 0:16]),
                           waits=[(ve, t_mr)], inc=se)
            t_mbb = ctx.op("tensor", lambda e: e.matmul(
                pone[:, 0:16], cm.ones1[:], rden_bf[:1, 0:16], start=True, stop=True),
                waits=[(se, t_mrb)], inc=pe)
            t_Mb = ctx.chain("vector", [
                lambda e: e.tensor_copy(Mb[:, 0:16], pone[:, 0:16]),
                lambda e: e.tensor_scalar_mul(Mb[:, 16:32], Mb[:, 0:16], -scale),
            ], waits=[(pe, t_mbb)], inc=ve)
            pone_free = t_Mb

            oT_toks = []
            for h in range(H):
                base = (h % 2) * 64
                cp = h // 2
                eC_tok = 0
                if cfg.get("toep"):
                    for hh in range(4):
                        t_l = ctx.op("sync", lambda e, h=h, hh=hh: e.dma_start(
                            out=Cstg[:],
                            in_=wap[cfg["toep"]][h, :, hh * 1024:(hh + 1) * 1024]),
                            waits=[(se, ldrow.get("Cs", 0))], inc=ld, inc_k=16)
                        t_e = ctx.op("scalar", lambda e, hh=hh: e.activation(
                            expC[:, hh * 1024:(hh + 1) * 1024], Cstg[:], AF.Exp),
                            waits=[(ld, t_l), (ve, eC_free)], inc=se)
                        ldrow["Cs"] = t_e
                        eC_tok = t_e
                mby = Mb[:, 16 + h:16 + h + 1]
                for qsh in range(2):
                    q0 = qsh * 1024
                    kc_hi = 16
                    if cfg.get("causal"):
                        kc_hi = min(16, (q0 + 1023) // 128 + 1)
                    ex_toks = {}
                    av_last = {}
                    av_started = {0: False, 1: False}

                    def emit_st_exp(kc, base=base, cp=cp, q0=q0, mby=mby):
                        def mmst(e, kc=kc, base=base, cp=cp, q0=q0):
                            rr = None
                            for qs in range(2):
                                rr = e.matmul(
                                    pbig[:, (kc % 2) * 1024 + qs * 512:
                                         (kc % 2) * 1024 + (qs + 1) * 512],
                                    kTb[base:base + 64,
                                        cp * TS + kc * 128:cp * TS + (kc + 1) * 128],
                                    qT[base:base + 64,
                                       cp * TS + q0 + qs * 512:cp * TS + q0 + (qs + 1) * 512],
                                    start=True, stop=True)
                            return rr
                        t_st = ctx.op("tensor", mmst,
                                      waits=[(ve, proj_toks[("k", 15)]),
                                             (ve, proj_toks[("q", 15)]),
                                             (se, pbig_free[kc % 2]),
                                             (ve, t_Mb)],
                                      inc=pe)
                        t_ex = ctx.op("scalar", lambda e, kc=kc, mby=mby: e.activation(
                            pbuf[:, (kc % 2) * 1024:(kc % 2 + 1) * 1024],
                            pbig[:, (kc % 2) * 1024:(kc % 2 + 1) * 1024],
                            AF.Exp, bias=mby, scale=scale),
                            waits=[(pe, t_st), (pe, pbuf_free[kc % 2]),
                                   (se, eC_tok)],
                            inc=se)
                        pbig_free[kc % 2] = t_ex
                        fixes = []
                        if cfg.get("toep"):
                            for qs in range(2):
                                n0 = (q0 + qs * 512) - kc * 128 + 1920
                                fixes.append(lambda e, kc=kc, qs=qs, n0=n0: e.tensor_mul(
                                    pbuf[:, (kc % 2) * 1024 + qs * 512:
                                         (kc % 2) * 1024 + (qs + 1) * 512],
                                    pbuf[:, (kc % 2) * 1024 + qs * 512:
                                         (kc % 2) * 1024 + (qs + 1) * 512],
                                    expC[:, n0:n0 + 512]))
                        if cfg.get("causal"):
                            for qs in range(2):
                                off = kc * 128 - (q0 + qs * 512)
                                if 0 <= off < 512:
                                    oi = off // 128
                                    fixes.append(lambda e, kc=kc, qs=qs, oi=oi: e.tensor_mul(
                                        pbuf[:, (kc % 2) * 1024 + qs * 512:
                                             (kc % 2) * 1024 + (qs + 1) * 512],
                                        pbuf[:, (kc % 2) * 1024 + qs * 512:
                                             (kc % 2) * 1024 + (qs + 1) * 512],
                                        cm.causal[:, oi * 512:(oi + 1) * 512]))
                        if fixes:
                            t_fix = ctx.op("vector",
                                           lambda e, fixes=fixes: [f(e) for f in fixes][-1],
                                           waits=[(se, t_ex)], inc=ve)
                            ex_toks[kc] = (ve, t_fix)
                        else:
                            ex_toks[kc] = (se, t_ex)

                    def emit_av(kc, h=h, q0=q0, kc_hi=kc_hi):
                        sem_, tok = ex_toks[kc]
                        for qs in range(2):
                            qb0 = q0 + qs * 512
                            if cfg.get("causal") and kc * 128 > qb0 + 511:
                                continue
                            kmax = min(kc_hi - 1, (qb0 + 511) // 128) \
                                if cfg.get("causal") else 15
                            t_av = ctx.op("tensor", lambda e, kc=kc, qs=qs, kmax=kmax, h=h:
                                          e.matmul(
                                              pmid[0:65, qs * 512:(qs + 1) * 512],
                                              v_row[:, kc * 1040 + h * 65:
                                                    kc * 1040 + (h + 1) * 65],
                                              pbuf[:, (kc % 2) * 1024 + qs * 512:
                                                   (kc % 2) * 1024 + (qs + 1) * 512],
                                              start=(kc == 0), stop=(kc == kmax)),
                                          waits=[(sem_, tok), (ve, pmid_free),
                                                 (ve, proj_toks[("v", 15)])],
                                          inc=pe)
                            av_last[qs] = t_av
                        pbuf_free[kc % 2] = pe.v

                    emit_st_exp(0)
                    for kc in range(1, kc_hi):
                        emit_st_exp(kc)
                        emit_av(kc - 1)
                    emit_av(kc_hi - 1)

                    for qs in range(2):
                        t_de = ctx.op("vector", lambda e, qs=qs: e.tensor_copy(
                            den_sb[:1, :], pmid[64:65, qs * 512:(qs + 1) * 512]),
                            waits=[(pe, av_last[qs]), (ve, pone_free)], inc=ve)
                        t_rd = ctx.chain("scalar", [
                            lambda e: e.activation(den_sb[:1, :], den_sb[:1, :], AF.Ln),
                            lambda e: e.activation(den_sb[:1, :], den_sb[:1, :],
                                                   AF.Exp, scale=-1.0),
                            lambda e: e.copy(rden_bf[:1, 0:512], den_sb[:1, :]),
                        ], waits=[(ve, t_de)], inc=se)
                        t_rb = ctx.op("tensor", lambda e: e.matmul(
                            pone[:, 0:512], cm.ones1[:], rden_bf[:1, 0:512],
                            start=True, stop=True),
                            waits=[(se, t_rd)], inc=pe)
                        t_rB = ctx.op("vector", lambda e: e.tensor_copy(
                            rdenB[:], pone[:, 0:512]),
                            waits=[(pe, t_rb)], inc=ve)
                        pone_free = t_rB
                        dst = oT[base:base + 64,
                                 cp * TS + q0 + qs * 512:cp * TS + q0 + (qs + 1) * 512]
                        t_o = ctx.op("vector", lambda e, dst=dst, qs=qs: e.tensor_mul(
                            dst, pmid[0:64, qs * 512:(qs + 1) * 512], rdenB[0:64, :]),
                            waits=[(ve, t_rB)], inc=ve)
                        oT_toks.append(t_o)
                        if qs == 1:
                            pmid_free = t_o
                eC_free = ve.v
            oT_ready = oT_toks[-1]

            if cfg.get("wo"):
                wtok_o = load_w(cfg["wo"])
                res_src = x_src if cfg.get("res_src", "x") == "x" else cfg["save_h"]
                bias_o = 3 if (cfg.get("bias") and cfg["bias"][3]) else None
                for ti in range(16):
                    g = tok0 // 128 + ti
                    slot = ti % 2
                    xw = ctx.op("sync", lambda e, g=g, slot=slot: e.dma_start(
                        out=xb[:, slot * D:(slot + 1) * D],
                        in_=res_src[g * 128:(g + 1) * 128, :]),
                        waits=[(ve, ldrow.get(("xb", slot), 0)),
                               (st_s, store_toks.get(("h", r, ti), 0))],
                        inc=ld, inc_k=16)
                    def mmo(e, ti=ti, bias_o=bias_o):
                        rr = None
                        for n2 in range(2):
                            for c in range(8):
                                rr = e.matmul(pmid[:, n2 * 512:(n2 + 1) * 512],
                                              oT[:, c * TS + ti * 128:c * TS + (ti + 1) * 128],
                                              wu[:, c * D + n2 * 512:c * D + (n2 + 1) * 512],
                                              start=(c == 0),
                                              stop=(c == 7 and bias_o is None))
                            if bias_o is not None:
                                rr = e.matmul(pmid[:, n2 * 512:(n2 + 1) * 512], cm.ones1[:],
                                              bias_sb[:1, bias_o * D + n2 * 512:
                                                      bias_o * D + (n2 + 1) * 512],
                                              start=False, stop=True)
                        return rr
                    t_mm = ctx.op("tensor", mmo,
                                  waits=[(gld, wtok_o), (ve, oT_ready), (ve, pmid_free)],
                                  inc=pe)
                    t_x = ctx.op("vector", lambda e, slot=slot: e.tensor_add(
                        xb[:, slot * D:(slot + 1) * D],
                        xb[:, slot * D:(slot + 1) * D], pmid[:]),
                        waits=[(pe, t_mm), (ld, xw)], inc=ve)
                    pmid_free = t_x
                    ldrow[("xb", slot)] = t_x
                    store_toks[(r, ti)] = ctx.op(
                        "sync", lambda e, g=g, slot=slot: e.dma_start(
                            out=x_dst[g * 128:(g + 1) * 128, :],
                            in_=xb[:, slot * D:(slot + 1) * D]),
                        waits=[(ve, t_x)], inc=st_s, inc_k=16)
                load_w.free = pe.v
            else:
                for ti in range(16):
                    g = tok0 // 128 + ti
                    slot = ti % 2
                    xw = ctx.op("sync", lambda e, g=g, slot=slot: e.dma_start(
                        out=xb[:, slot * D:(slot + 1) * D],
                        in_=x_src[g * 128:(g + 1) * 128, :]),
                        waits=[(ve, ldrow.get(("xb", slot), 0))],
                        inc=ld, inc_k=16)
                    def pe_tr(e, ti=ti):
                        rr = None
                        for c in range(8):
                            rr = e.transpose(ptr[:, c * 128:(c + 1) * 128],
                                             oT[:, c * TS + ti * 128:c * TS + (ti + 1) * 128],
                                             cm.ident[:])
                        return rr
                    t_tr = ctx.op("tensor", pe_tr,
                                  waits=[(ve, oT_ready), (ve, ptr_free)], inc=pe)
                    t_x = ctx.op("vector", lambda e, slot=slot: e.tensor_add(
                        xb[:, slot * D:(slot + 1) * D],
                        xb[:, slot * D:(slot + 1) * D], ptr[:]),
                        waits=[(pe, t_tr), (ld, xw)], inc=ve)
                    ptr_free = t_x
                    ldrow[("xb", slot)] = t_x
                    store_toks[(r, ti)] = ctx.op(
                        "sync", lambda e, g=g, slot=slot: e.dma_start(
                            out=x_dst[g * 128:(g + 1) * 128, :],
                            in_=xb[:, slot * D:(slot + 1) * D]),
                        waits=[(ve, t_x)], inc=st_s, inc_k=16)
        ctx.op("sync", lambda e: None, waits=[(st_s, st_s.v)])



def emit_l0_attn(ctx, cm, wap, x_src, x_dst):
    nc = ctx.nc
    nm = "p0"
    ctx.serial = True
    with contextlib.ExitStack() as es:
        qT = es.enter_context(nc.sbuf_tensor(nm + "_qT", [128, 8 * TS], BF16))
        k_row = es.enter_context(nc.sbuf_tensor(nm + "_k", [128, 16 * D], BF16))
        v_row = es.enter_context(nc.sbuf_tensor(nm + "_v", [128, 16 * D], BF16))
        hT = es.enter_context(nc.sbuf_tensor(nm + "_hT", [128, 16 * D], BF16))
        oT = hT
        wu = es.enter_context(nc.sbuf_tensor(nm + "_wu", [128, 8 * D], BF16))
        xb = es.enter_context(nc.sbuf_tensor(nm + "_xb", [128, 2 * D], F32))
        hb = es.enter_context(nc.sbuf_tensor(nm + "_hb", [128, D], BF16))
        qf32 = es.enter_context(nc.sbuf_tensor(nm + "_qf", [128, D], F32))
        qln = es.enter_context(nc.sbuf_tensor(nm + "_ql", [128, D], BF16))
        qtok = es.enter_context(nc.sbuf_tensor(nm + "_qt", [128, D], BF16))
        sqb = es.enter_context(nc.sbuf_tensor(nm + "_sq", [128, D], F32))
        sv = es.enter_context(nc.sbuf_tensor(nm + "_sv", [128, 16 * 8], F32))
        st16 = es.enter_context(nc.sbuf_tensor(nm + "_st16", [128, 64], F32))
        gB = es.enter_context(nc.sbuf_tensor(nm + "_gB", [128, 4 * D], BF16))
        gr = es.enter_context(nc.sbuf_tensor(nm + "_gr", [1, 4 * D], BF16))
        S_sb = es.enter_context(nc.sbuf_tensor(nm + "_S", [128, 64], BF16))
        kTc = es.enter_context(nc.sbuf_tensor(nm + "_kTc", [128, 128], BF16))
        ATm = es.enter_context(nc.sbuf_tensor(nm + "_ATm", [128, 128], BF16))
        obuf = es.enter_context(nc.sbuf_tensor(nm + "_ob", [128, 64], BF16))
        triu = es.enter_context(nc.sbuf_tensor(nm + "_tri", [128, 128], BF16))
        pmid = es.enter_context(nc.psum_tensor(nm + "_pM", [128, 1024], F32))
        ptr = es.enter_context(nc.psum_tensor(nm + "_pT", [128, 1024], BF16))
        psS = es.enter_context(nc.psum_tensor(nm + "_pS", [64, 64], F32))
        psA = es.enter_context(nc.psum_tensor(nm + "_pA", [128, 128], F32))
        psO = es.enter_context(nc.psum_tensor(nm + "_pO", [128, 64], F32))

        loader = CastLoader(ctx, es, nm, 1024, 2)
        gld = loader.done
        pe = ctx.sem(nm + "_pe")
        ve = ctx.sem(nm + "_ve")
        se = ctx.sem(nm + "_se")
        ld = ctx.sem(nm + "_ld")
        st_s = ctx.sem(nm + "_st")

        t_tri = ctx.op("sync", lambda e: e.dma_start(out=triu[:], in_=wap["c_triu"]),
                       inc=ld, inc_k=16)
        gtok = 0
        for j, bn in enumerate(("l0a_qg_full", "l0a_qb_full",
                                "l0a_kg_full", "l0a_kb_full")):
            gtok = loader.load(gr[:1, j * D:(j + 1) * D], wap[bn], D, nparts=1)
        pm_free = 0
        gB_tok = 0
        for j in range(4):
            t_b = ctx.op("tensor", lambda e, j=j: [
                e.matmul(pmid[:, n2 * 512:(n2 + 1) * 512], cm.ones1[:],
                         gr[:1, j * D + n2 * 512:j * D + (n2 + 1) * 512],
                         start=True, stop=True) for n2 in range(2)][-1],
                waits=[(gld, gtok), (cm.ldc, cm.ld_tok), (ve, pm_free)],
                inc=pe)
            gB_tok = ctx.op("vector", lambda e, j=j: e.tensor_copy(
                gB[:, j * D:(j + 1) * D], pmid[:]),
                waits=[(pe, t_b)], inc=ve)
            pm_free = gB_tok

        def load_w(name):
            t = None
            for hh in range(8):
                src = wap[name][hh * 128:(hh + 1) * 128, :]
                t = loader.load(wu[:, hh * D:(hh + 1) * D], src, 1024,
                                ring_waits=[(pe, load_w.free)] if hh == 0 else ())
            return t
        load_w.free = 0

        ptr_free = 0
        hb_free = 0
        qtok_free = 0
        qf_free = 0
        qln_free = 0
        ldrow = {}
        store_toks = {}
        st3 = {}

        for r in range(R):
            tok0 = r * TS
            h_toks = []
            wtok_q = load_w("l0a_wqT")
            for ti in range(16):
                g = tok0 // 128 + ti
                slot = ti % 2
                xw = ctx.op("sync", lambda e, g=g, slot=slot: e.dma_start(
                    out=xb[:, slot * D:(slot + 1) * D],
                    in_=x_src[g * 128:(g + 1) * 128, :]),
                    waits=[(ve, ldrow.get(("xb", slot), 0))],
                    inc=ld, inc_k=16)
                sc_t, h_t = emit_ln_tile(ctx, cm, xb[:, slot * D:(slot + 1) * D],
                                         hb[:], sv[:, ti * 8:ti * 8 + 8],
                                         [(ld, xw), (ve, hb_free)], se, ve)
                ldrow[("xb", slot)] = h_t
                tr_t = emit_transpose_h(ctx, cm, hb[:], ptr,
                                        hT[:, ti * D:(ti + 1) * D], 8,
                                        [(ve, h_t), (ve, ptr_free)], pe, ve)
                ptr_free = tr_t
                hb_free = tr_t
                h_toks.append(tr_t)

            proj_last = {}
            for pi, (pname, wname) in enumerate(
                    (("q", "l0a_wqT"), ("k", "l0a_wkT"), ("v", "l0a_wvT"))):
                wtok = wtok_q if pi == 0 else load_w(wname)
                for ti in range(16):
                    def mmp(e, ti=ti):
                        rr = None
                        for n2 in range(2):
                            for c in range(8):
                                rr = e.matmul(pmid[:, n2 * 512:(n2 + 1) * 512],
                                              hT[:, ti * D + c * 128:ti * D + (c + 1) * 128],
                                              wu[:, c * D + n2 * 512:c * D + (n2 + 1) * 512],
                                              start=(c == 0), stop=(c == 7))
                        return rr
                    t_mm = ctx.op("tensor", mmp,
                                  waits=[(gld, wtok), (ve, h_toks[ti]), (ve, pm_free)],
                                  inc=pe)
                    if pname == "v":
                        t_ev = ctx.op("vector", lambda e, ti=ti: e.tensor_copy(
                            v_row[:, ti * D:(ti + 1) * D], pmid[:]),
                            waits=[(pe, t_mm)], inc=ve)
                        pm_free = t_ev
                        proj_last["v"] = t_ev
                        continue
                    t_cp = ctx.op("vector", lambda e: e.tensor_copy(qf32[:], pmid[:]),
                                  waits=[(pe, t_mm), (ve, qf_free)], inc=ve)
                    pm_free = t_cp
                    t_st = ctx.chain("vector", [
                        lambda e: e.tensor_reduce(
                            st16[:, 0:16], qf32[:].rearrange("p (h d) -> p h d", h=16),
                            AX.X, ALU.add),
                        lambda e: e.tensor_scalar_mul(st16[:, 0:16], st16[:, 0:16], 1.0 / 64),
                        lambda e: e.tensor_mul(sqb[:], qf32[:], qf32[:]),
                        lambda e: e.tensor_reduce(
                            st16[:, 16:32], sqb[:].rearrange("p (h d) -> p h d", h=16),
                            AX.X, ALU.add),
                        lambda e: e.tensor_scalar_mul(st16[:, 16:32], st16[:, 16:32], 1.0 / 64),
                        lambda e: e.tensor_mul(st16[:, 32:48], st16[:, 0:16], st16[:, 0:16]),
                        lambda e: e.tensor_sub(st16[:, 16:32], st16[:, 16:32], st16[:, 32:48]),
                    ], waits=[(ve, t_cp)], inc=ve)
                    t_rs = ctx.chain("scalar", [
                        lambda e: e.activation(st16[:, 32:48], st16[:, 16:32], AF.Ln,
                                               bias=cm.eps[:, 0:1]),
                        lambda e: e.activation(st16[:, 32:48], st16[:, 32:48], AF.Exp,
                                               scale=-0.5),
                    ], waits=[(ve, t_st)], inc=se)
                    goff = 0 if pname == "q" else 2 * D
                    steps = []
                    for hh in range(16):
                        steps.append(lambda e, hh=hh: e.tensor_scalar(
                            qln[:, hh * 64:(hh + 1) * 64], qf32[:, hh * 64:(hh + 1) * 64],
                            st16[:, hh:hh + 1], st16[:, 32 + hh:33 + hh],
                            ALU.subtract, ALU.mult))
                    steps.append(lambda e, goff=goff: e.tensor_mul(
                        qln[:], qln[:], gB[:, goff:goff + D]))
                    if pname == "q":
                        steps.append(lambda e, goff=goff: e.tensor_add(
                            qtok[:], qln[:], gB[:, goff + D:goff + 2 * D]))
                    else:
                        steps.append(lambda e, ti=ti, goff=goff: e.tensor_add(
                            k_row[:, ti * D:(ti + 1) * D], qln[:],
                            gB[:, goff + D:goff + 2 * D]))
                    t_ap = ctx.chain("vector", steps,
                                     waits=[(se, t_rs), (ve, gB_tok),
                                            (ve, qln_free),
                                            (ve, qtok_free if pname == "q" else 0)],
                                     inc=ve)
                    qf_free = t_ap
                    qln_free = t_ap
                    if pname == "q":
                        def pe_tr(e):
                            rr = None
                            for c in range(8):
                                rr = e.transpose(ptr[:, c * 128:(c + 1) * 128],
                                                 qtok[:, c * 128:(c + 1) * 128],
                                                 cm.ident[:])
                            return rr
                        t_tr = ctx.op("tensor", pe_tr,
                                      waits=[(ve, t_ap), (ve, ptr_free)], inc=pe)
                        dst = qT[:].rearrange("p (c t) -> p c t", c=8)[
                            :, :, ti * 128:(ti + 1) * 128]
                        t_te = ctx.op("vector", lambda e, dst=dst: e.tensor_copy(
                            dst, ptr[:].rearrange("p (c t) -> p c t", c=8)),
                            waits=[(pe, t_tr)], inc=ve)
                        ptr_free = t_te
                        qtok_free = t_te
                        proj_last["q"] = t_te
                    else:
                        proj_last["k"] = t_ap
                load_w.free = pe.v

            oT_toks = []
            for h in range(H):
                base = (h % 2) * 64
                cp = h // 2
                s_upd = 0
                for i in range(16):
                    t_kt = ctx.op("tensor", lambda e, h=h, i=i: e.transpose(
                        ptr[0:64, 0:128], k_row[:, i * D + h * 64:i * D + (h + 1) * 64],
                        cm.ident[:]),
                        waits=[(ve, proj_last["k"]), (ve, st3.get("ptr", ptr_free))],
                        inc=pe)
                    t_kte = ctx.op("vector", lambda e, base=base: e.tensor_copy(
                        kTc[base:base + 64, :], ptr[0:64, 0:128]),
                        waits=[(pe, t_kt), (pe, st3.get("kTc", 0))], inc=ve)
                    st3["ptr"] = t_kte
                    t_at = ctx.op("tensor", lambda e, base=base, cp=cp, i=i: e.matmul(
                        psA[:], kTc[base:base + 64, :],
                        qT[base:base + 64, cp * TS + i * 128:cp * TS + (i + 1) * 128],
                        start=True, stop=True),
                        waits=[(ve, t_kte), (ve, proj_last["q"]),
                               (ve, st3.get("ATm_v", 0))],
                        inc=pe)
                    st3["kTc"] = t_at
                    t_am = ctx.op("vector", lambda e: e.tensor_mul(
                        ATm[:], psA[:], triu[:]),
                        waits=[(pe, t_at), (pe, st3.get("ATm_pe", 0)), (ld, t_tri)],
                        inc=ve)
                    st3["ATm_v"] = t_am
                    if i > 0:
                        t_se = ctx.op("vector", lambda e, base=base: e.tensor_copy(
                            S_sb[base:base + 64, :], psS[0:64, :]),
                            waits=[(pe, st3.get("S_pe", 0)), (pe, s_upd)], inc=ve)
                        st3["se_last"] = t_se
                    else:
                        t_se = 0
                    def mmo(e, h=h, i=i, base=base, cp=cp, first=(i == 0)):
                        rr = e.matmul(psO[:], ATm[:],
                                      v_row[:, i * D + h * 64:i * D + (h + 1) * 64],
                                      start=True, stop=first)
                        if not first:
                            rr = e.matmul(psO[:],
                                          qT[base:base + 64,
                                             cp * TS + i * 128:cp * TS + (i + 1) * 128],
                                          S_sb[base:base + 64, :], start=False, stop=True)
                        return rr
                    t_o = ctx.op("tensor", mmo,
                                 waits=[(ve, t_am), (ve, t_se), (ve, proj_last["v"]),
                                        (ve, st3.get("ob_v", 0))],
                                 inc=pe)
                    st3["ATm_pe"] = t_o
                    st3["S_pe"] = t_o
                    t_su = ctx.op("tensor", lambda e, h=h, i=i: e.matmul(
                        psS[0:64, :], k_row[:, i * D + h * 64:i * D + (h + 1) * 64],
                        v_row[:, i * D + h * 64:i * D + (h + 1) * 64],
                        start=(i == 0), stop=(i == 15)),
                        waits=[(ve, t_se if i else st3.get("se_last", 0))], inc=pe)
                    s_upd = t_su
                    t_oe = ctx.op("vector", lambda e: e.tensor_copy(obuf[:], psO[:]),
                                  waits=[(pe, t_o), (pe, st3.get("ob_pe", 0))], inc=ve)
                    st3["ob_v"] = t_oe
                    t_ot = ctx.op("tensor", lambda e: e.transpose(
                        ptr[0:64, 128:256], obuf[:, 0:64], cm.ident[:]),
                        waits=[(ve, t_oe), (ve, st3.get("ptr", 0))], inc=pe)
                    st3["ob_pe"] = t_ot
                    t_ote = ctx.op("vector", lambda e, base=base, cp=cp, i=i: e.tensor_copy(
                        oT[base:base + 64, cp * TS + i * 128:cp * TS + (i + 1) * 128],
                        ptr[0:64, 128:256]),
                        waits=[(pe, t_ot)], inc=ve)
                    st3["ptr"] = t_ote
                    oT_toks.append(t_ote)
            ptr_free = st3["ptr"]
            oT_ready = oT_toks[-1]

            wtok_o = load_w("l0a_woT")
            for ti in range(16):
                g = tok0 // 128 + ti
                slot = ti % 2
                xw = ctx.op("sync", lambda e, g=g, slot=slot: e.dma_start(
                    out=xb[:, slot * D:(slot + 1) * D],
                    in_=x_src[g * 128:(g + 1) * 128, :]),
                    waits=[(ve, ldrow.get(("xb", slot), 0))],
                    inc=ld, inc_k=16)
                def mmo4(e, ti=ti):
                    rr = None
                    for n2 in range(2):
                        for c in range(8):
                            rr = e.matmul(pmid[:, n2 * 512:(n2 + 1) * 512],
                                          oT[:, c * TS + ti * 128:c * TS + (ti + 1) * 128],
                                          wu[:, c * D + n2 * 512:c * D + (n2 + 1) * 512],
                                          start=(c == 0), stop=(c == 7))
                    return rr
                t_mm = ctx.op("tensor", mmo4,
                              waits=[(gld, wtok_o), (ve, oT_ready), (ve, pm_free)],
                              inc=pe)
                t_x = ctx.op("vector", lambda e, slot=slot: e.tensor_add(
                    xb[:, slot * D:(slot + 1) * D],
                    xb[:, slot * D:(slot + 1) * D], pmid[:]),
                    waits=[(pe, t_mm), (ld, xw)], inc=ve)
                pm_free = t_x
                ldrow[("xb", slot)] = t_x
                store_toks[(r, ti)] = ctx.op(
                    "sync", lambda e, g=g, slot=slot: e.dma_start(
                        out=x_dst[g * 128:(g + 1) * 128, :],
                        in_=xb[:, slot * D:(slot + 1) * D]),
                    waits=[(ve, t_x)], inc=st_s, inc_k=16)
            load_w.free = pe.v
        ctx.op("sync", lambda e: None, waits=[(st_s, st_s.v)])
    ctx.serial = False


def emit_l3_moe(ctx, cm, wap, a_src, x_src, x_dst, wait_sem, wait_tok):
    nc = ctx.nc
    nm = "p7b"
    NE = 8
    NSTM = 16
    with contextlib.ExitStack() as es:
        ew1_sb = es.enter_context(nc.sbuf_tensor(nm + "_w1", [128, 8 * 4096], BF16))
        ew2_sb = es.enter_context(nc.sbuf_tensor(nm + "_w2", [128, 32 * 1024], BF16))
        eb1_sb = es.enter_context(nc.sbuf_tensor(nm + "_b1", [128, 32], F32))
        eb2m = es.enter_context(nc.sbuf_tensor(nm + "_b2", [8, 1024], BF16))
        midF = es.enter_context(nc.sbuf_tensor(nm + "_mid", [128, 32 * 256], BF16))
        h2T = es.enter_context(nc.sbuf_tensor(nm + "_h2T", [128, 8 * 256], BF16))
        ab = es.enter_context(nc.sbuf_tensor(nm + "_ab", [128, 2 * D], F32))
        ac = es.enter_context(nc.sbuf_tensor(nm + "_ac", [128, 2 * D], F32))
        hf = es.enter_context(nc.sbuf_tensor(nm + "_hf", [128, D], F32))
        h2b = es.enter_context(nc.sbuf_tensor(nm + "_h2b", [128, D], BF16))
        affb = es.enter_context(nc.sbuf_tensor(nm + "_aff", [128, 2 * D], BF16))
        affr = es.enter_context(nc.sbuf_tensor(nm + "_afr", [1, 2 * D], BF16))
        sv = es.enter_context(nc.sbuf_tensor(nm + "_sv", [128, 16], F32))
        wr_sb = es.enter_context(nc.sbuf_tensor(nm + "_wr", [128, 64], BF16))
        rb_sb = es.enter_context(nc.sbuf_tensor(nm + "_rb", [1, 16], BF16))
        lgb = es.enter_context(nc.sbuf_tensor(nm + "_lg", [128, 8], F32))
        web = es.enter_context(nc.sbuf_tensor(nm + "_we", [128, 8], F32))
        t8 = es.enter_context(nc.sbuf_tensor(nm + "_t8", [128, 16], F32))
        w_st = es.enter_context(nc.sbuf_tensor(nm + "_wst", [128, NTILES * 8], F32))
        wbf = es.enter_context(nc.sbuf_tensor(nm + "_wbf", [128, 8], BF16))
        wT_st = es.enter_context(nc.sbuf_tensor(nm + "_wT", [8, NTILES * 128], BF16))
        psF = es.enter_context(nc.psum_tensor(nm + "_pF", [128, 256], F32))
        psO = es.enter_context(nc.psum_tensor(nm + "_pO", [128, D], F32))
        psO2 = es.enter_context(nc.psum_tensor(nm + "_pO2", [128, D], F32))
        ps_tr = es.enter_context(nc.psum_tensor(nm + "_pT", [128, 1024], BF16))
        ps_r = es.enter_context(nc.psum_tensor(nm + "_pR", [128, 8], F32))

        loader = CastLoader(ctx, es, nm, 2048, 1)
        gld = loader.done
        pe = ctx.sem(nm + "_pe")
        ve = ctx.sem(nm + "_ve")
        se = ctx.sem(nm + "_se")
        ld = ctx.sem(nm + "_ld")
        st_s = ctx.sem(nm + "_st")

        loader.load(wr_sb[:], wap["l3f_wr"].rearrange("(c p) n -> p c n", p=128), 64)
        loader.load(rb_sb[:1, 0:8], wap["l3f_br"], 8, nparts=1)
        loader.load(eb2m[:8, :], wap["l3f_eb2"], 1024, nparts=8)
        loader.load(affr[:1, 0:D], wap["l3f_g2"], D, nparts=1)
        tok_static = loader.load(affr[:1, D:2 * D], wap["l3f_b2"], D, nparts=1)
        pm_free = 0
        aff_toks = []
        for j in range(2):
            t_b = ctx.op("tensor", lambda e, j=j: [
                e.matmul(psO[:, n2 * 512:(n2 + 1) * 512], cm.ones1[:],
                         affr[:1, j * D + n2 * 512:j * D + (n2 + 1) * 512],
                         start=True, stop=True) for n2 in range(2)][-1],
                waits=[(gld, tok_static), (cm.ldc, cm.ld_tok), (ve, pm_free)],
                inc=pe)
            t_a = ctx.op("vector", lambda e, j=j: e.tensor_copy(
                affb[:, j * D:(j + 1) * D], psO[:]),
                waits=[(pe, t_b)], inc=ve)
            pm_free = t_a
            aff_toks.append(t_a)
        aff_tok = aff_toks[-1]

        psF_free = 0
        psO_free = [pm_free, 0]
        ptr_free = 0
        psr_free = 0
        h2b_free = 0
        ab_free = {}
        ac_free = {}
        w_toks = {}
        store_toks = {}

        for e_ in range(NE):
            t = None
            for dc in range(8):
                for hh in range(2):
                    src = wap["l3f_ew1"][e_, dc * 128:(dc + 1) * 128,
                                         hh * 2048:(hh + 1) * 2048]
                    t = loader.load(ew1_sb[:, dc * 4096 + hh * 2048:
                                           dc * 4096 + (hh + 1) * 2048],
                                    src, 2048,
                                    ring_waits=[(pe, w_toks.get(("w1f", e_ - 1), 0))]
                                    if dc == 0 and hh == 0 else ())
            tok_w1 = t
            for fc in range(32):
                src = wap["l3f_ew2"][e_, fc * 128:(fc + 1) * 128, :]
                t = loader.load(ew2_sb[:, fc * 1024:(fc + 1) * 1024], src, 1024,
                                ring_waits=[(pe, w_toks.get(("w2f", e_ - 1), 0))]
                                if fc == 0 else ())
            tok_w2 = t
            tok_b1 = loader.load(eb1_sb[:, 0:32], wap["l3f_eb1T"][e_], 32,
                                 ring_waits=[(se, w_toks.get(("b1f", e_ - 1), 0))])
            for s in range(NSTM):
                tis = (2 * s, 2 * s + 1)
                h2_toks = []
                for j, g in enumerate(tis):
                    aw = ctx.op("sync", lambda e, g=g, j=j: e.dma_start(
                        out=ab[:, j * D:(j + 1) * D],
                        in_=a_src[g * 128:(g + 1) * 128, :]),
                        waits=[(ve, ab_free.get(j, 0))] +
                              ([(wait_sem, wait_tok)] if e_ == 0 and s == 0 else []),
                        inc=ld, inc_k=16)
                    sc_t, hf_t = emit_ln_tile(ctx, cm, ab[:, j * D:(j + 1) * D],
                                              hf[:], sv[:, j * 8:j * 8 + 8],
                                              [(ld, aw)], se, ve)
                    t_h2 = ctx.chain("vector", [
                        lambda e: e.tensor_mul(hf[:], hf[:], affb[:, 0:D]),
                        lambda e: e.tensor_add(hf[:], hf[:], affb[:, D:2 * D]),
                        lambda e: e.tensor_copy(h2b[:], hf[:]),
                    ], waits=[(ve, hf_t), (ve, aff_tok), (ve, h2b_free)], inc=ve)
                    ab_free[j] = t_h2
                    def pe_tr(e, j=j):
                        rr = None
                        for c in range(8):
                            rr = e.transpose(ps_tr[:, c * 128:(c + 1) * 128],
                                             h2b[:, c * 128:(c + 1) * 128], cm.ident[:])
                        return rr
                    t_tr = ctx.op("tensor", pe_tr,
                                  waits=[(ve, t_h2), (ve, ptr_free)], inc=pe)
                    dst = h2T[:].rearrange("p (c t) -> p c t", c=8)[:, :, j * 128:(j + 1) * 128]
                    t_te = ctx.op("vector", lambda e, dst=dst: e.tensor_copy(
                        dst, ps_tr[:].rearrange("p (c t) -> p c t", c=8)),
                        waits=[(pe, t_tr)], inc=ve)
                    ptr_free = t_te
                    h2b_free = t_te
                    h2_toks.append(t_te)
                    if e_ == 0:
                        def mmr(e, j=j):
                            rr = None
                            for c in range(8):
                                rr = e.matmul(ps_r[:],
                                              h2T[:, c * 256 + j * 128:c * 256 + (j + 1) * 128],
                                              wr_sb[:, c * 8:(c + 1) * 8],
                                              start=(c == 0), stop=False)
                            return e.matmul(ps_r[:], cm.ones1[:], rb_sb[:1, 0:8],
                                            start=False, stop=True)
                        t_mm = ctx.op("tensor", mmr,
                                      waits=[(ve, t_te), (se, psr_free)], inc=pe)
                        t_lg = ctx.chain("vector", [
                            lambda e: e.tensor_copy(lgb[:], ps_r[:]),
                            lambda e: e.tensor_reduce(t8[:, 0:1], lgb[:], AX.X, ALU.max),
                            lambda e: e.tensor_scalar(t8[:, 8:16], lgb[:], t8[:, 0:1],
                                                      None, ALU.is_ge),
                            lambda e: e.scalar_tensor_tensor(t8[:, 8:16], t8[:, 8:16],
                                                             cm.negbig[:], lgb[:],
                                                             ALU.mult, ALU.add),
                            lambda e: e.tensor_reduce(t8[:, 1:2], t8[:, 8:16], AX.X, ALU.max),
                            lambda e: e.tensor_scalar(t8[:, 8:16], lgb[:], t8[:, 1:2],
                                                      None, ALU.is_ge),
                            lambda e: e.tensor_scalar_mul(t8[:, 2:3], t8[:, 0:1], -1.0),
                        ], waits=[(pe, t_mm), (cm.ldc, cm.ld_tok)], inc=ve)
                        t_we = ctx.op("scalar", lambda e: e.activation(
                            web[:], ps_r[:], AF.Exp, bias=t8[:, 2:3]),
                            waits=[(ve, t_lg)], inc=se)
                        psr_free = t_we
                        t_w = ctx.chain("vector", [
                            lambda e: e.tensor_mul(web[:], web[:], t8[:, 8:16]),
                            lambda e: e.tensor_reduce(t8[:, 3:4], web[:], AX.X, ALU.add),
                        ], waits=[(se, t_we)], inc=ve)
                        t_rs = ctx.chain("scalar", [
                            lambda e: e.activation(t8[:, 4:5], t8[:, 3:4], AF.Ln),
                            lambda e: e.activation(t8[:, 4:5], t8[:, 4:5], AF.Exp,
                                                   scale=-1.0),
                        ], waits=[(ve, t_w)], inc=se)
                        wcol = w_st[:, g * 8:(g + 1) * 8]
                        t_cw = ctx.chain("vector", [
                            lambda e, wcol=wcol: e.tensor_scalar(
                                wcol, web[:], t8[:, 4:5], None, ALU.mult),
                            lambda e, wcol=wcol: e.tensor_copy(wbf[:], wcol),
                        ], waits=[(se, t_rs)], inc=ve)
                        t_wt = ctx.op("tensor", lambda e: e.transpose(
                            ps_tr[0:8, 0:128], wbf[:], cm.ident[:]),
                            waits=[(ve, t_cw), (ve, ptr_free)], inc=pe)
                        t_wte = ctx.op("vector", lambda e, g=g: e.tensor_copy(
                            wT_st[:8, g * 128:(g + 1) * 128], ps_tr[:8, 0:128]),
                            waits=[(pe, t_wt)], inc=ve)
                        ptr_free = t_wte
                        w_toks[("w", g)] = t_wte
                for fc in range(32):
                    def mm1(e, fc=fc):
                        rr = None
                        for dc in range(8):
                            rr = e.matmul(psF[:],
                                          ew1_sb[:, dc * 4096 + fc * 128:
                                                 dc * 4096 + fc * 128 + 128],
                                          h2T[:, dc * 256:(dc + 1) * 256],
                                          start=(dc == 0), stop=(dc == 7))
                        return rr
                    tA = ctx.op("tensor", mm1,
                                waits=[(gld, tok_w1), (ve, h2_toks[-1]),
                                       (se, psF_free)],
                                inc=pe)
                    t_g = ctx.op("scalar", lambda e, fc=fc: e.activation(
                        midF[:, fc * 256:(fc + 1) * 256], psF[:], AF.Gelu,
                        bias=eb1_sb[:, fc:fc + 1]),
                        waits=[(pe, tA), (gld, tok_b1), (pe, w_toks.get("mf_last", 0))],
                        inc=se)
                    psF_free = t_g
                for j, g in enumerate(tis):
                    ps = psO if j == 0 else psO2
                    if e_ == 0:
                        aw = ctx.op("sync", lambda e, g=g, j=j: e.dma_start(
                            out=ac[:, j * D:(j + 1) * D],
                            in_=x_src[g * 128:(g + 1) * 128, :]),
                            waits=[(st_s, ac_free.get(j, 0))], inc=ld, inc_k=16)
                        t_in = ctx.op("vector", lambda e, j=j: e.tensor_add(
                            ac[:, j * D:(j + 1) * D], ac[:, j * D:(j + 1) * D],
                            ab[:, j * D:(j + 1) * D]),
                            waits=[(ld, aw)], inc=ve)
                        ab_free[j] = t_in
                    else:
                        aw = ctx.op("sync", lambda e, g=g, j=j: e.dma_start(
                            out=ac[:, j * D:(j + 1) * D],
                            in_=x_dst[g * 128:(g + 1) * 128, :]),
                            waits=[(st_s, ac_free.get(j, 0)),
                                   (st_s, store_toks.get((e_ - 1, g), 0))],
                            inc=ld, inc_k=16)
                        t_in = None
                    def mm2(e, j=j, ps=ps, g=g, e_=e_):
                        rr = None
                        for n2 in range(2):
                            for fc in range(32):
                                rr = e.matmul(ps[:, n2 * 512:(n2 + 1) * 512],
                                              midF[:, fc * 256 + j * 128:
                                                   fc * 256 + j * 128 + 128],
                                              ew2_sb[:, fc * 1024 + n2 * 512:
                                                     fc * 1024 + (n2 + 1) * 512],
                                              start=(fc == 0),
                                              stop=(fc == 31 and e_ != NE - 1))
                            if e_ == NE - 1:
                                rr = e.matmul(ps[:, n2 * 512:(n2 + 1) * 512],
                                              wT_st[:8, g * 128:(g + 1) * 128],
                                              eb2m[:8, n2 * 512:(n2 + 1) * 512],
                                              start=False, stop=True)
                        return rr
                    t_mm = ctx.op("tensor", mm2,
                                  waits=[(gld, tok_w2), (se, psF_free),
                                         (ve, psO_free[j]),
                                         (ve, w_toks.get(("w", g), 0))],
                                  inc=pe)
                    wcol = w_st[:, g * 8 + e_:g * 8 + e_ + 1]
                    waits_acc = [(pe, t_mm), (ld, aw)]
                    if t_in is not None:
                        waits_acc.append((ve, t_in))
                    t_acc = ctx.op("vector", lambda e, j=j, ps=ps, wcol=wcol:
                                   e.scalar_tensor_tensor(
                                       ac[:, j * D:(j + 1) * D], ps[:], wcol,
                                       ac[:, j * D:(j + 1) * D], ALU.mult, ALU.add),
                                   waits=waits_acc, inc=ve)
                    psO_free[j] = t_acc
                    store_toks[(e_, g)] = ctx.op(
                        "sync", lambda e, g=g, j=j: e.dma_start(
                            out=x_dst[g * 128:(g + 1) * 128, :],
                            in_=ac[:, j * D:(j + 1) * D]),
                        waits=[(ve, t_acc)], inc=st_s, inc_k=16)
                    ac_free[j] = store_toks[(e_, g)]
                w_toks["mf_last"] = pe.v
            w_toks[("w1f", e_)] = pe.v
            w_toks[("w2f", e_)] = pe.v
            w_toks[("b1f", e_)] = se.v
        ctx.op("sync", lambda e: None, waits=[(st_s, st_s.v)])


def emit_l3_ffn(ctx, cm, wap, x_src, x_dst):
    nc = ctx.nc
    hbuf = nc.dram_tensor("p7_hbuf", [NT, D], F32, kind="Internal").ap()
    abuf = nc.dram_tensor("p7_abuf", [NT, D], F32, kind="Internal").ap()
    emit_softmax_attention(ctx, cm, wap, dict(
        name="p7a", wq="l3f_wq", wk="l3f_wk", wv="l3f_wv", wo="l3f_wo",
        bias=("l3f_bq", "l3f_bk", "l3f_bv", "l3f_bo"), causal=True,
        affine=("l3f_g1", "l3f_b1"), save_h=hbuf, res_src="h"),
        x_src, abuf)
    st_a = ctx.sem("p7a_st")
    emit_l3_moe(ctx, cm, wap, abuf, x_src, x_dst, st_a, st_a.v)


def emit_l3_attn(ctx, cm, wap, x_src, x_dst):
    emit_softmax_attention(ctx, cm, wap, dict(
        name="p6", wq="l3a_wq", wk="l3a_wk", wv="l3a_wv", wo=None), x_src, x_dst)


def emit_l2_attn(ctx, cm, wap, x_src, x_dst):
    emit_softmax_attention(ctx, cm, wap, dict(
        name="p4", wq="l2a_wqT", wk="l2a_wkT", wv="l2a_wvT", wo="l2a_woT",
        rope=True), x_src, x_dst)


def emit_l1_attn(ctx, cm, wap, x_src, x_dst):
    emit_softmax_attention(ctx, cm, wap, dict(
        name="p2", wq="l1a_wq", wk="l1a_wk", wv="l1a_wv", wo="l1a_wo",
        bias=("l1a_bq", "l1a_bk", "l1a_bv", "l1a_bo"), toep="l1a_toep"),
        x_src, x_dst)



PHASE_EMITTERS = {}


def _register():
    PHASE_EMITTERS["l2_ffn"] = (emit_l2_ffn,
                                ["l2f_rwT", "l2f_rb", "l2f_ewT", "l2f_eb"])
    PHASE_EMITTERS["l0_attn"] = (emit_l0_attn,
                                 ["l0a_wqT", "l0a_wkT", "l0a_wvT", "l0a_woT",
                                  "l0a_qg_full", "l0a_qb_full", "l0a_kg_full",
                                  "l0a_kb_full"])
    PHASE_EMITTERS["l3_attn"] = (emit_l3_attn, ["l3a_wq", "l3a_wk", "l3a_wv"])
    PHASE_EMITTERS["l3_ffn"] = (emit_l3_ffn,
                                ["l3f_g1", "l3f_b1", "l3f_g2", "l3f_b2",
                                 "l3f_wq", "l3f_bq", "l3f_wk", "l3f_bk",
                                 "l3f_wv", "l3f_bv", "l3f_wo", "l3f_bo",
                                 "l3f_wr", "l3f_br", "l3f_ew1", "l3f_eb1T",
                                 "l3f_ew2", "l3f_eb2"])
    PHASE_EMITTERS["l2_attn"] = (emit_l2_attn,
                                 ["l2a_wqT", "l2a_wkT", "l2a_wvT", "l2a_woT"])
    PHASE_EMITTERS["l1_attn"] = (emit_l1_attn,
                                 ["l1a_wq", "l1a_wk", "l1a_wv", "l1a_wo",
                                  "l1a_bq", "l1a_bk", "l1a_bv", "l1a_bo", "l1a_toep"])
    PHASE_EMITTERS["l1_ffn"] = (emit_l1_ffn,
                                ["l1f_rw", "l1f_rb", "l1f_w1p", "l1f_b1p",
                                 "l1f_w2", "l1f_b2m", "l1f_sw1", "l1f_sb1",
                                 "l1f_sw2", "l1f_sb2"])
    PHASE_EMITTERS["l0_ffn"] = (emit_l0_ffn,
                                ["l0f_w12T", "l0f_w3T", "l0f_sw1T", "l0f_sw2T",
                                 "l0f_sw3T", "l0f_rwT", "l0f_eb"])


_register()

COMMON_NAMES = ["c_ident", "c_identf", "c_eps", "c_negbig", "c_ones1", "c_ones128",
                "c_triu", "c_causal", "c_rope_cos", "c_rope_sin"]

PHASE_ORDER = ["l0_attn", "l0_ffn", "l1_attn", "l1_ffn",
               "l2_attn", "l2_ffn", "l3_attn", "l3_ffn"]


def build_program(phases, Wshapes):
    nc = bass.Bass("TRN2", target_bir_lowering=False)
    es = contextlib.ExitStack()
    nc._es = es
    ctx = Ctx(nc, es)

    x_in = nc.dram_tensor("x_in", [NT, D], F32, kind="ExternalInput").ap()
    x_out = nc.dram_tensor("x_out", [NT, D], F32, kind="ExternalOutput").ap()
    if len(phases) > 1:
        xres = nc.dram_tensor("xres", [NT, D], F32, kind="Internal").ap()
    names = set(COMMON_NAMES)
    for ph in phases:
        names.update(PHASE_EMITTERS[ph][1])
    wap = {}
    for n in sorted(names):
        sh, dt = Wshapes[n]
        wap[n] = nc.dram_tensor(n, list(sh), dt, kind="ExternalInput").ap()

    cm = Common(ctx, es, wap)
    for k, ph in enumerate(phases):
        src = x_in if k == 0 else xres
        dst = x_out if k == len(phases) - 1 else xres
        PHASE_EMITTERS[ph][0](ctx, cm, wap, src, dst)
        if k != len(phases) - 1:
            ctx.barrier(f"ph{k}")

    with nc.Block() as block:
        for st in STREAMS:
            ops = ctx.streams[st]
            getattr(block, st)(lambda eng, ops=ops: [f(eng) for f in ops])
    es.close()
    return nc, sorted(names)


def _np_dt(dt):
    return F32 if dt == np.float32 else BF16


def run_phases(phase_list, x, W, fused=False):
    Wshapes = {k: (v.shape, F32 if v.dtype == np.float32 else BF16)
               for k, v in W.items()}
    x = np.ascontiguousarray(x, dtype=np.float32)
    groups = [phase_list] if fused else [[p] for p in phase_list]
    for grp in groups:
        nc, names = build_program(grp, Wshapes)
        in_maps = []
        for c in range(NCORES):
            m = {n: W[n] for n in names}
            m["x_in"] = x[2 * c:2 * c + 2].reshape(NT, D)
            in_maps.append(m)
        res = run_bass_kernel_spmd(nc, in_maps, core_ids=list(range(NCORES)))
        x = np.stack([r["x_out"].reshape(2, TS, D) for r in res.results])
        x = x.reshape(16, TS, D)
    return x


def kernel(**inputs):
    W = stage_weights(inputs)
    x = np.asarray(inputs["L0_attn_x"], dtype=np.float32)
    x = run_phases(PHASE_ORDER, x, W, fused=False)
    return x


# version 03 (final)
# speedup vs baseline: 1.0049x; 1.0049x over previous
import contextlib
import numpy as np
import ml_dtypes
import concourse.bass as bass
import concourse.mybir as mybir
from concourse.bass_utils import run_bass_kernel_spmd

F32 = mybir.dt.float32
BF16 = mybir.dt.bfloat16
AF = mybir.ActivationFunctionType
ALU = mybir.AluOpType
AX = mybir.AxisListType
BFNP = ml_dtypes.bfloat16

NCORES = 8
R = 2
TS = 2048
D = 1024
NT = R * TS
NTILES = NT // 128
H = 16
HD = 64
EPS = 1e-5



class Sem:
    __slots__ = ("h", "v", "name")

    def __init__(self, h, name):
        self.h = h
        self.v = 0
        self.name = name


STREAMS = ("tensor", "vector", "scalar", "gpsimd", "sync")


class Ctx:
    def __init__(self, nc, es):
        self.nc = nc
        self.es = es
        self.streams = {k: [] for k in STREAMS}
        self._sems = {}
        self._lastwait = {}
        self.serial = False
        self._last = {}

    def sem(self, name):
        s = self._sems.get(name)
        if s is None:
            h = self.es.enter_context(self.nc.semaphore(name))
            s = Sem(h, name)
            self._sems[name] = s
        return s

    def op(self, stream, fn, waits=(), inc=None, inc_k=1):
        if self.serial:
            waits = list(waits) + [sv for st2, sv in self._last.items()
                                   if st2 != stream]
        w = []
        for s, v in waits:
            key = (stream, s.name)
            if self._lastwait.get(key, -1) >= v:
                continue
            self._lastwait[key] = v
            w.append((s.h, v))
        tgt = None
        if inc is not None:
            inc.v += inc_k
            tgt = (inc.h, inc_k)
            if self.serial:
                self._last[stream] = (inc, inc.v)
        ret = inc.v if inc is not None else None

        def run(eng, w=w, fn=fn, tgt=tgt):
            for hh, vv in w:
                eng.wait_ge(hh, vv)
            r = fn(eng)
            if tgt is not None:
                if r is None:
                    r = eng.nop()
                r.then_inc(tgt[0], tgt[1])
        self.streams[stream].append(run)
        return ret

    def chain(self, stream, fns, waits=(), inc=None, inc_k=1):
        s = self.sem("chain_" + stream)
        base = s.v
        s.v += len(fns) - 1

        def run_all(e, fns=list(fns), base=base, h=s.h):
            r = None
            for i, fn in enumerate(fns):
                if i:
                    e.wait_ge(h, base + i)
                r = fn(e)
                if i < len(fns) - 1:
                    r.then_inc(h, 1)
            return r
        return self.op(stream, run_all, waits=waits, inc=inc, inc_k=inc_k)

    def barrier(self, tag):
        sems = [self.sem(f"bar_{st}") for st in STREAMS]
        vals = []
        for st, s in zip(STREAMS, sems):
            vals.append(self.op(st, lambda e: e.nop(), inc=s))
        for st in STREAMS:
            self.op(st, lambda e: None,
                    waits=[(s, v) for s, v in zip(sems, vals)])


class Ring:

    def __init__(self, ctx, name, nslots, prod_step=1, cons_step=1):
        self.ctx = ctx
        self.n = nslots
        self.filled = ctx.sem(name + "_f")
        self.freed = ctx.sem(name + "_e")
        self.prod_step = prod_step
        self.cons_step = cons_step
        self.fill_tok = []
        self.free_tok = []
        self.i_prod = 0
        self.i_cons = 0

    def prod_waits(self):
        i = self.i_prod
        if i >= self.n:
            return [(self.freed, self.free_tok[i - self.n])]
        return []

    def prod_done(self):
        self.fill_tok.append(self.filled.v)
        self.i_prod += 1

    def cons_waits(self):
        return [(self.filled, self.fill_tok[self.i_cons])]

    def cons_done(self):
        self.free_tok.append(self.freed.v)
        self.i_cons += 1


class CastLoader:

    def __init__(self, ctx, es, name, cols32, nh=2):
        self.ctx = ctx
        self.stage = es.enter_context(
            ctx.nc.sbuf_tensor(name + "_stg", [128, nh * cols32], F32))
        self.cols32 = cols32
        self.nh = nh
        self.fill = ctx.sem(name + "_lf")
        self.done = ctx.sem(name + "_lc")
        self.i = 0
        self.cast_toks = []

    def load(self, dst, src, cols, ring_waits=(), nparts=128):
        assert cols <= self.cols32
        sh = (self.i % self.nh) * self.cols32
        w = []
        if self.i >= self.nh:
            w.append((self.done, self.cast_toks[self.i - self.nh]))
        stg = self.stage[:nparts, sh:sh + cols]
        nd = len(src.shape)
        if nd > 2:
            names = [f"d{j}" for j in range(nd - 1)]
            spec = "p (" + " ".join(names) + ") -> p " + " ".join(names)
            kw = {names[j]: src.shape[1 + j] for j in range(nd - 2)}
            stg_in = stg.rearrange(spec, **kw)
        else:
            stg_in = stg
        t_l = self.ctx.op("sync", lambda e: e.dma_start(out=stg_in, in_=src),
                          waits=w, inc=self.fill, inc_k=16)
        t_c = self.ctx.op("scalar", lambda e: e.copy(dst, stg),
                          waits=[(self.fill, t_l)] + list(ring_waits),
                          inc=self.done)
        self.cast_toks.append(t_c)
        self.i += 1
        return t_c


def bcast_pp(ctx, ps_ap, ones128, src_ap, waits=(), inc=None):
    return ctx.op("tensor",
                  lambda e: e.matmul(ps_ap, ones128, src_ap, start=True, stop=True),
                  waits=waits, inc=inc)




def _T(a):
    return np.ascontiguousarray(np.swapaxes(np.asarray(a), -1, -2))


def _C(a):
    return np.ascontiguousarray(np.asarray(a))


def stage_weights(i):
    W = {}
    W["c_ident"] = np.eye(128, dtype=BFNP)
    W["c_identf"] = np.eye(128, dtype=np.float32)
    W["c_eps"] = np.full((128, 1), EPS, dtype=np.float32)
    W["c_negbig"] = np.full((128, 1), -1e30, dtype=np.float32)
    W["c_ones1"] = np.ones((1, 128), dtype=BFNP)
    W["c_ones128"] = np.ones((128, 1), dtype=BFNP)
    W["c_triu"] = np.triu(np.ones((128, 128), dtype=np.float32)).astype(BFNP)
    cm = np.zeros((4, 128, 512), dtype=np.float32)
    for oi, off in enumerate((0, 128, 256, 384)):
        cm[oi] = (np.arange(128)[:, None] + off <= np.arange(512)[None, :])
    W["c_causal"] = cm.astype(BFNP)
    import math
    pos = np.arange(TS, dtype=np.float64)
    freqs = np.exp(np.arange(0, HD, 2, dtype=np.float64) * (-math.log(10000.0) / HD))
    ang = pos[:, None] * freqs[None, :]
    cosm = np.cos(ang).astype(np.float32)
    sinm = np.sin(ang).astype(np.float32)
    W["c_rope_cos"] = np.tile(cosm, (1, H))
    W["c_rope_sin"] = np.tile(-sinm, (1, H))

    W["l0a_wqT"] = _T(i["L0_attn_Wq"])
    W["l0a_wkT"] = _T(i["L0_attn_Wk"])
    W["l0a_wvT"] = _T(i["L0_attn_Wv"])
    W["l0a_woT"] = _T(i["L0_attn_Wo"])
    W["l0a_qg_full"] = np.ascontiguousarray(np.tile(np.asarray(i["L0_attn_q_gamma"]), H)).reshape(1, D)
    W["l0a_qb_full"] = np.ascontiguousarray(np.tile(np.asarray(i["L0_attn_q_beta"]), H)).reshape(1, D)
    W["l0a_kg_full"] = np.ascontiguousarray(np.tile(np.asarray(i["L0_attn_k_gamma"]), H)).reshape(1, D)
    W["l0a_kb_full"] = np.ascontiguousarray(np.tile(np.asarray(i["L0_attn_k_beta"]), H)).reshape(1, D)
    W["l0f_w12T"] = _T(i["L0_ffn_w12"])
    W["l0f_w3T"] = _T(i["L0_ffn_w3"])
    W["l0f_sw1T"] = _T(i["L0_ffn_shared_w1"])
    W["l0f_sw2T"] = _T(i["L0_ffn_shared_w2"])
    W["l0f_sw3T"] = _T(i["L0_ffn_shared_w3"])
    W["l0f_rwT"] = _T(i["L0_ffn_router_w"])
    W["l0f_eb"] = _C(i["L0_ffn_expert_bias"]).reshape(1, 8)
    W["l1a_wq"] = _C(i["L1_attn_Wq"])
    W["l1a_wk"] = _C(i["L1_attn_Wk"])
    W["l1a_wv"] = _C(i["L1_attn_Wv"])
    W["l1a_wo"] = _C(i["L1_attn_Wo"])
    W["l1a_bq"] = _C(i["L1_attn_bq"]).reshape(1, D)
    W["l1a_bk"] = _C(i["L1_attn_bk"]).reshape(1, D)
    W["l1a_bv"] = _C(i["L1_attn_bv"]).reshape(1, D)
    W["l1a_bo"] = _C(i["L1_attn_bo"]).reshape(1, D)
    rel = np.asarray(i["L1_attn_rel_table"])
    ext = rel[np.minimum(np.arange(4095), rel.shape[0] - 1)]
    idx = np.clip(np.arange(4096)[None, :] + 127 - np.arange(128)[:, None], 0, 4094)
    W["l1a_toep"] = np.ascontiguousarray(ext[idx].transpose(2, 0, 1))
    W["l1f_rw"] = _C(i["L1_ffn_router_w"])
    W["l1f_rb"] = _C(i["L1_ffn_router_b"]).reshape(1, 64)
    w1 = np.asarray(i["L1_ffn_w1"])
    W["l1f_w1p"] = np.ascontiguousarray(
        w1.reshape(32, 2, 1024, 256).transpose(0, 2, 1, 3).reshape(32, 1024, 512))
    b1 = np.asarray(i["L1_ffn_b1"])
    W["l1f_b1p"] = np.ascontiguousarray(b1.reshape(32, 1, 512))
    W["l1f_w2"] = _C(i["L1_ffn_w2"])
    W["l1f_b2m"] = _C(i["L1_ffn_b2"])
    W["l1f_sw1"] = _C(i["L1_ffn_sw1"])
    W["l1f_sb1"] = _C(i["L1_ffn_sb1"]).reshape(2, 1, 2048)
    W["l1f_sw2"] = _C(i["L1_ffn_sw2"])
    W["l1f_sb2"] = _C(i["L1_ffn_sb2"]).reshape(2, 1, 1024)
    W["l2a_wqT"] = _T(i["L2_attn_Wq"])
    W["l2a_wkT"] = _T(i["L2_attn_Wk"])
    W["l2a_wvT"] = _T(i["L2_attn_Wv"])
    W["l2a_woT"] = _T(i["L2_attn_Wo"])
    W["l2f_rwT"] = _T(i["L2_ffn_router_w"])
    W["l2f_rb"] = _C(i["L2_ffn_router_b"]).reshape(2, 1, 8)
    W["l2f_ewT"] = _T(i["L2_ffn_expert_w"])
    W["l2f_eb"] = _C(i["L2_ffn_expert_b"])
    W["l3a_wq"] = _C(i["L3_attn_Wq"])
    W["l3a_wk"] = _C(i["L3_attn_Wk"])
    W["l3a_wv"] = _C(i["L3_attn_Wv"])
    W["l3f_g1"] = _C(i["L3_ffn_ln1_g"]).reshape(1, D)
    W["l3f_b1"] = _C(i["L3_ffn_ln1_b"]).reshape(1, D)
    W["l3f_g2"] = _C(i["L3_ffn_ln2_g"]).reshape(1, D)
    W["l3f_b2"] = _C(i["L3_ffn_ln2_b"]).reshape(1, D)
    W["l3f_wq"] = _C(i["L3_ffn_Wq"])
    W["l3f_wk"] = _C(i["L3_ffn_Wk"])
    W["l3f_wv"] = _C(i["L3_ffn_Wv"])
    W["l3f_wo"] = _C(i["L3_ffn_Wo"])
    W["l3f_bq"] = _C(i["L3_ffn_bq"]).reshape(1, D)
    W["l3f_bk"] = _C(i["L3_ffn_bk"]).reshape(1, D)
    W["l3f_bv"] = _C(i["L3_ffn_bv"]).reshape(1, D)
    W["l3f_bo"] = _C(i["L3_ffn_bo"]).reshape(1, D)
    W["l3f_wr"] = _C(i["L3_ffn_Wr"])
    W["l3f_br"] = _C(i["L3_ffn_br"]).reshape(1, 8)
    W["l3f_ew1"] = _C(i["L3_ffn_e_w1"])
    eb1 = np.asarray(i["L3_ffn_e_b1"])
    W["l3f_eb1T"] = np.ascontiguousarray(eb1.reshape(8, 32, 128).transpose(0, 2, 1))
    W["l3f_ew2"] = _C(i["L3_ffn_e_w2"])
    W["l3f_eb2"] = _C(i["L3_ffn_e_b2"])
    return W



class Common:

    def __init__(self, ctx, es, wap):
        nc = ctx.nc
        self.ident = es.enter_context(nc.sbuf_tensor("cc_ident", [128, 128], BF16))
        self.ones1 = es.enter_context(nc.sbuf_tensor("cc_ones1", [1, 128], BF16))
        self.ones128 = es.enter_context(nc.sbuf_tensor("cc_ones128", [128, 1], BF16))
        self.junk = es.enter_context(nc.sbuf_tensor("cc_junk", [128, 1024], BF16))
        self.identf = es.enter_context(nc.sbuf_tensor("cc_identf", [128, 128], F32))
        self.causal = es.enter_context(nc.sbuf_tensor("cc_causal", [128, 4 * 512], BF16))
        self.eps = es.enter_context(nc.sbuf_tensor("cc_eps", [128, 1], F32))
        self.negbig = es.enter_context(nc.sbuf_tensor("cc_negbig", [128, 1], F32))
        self.ldc = ctx.sem("ldc")
        for dst, src in ((self.ident, wap["c_ident"]), (self.ones1, wap["c_ones1"]),
                         (self.ones128, wap["c_ones128"]), (self.eps, wap["c_eps"]),
                         (self.negbig, wap["c_negbig"]), (self.identf, wap["c_identf"])):
            ctx.op("sync", lambda e, d=dst, s=src: e.dma_start(out=d[:], in_=s),
                   inc=self.ldc, inc_k=16)
        ctx.op("sync", lambda e: e.dma_start(
            out=self.causal[:].rearrange("p (a n) -> p a n", a=4),
            in_=wap["c_causal"].rearrange("a p n -> p a n")),
            inc=self.ldc, inc_k=16)
        self.ld_tok = self.ldc.v


def emit_ln_tile(ctx, cm, x_ap, h_dst, sv, dve_waits, sc_sem, dve_sem,
                 junk=None):
    s_col = sv[:, 0:1]
    mean_col = sv[:, 1:2]
    nmean_col = sv[:, 2:3]
    ssq_col = sv[:, 3:4]
    rstd_col = sv[:, 4:5]
    dcols = x_ap.shape[-1]
    junk = junk if junk is not None else cm.junk

    t1 = ctx.chain("vector", [
        lambda e: e.tensor_reduce(s_col, x_ap, AX.X, ALU.add),
        lambda e: e.tensor_scalar_mul(mean_col, s_col, 1.0 / dcols),
        lambda e: e.tensor_scalar_mul(nmean_col, s_col, -1.0 / dcols),
    ], waits=dve_waits, inc=dve_sem)

    t2 = ctx.chain("scalar", [
        lambda e: e.activation(junk[:, 0:dcols], x_ap, AF.Square, bias=nmean_col,
                               accum_out=ssq_col),
        lambda e: e.activation(ssq_col, ssq_col, AF.Ln, bias=cm.eps[:, 0:1],
                               scale=1.0 / dcols),
        lambda e: e.activation(rstd_col, ssq_col, AF.Exp, scale=-0.5),
    ], waits=[(dve_sem, t1)], inc=sc_sem)

    def dve2(e):
        return e.tensor_scalar(h_dst, x_ap, mean_col, rstd_col,
                               ALU.subtract, ALU.mult)
    t3 = ctx.op("vector", dve2, waits=[(sc_sem, t2)], inc=dve_sem)
    return t2, t3


def emit_transpose_h(ctx, cm, h_src, ps_tr, hT_dst, nchunk, pe_waits, pe_sem,
                     dve_sem, evac_stream="vector"):
    def pe(e):
        r = None
        for c in range(nchunk):
            r = e.transpose(ps_tr[:, c * 128:(c + 1) * 128],
                            h_src[:, c * 128:(c + 1) * 128], cm.ident[:])
        return r
    t = ctx.op("tensor", pe, waits=pe_waits, inc=pe_sem)
    tok = ctx.op(evac_stream, lambda e: e.tensor_copy(hT_dst, ps_tr[:, 0:nchunk * 128]),
                 waits=[(pe_sem, t)], inc=dve_sem)
    return tok


def emit_gemm_acc(ctx, psum_ap, hT_slices, w_slices, extra=(), waits=()):
    def pe(e):
        r = None
        n = len(hT_slices)
        for c in range(n):
            r = e.matmul(psum_ap, hT_slices[c], w_slices[c],
                         start=(c == 0), stop=(c == n - 1 and not extra))
        for j, (lh, rh) in enumerate(extra):
            r = e.matmul(psum_ap, lh, rh, start=False, stop=(j == len(extra) - 1))
        return r
    return pe, waits



def emit_l2_ffn(ctx, cm, wap, x_src, x_dst):
    nc = ctx.nc
    STT = 8
    NST = NTILES // STT

    with contextlib.ExitStack() as es:
        x_st = es.enter_context(nc.sbuf_tensor("p5_x", [128, STT * D], F32))
        acc0 = es.enter_context(nc.sbuf_tensor("p5_acc0", [128, STT * D], F32))
        hT = es.enter_context(nc.sbuf_tensor("p5_hT", [128, STT * D], BF16))
        hb = es.enter_context(nc.sbuf_tensor("p5_hb", [128, D], BF16))
        lg = es.enter_context(nc.sbuf_tensor("p5_lg", [128, STT * 8], F32))
        msk = es.enter_context(nc.sbuf_tensor("p5_msk", [128, STT * 8], F32))
        mskb = es.enter_context(nc.sbuf_tensor("p5_mskb", [128, 8], BF16))
        mskT = es.enter_context(nc.sbuf_tensor("p5_mskT", [8, STT * 128], BF16))
        sv = es.enter_context(nc.sbuf_tensor("p5_sv", [128, STT * 8], F32))
        tmp8 = es.enter_context(nc.sbuf_tensor("p5_tmp8", [128, 16], F32))
        rw_sb = es.enter_context(nc.sbuf_tensor("p5_rw", [128, 2 * 64], BF16))
        rb_sb = es.enter_context(nc.sbuf_tensor("p5_rb", [1, 16], BF16))
        eb_sb = es.enter_context(nc.sbuf_tensor("p5_eb", [8, 2 * D], BF16))
        wslot = es.enter_context(nc.sbuf_tensor("p5_w", [128, 3 * 8 * D], BF16))
        ps_tr = es.enter_context(nc.psum_tensor("p5_pstr", [128, 1024], BF16))
        ps_mt = es.enter_context(nc.psum_tensor("p5_psmt", [128, 512], BF16))
        ps_r = es.enter_context(nc.psum_tensor("p5_psr", [128, 8], F32))
        ps_o = [es.enter_context(nc.psum_tensor(f"p5_pso{i}", [128, D], F32))
                for i in range(2)]

        ld = ctx.sem("p5_ld")
        pe = ctx.sem("p5_pe")
        ve = ctx.sem("p5_ve")
        se = ctx.sem("p5_se")
        st_s = ctx.sem("p5_st")
        loader = CastLoader(ctx, es, "p5", 4096, 2)
        gld = loader.done
        loader.load(rw_sb[:], wap["l2f_rwT"].rearrange("l (c p) n -> p l c n", p=128), 128)
        loader.load(rb_sb[:], wap["l2f_rb"].rearrange("l o n -> o l n"), 16, nparts=1)
        tok_static = loader.load(
            eb_sb[:], wap["l2f_eb"].rearrange("l p n -> p l n"), 2048, nparts=8)

        ewT = wap["l2f_ewT"]
        efill = []
        efree = []
        NSLOT = 3

        def emit_expert_load(l, e_):
            i = len(efill)
            slot = i % NSLOT
            pw = [efree[i - NSLOT]] if i >= NSLOT else []
            t = None
            for half in range(2):
                src = ewT[l, e_, half * 512:(half + 1) * 512, :]
                src = src.rearrange("(c p) n -> p c n", p=128)
                t = loader.load(wslot[:, slot * 8 * D + half * 4 * D:
                                      slot * 8 * D + (half + 1) * 4 * D],
                                src, 4096,
                                ring_waits=pw if half == 0 else ())
            efill.append(t)

        nload = 0
        total_loads = NST * 2 * 8
        for _ in range(3):
            emit_expert_load(nload // 8 % 2, nload % 8)
            nload += 1

        def router_and_mask(l, ti, hT_base, lgs, msks, pe_src_tok):
            rw_off = l * 64
            def mm(e):
                r = None
                for c in range(8):
                    r = e.matmul(ps_r[:], hT[:, hT_base + c * 128: hT_base + (c + 1) * 128],
                                 rw_sb[:, rw_off + c * 8: rw_off + (c + 1) * 8],
                                 start=(c == 0), stop=False)
                return e.matmul(ps_r[:], cm.ones1[:], rb_sb[:, l * 8:(l + 1) * 8],
                                start=False, stop=True)
            t_mm = ctx.op("tensor", mm,
                          waits=[(ve, pe_src_tok), (gld, tok_static)],
                          inc=pe)
            lgc = lg[:, lgs:lgs + 8]
            mskc = msk[:, msks:msks + 8]
            m1 = tmp8[:, 0:1]
            m2 = tmp8[:, 1:2]
            eqm = tmp8[:, 8:16]

            t_d = ctx.chain("vector", [
                lambda e: e.tensor_copy(lgc, ps_r[:]),
                lambda e: e.tensor_reduce(m1, lgc, AX.X, ALU.max),
                lambda e: e.tensor_scalar(eqm, lgc, m1, None, ALU.is_ge),
                lambda e: e.scalar_tensor_tensor(eqm, eqm, cm.negbig[:], lgc,
                                                 ALU.mult, ALU.add),
                lambda e: e.tensor_reduce(m2, eqm, AX.X, ALU.max),
                lambda e: e.tensor_scalar(mskc, lgc, m2, None, ALU.is_ge),
                lambda e: e.tensor_copy(mskb[:], mskc),
            ], waits=[(pe, t_mm), (cm.ldc, cm.ld_tok)], inc=ve)
            t_p = ctx.op("tensor",
                         lambda e: e.transpose(ps_mt[0:8, 0:128], mskb[:], cm.ident[:]),
                         waits=[(ve, t_d)], inc=pe)
            t_e = ctx.op("vector",
                         lambda e: e.tensor_copy(mskT[:8, ti * 128:(ti + 1) * 128], ps_mt[:8, 0:128]),
                         waits=[(pe, t_p)], inc=ve)
            return t_e

        ps_free_tok = [0, 0]
        tr_free_tok = 0
        store_toks = {}

        import os
        _dbg_bar = os.environ.get("K_DEBUG_BARRIER", "0") == "1"
        for s in range(NST):
            if _dbg_bar and s:
                ctx.barrier(f"dbg{s}")
            xw = []
            for ti in range(STT):
                g = NTILES // NST * s + ti
                tok = ctx.op("sync", lambda e, ti=ti, g=g: e.dma_start(
                    out=x_st[:, ti * D:(ti + 1) * D], in_=x_src[g * 128:(g + 1) * 128, :]),
                    waits=[(st_s, store_toks[(s - 1, ti)])] if s else [],
                    inc=ld, inc_k=16)
                xw.append(tok)
            mskA_toks = []
            mskB_toks = []
            for ti in range(STT):
                sc_t, h_t = emit_ln_tile(
                    ctx, cm, x_st[:, ti * D:(ti + 1) * D], hb[:],
                    sv[:, ti * 8: ti * 8 + 8], [(ld, xw[ti])], se, ve)
                tr_t = emit_transpose_h(
                    ctx, cm, hb[:], ps_tr, hT[:, ti * D:(ti + 1) * D], 8,
                    [(ve, h_t), (ve, tr_free_tok), (ldc := cm.ldc, cm.ld_tok)], pe, ve)
                tr_free_tok = tr_t
                mskA_toks.append(router_and_mask(0, ti, ti * D, ti * 8, ti * 8, tr_t))
            hT_ready = tr_free_tok
            for e_ in range(8):
                icons = len(efree)
                slot = icons % NSLOT
                cw = [(gld, efill[icons])]
                for ti in range(STT):
                    ps = ps_o[ti % 2]
                    def mm(e, ti=ti, slot=slot, ps=ps):
                        r = None
                        for n in range(2):
                            for c in range(8):
                                r = e.matmul(ps[:, n * 512:(n + 1) * 512],
                                             hT[:, ti * D + c * 128: ti * D + (c + 1) * 128],
                                             wslot[:, slot * 8 * D + c * D + n * 512:
                                                   slot * 8 * D + c * D + (n + 1) * 512],
                                             start=(c == 0), stop=(c == 7))
                        return r
                    t_mm = ctx.op("tensor", mm,
                                  waits=list(cw) + [(ve, mskA_toks[ti]), (ve, ps_free_tok[ti % 2])],
                                  inc=pe)
                    mcol = msk[:, ti * 8 + e_: ti * 8 + e_ + 1]
                    if e_ == 0:
                        fn_d = (lambda e, ps=ps, ti=ti, mcol=mcol:
                                e.tensor_scalar(acc0[:, ti * D:(ti + 1) * D], ps[:], mcol, None, ALU.mult))
                    else:
                        fn_d = (lambda e, ps=ps, ti=ti, mcol=mcol:
                                e.scalar_tensor_tensor(acc0[:, ti * D:(ti + 1) * D], ps[:], mcol,
                                                       acc0[:, ti * D:(ti + 1) * D], ALU.mult, ALU.add))
                    t_d = ctx.op("vector", fn_d, waits=[(pe, t_mm)], inc=ve)
                    ps_free_tok[ti % 2] = t_d
                efree.append((ve, t_d))
                if nload < total_loads:
                    emit_expert_load(nload // 8 % 2, nload % 8)
                    nload += 1
            for ti in range(STT):
                ps = ps_o[ti % 2]
                def mmb(e, ti=ti, ps=ps):
                    r = None
                    for n in range(2):
                        r = e.matmul(ps[:, n * 512:(n + 1) * 512],
                                     mskT[:8, ti * 128:(ti + 1) * 128],
                                     eb_sb[:8, n * 512:(n + 1) * 512],
                                     start=True, stop=True)
                    return r
                t_mm = ctx.op("tensor", mmb,
                              waits=[(ve, mskA_toks[ti]), (ve, ps_free_tok[ti % 2]), (gld, tok_static)],
                              inc=pe)
                t_d = ctx.op("vector",
                             lambda e, ti=ti, ps=ps: e.tensor_add(
                                 acc0[:, ti * D:(ti + 1) * D], acc0[:, ti * D:(ti + 1) * D], ps[:]),
                             waits=[(pe, t_mm)], inc=ve)
                ps_free_tok[ti % 2] = t_d
                t_c = ctx.op("vector",
                             lambda e, ti=ti: e.tensor_copy(hb[:], acc0[:, ti * D:(ti + 1) * D]),
                             waits=[(ve, t_d)], inc=ve)
                tr_t = emit_transpose_h(
                    ctx, cm, hb[:], ps_tr, hT[:, ti * D:(ti + 1) * D], 8,
                    [(ve, t_c), (ve, tr_free_tok)], pe, ve)
                tr_free_tok = tr_t
                mskB_toks.append(router_and_mask(1, ti, ti * D, ti * 8, ti * 8, tr_t))
            for e_ in range(8):
                icons = len(efree)
                slot = icons % NSLOT
                cw = [(gld, efill[icons])]
                for ti in range(STT):
                    ps = ps_o[ti % 2]
                    def mm(e, ti=ti, slot=slot, ps=ps):
                        r = None
                        for n in range(2):
                            for c in range(8):
                                r = e.matmul(ps[:, n * 512:(n + 1) * 512],
                                             hT[:, ti * D + c * 128: ti * D + (c + 1) * 128],
                                             wslot[:, slot * 8 * D + c * D + n * 512:
                                                   slot * 8 * D + c * D + (n + 1) * 512],
                                             start=(c == 0), stop=(c == 7))
                        return r
                    t_mm = ctx.op("tensor", mm,
                                  waits=list(cw) + [(ve, mskB_toks[ti]), (ve, ps_free_tok[ti % 2])],
                                  inc=pe)
                    mcol = msk[:, ti * 8 + e_: ti * 8 + e_ + 1]
                    t_d = ctx.op("vector",
                                 lambda e, ps=ps, ti=ti, mcol=mcol:
                                 e.scalar_tensor_tensor(x_st[:, ti * D:(ti + 1) * D], ps[:], mcol,
                                                        x_st[:, ti * D:(ti + 1) * D], ALU.mult, ALU.add),
                                 waits=[(pe, t_mm)], inc=ve)
                    ps_free_tok[ti % 2] = t_d
                efree.append((ve, t_d))
                if nload < total_loads:
                    emit_expert_load(nload // 8 % 2, nload % 8)
                    nload += 1
            for ti in range(STT):
                g = NTILES // NST * s + ti
                ps = ps_o[ti % 2]
                def mmb(e, ti=ti, ps=ps):
                    r = None
                    for n in range(2):
                        r = e.matmul(ps[:, n * 512:(n + 1) * 512],
                                     mskT[:8, ti * 128:(ti + 1) * 128],
                                     eb_sb[:8, D + n * 512: D + (n + 1) * 512],
                                     start=True, stop=True)
                    return r
                t_mm = ctx.op("tensor", mmb,
                              waits=[(ve, mskB_toks[ti]), (ve, ps_free_tok[ti % 2])],
                              inc=pe)
                t_d = ctx.op("vector",
                             lambda e, ti=ti, ps=ps: e.tensor_add(
                                 x_st[:, ti * D:(ti + 1) * D], x_st[:, ti * D:(ti + 1) * D], ps[:]),
                             waits=[(pe, t_mm)], inc=ve)
                ps_free_tok[ti % 2] = t_d
                store_toks[(s, ti)] = ctx.op(
                    "sync", lambda e, ti=ti, g=g: e.dma_start(
                        out=x_dst[g * 128:(g + 1) * 128, :], in_=x_st[:, ti * D:(ti + 1) * D]),
                    waits=[(ve, t_d)], inc=st_s, inc_k=16)
        ctx.op("sync", lambda e: None, waits=[(st_s, st_s.v)])



def emit_l0_ffn(ctx, cm, wap, x_src, x_dst):
    nc = ctx.nc
    STT, NST = 8, 4
    UC = 8 * D
    NU = 6

    with contextlib.ExitStack() as es:
        x_st = es.enter_context(nc.sbuf_tensor("p1_x", [128, STT * D], F32))
        hT = es.enter_context(nc.sbuf_tensor("p1_hT", [128, STT * D], BF16))
        hb = es.enter_context(nc.sbuf_tensor("p1_hb", [128, D], BF16))
        s1b = es.enter_context(nc.sbuf_tensor("p1_s1", [128, D], BF16))
        gb = es.enter_context(nc.sbuf_tensor("p1_g", [128, D], BF16))
        gT_sb = es.enter_context(nc.sbuf_tensor("p1_gT", [128, D], BF16))
        sc_sb = es.enter_context(nc.sbuf_tensor("p1_sc", [128, STT * 8], F32))
        gates = es.enter_context(nc.sbuf_tensor("p1_gt", [128, STT * 8], F32))
        s8 = es.enter_context(nc.sbuf_tensor("p1_s8", [128, 8], F32))
        tmp8 = es.enter_context(nc.sbuf_tensor("p1_t8", [128, 16], F32))
        ebB = es.enter_context(nc.sbuf_tensor("p1_ebB", [128, 8], F32))
        sv = es.enter_context(nc.sbuf_tensor("p1_sv", [128, STT * 8], F32))
        rw_sb = es.enter_context(nc.sbuf_tensor("p1_rw", [128, 64], BF16))
        row32 = es.enter_context(nc.sbuf_tensor("p1_r32", [1, 16], F32))
        rowbf = es.enter_context(nc.sbuf_tensor("p1_rbf", [1, 16], BF16))
        wslot = es.enter_context(nc.sbuf_tensor("p1_w", [128, NU * UC], BF16))
        psA = es.enter_context(nc.psum_tensor("p1_psA", [128, D], F32))
        psB = es.enter_context(nc.psum_tensor("p1_psB", [128, D], F32))
        psC = es.enter_context(nc.psum_tensor("p1_psC", [128, D], F32))
        ps_tr = es.enter_context(nc.psum_tensor("p1_pstr", [128, 1024], BF16))
        ps_r = es.enter_context(nc.psum_tensor("p1_psr", [128, 8], F32))

        import os as _os0
        _dbg = _os0.environ.get("K_P1_DBG", "0") == "1"
        if _dbg:
            dbg_g = nc.dram_tensor("dbg_gates", [4, 128, 64], F32, kind="ExternalOutput").ap()
            dbg_s = nc.dram_tensor("dbg_sc", [4, 128, 64], F32, kind="ExternalOutput").ap()
        loader = CastLoader(ctx, es, "p1", 4096, 2)
        pe = ctx.sem("p1_pe")
        ve = ctx.sem("p1_ve")
        se = ctx.sem("p1_se")
        ld = ctx.sem("p1_ld")
        st_s = ctx.sem("p1_st")

        rw_tok = loader.load(rw_sb[:], wap["l0f_rwT"].rearrange("(c p) n -> p c n", p=128), 64)
        t_r = ctx.op("sync", lambda e: e.dma_start(out=row32[:1, 0:8], in_=wap["l0f_eb"]),
                     inc=ld, inc_k=16)
        t_rc = ctx.op("scalar", lambda e: e.copy(rowbf[:1, 0:8], row32[:1, 0:8]),
                      waits=[(ld, t_r)], inc=se)
        t_bb = ctx.op("tensor", lambda e: e.matmul(ps_r[:], cm.ones1[:], rowbf[:1, 0:8],
                                                   start=True, stop=True),
                      waits=[(se, t_rc), (cm.ldc, cm.ld_tok)], inc=pe)
        ebB_tok = ctx.op("vector", lambda e: e.tensor_copy(ebB[:], ps_r[:]),
                         waits=[(pe, t_bb)], inc=ve)

        unit_fill = []
        unit_free = []

        def load_unit(srcs):
            i = len(unit_fill)
            rw_ = [unit_free[i - NU]] if i >= NU else []
            col0 = (i % NU) * UC
            off = 0
            t = None
            for j, (src, cols) in enumerate(srcs):
                t = loader.load(wslot[:, col0 + off:col0 + off + cols], src, cols,
                                ring_waits=rw_ if j == 0 else ())
                off += cols
            unit_fill.append(t)
            return i

        def unit_ap(i, c, n0, n1):
            col0 = (i % NU) * UC
            return wslot[:, col0 + c * 1024 + n0: col0 + c * 1024 + n1]

        def half_srcs(w2d):
            return [(w2d[h * 512:(h + 1) * 512, :].rearrange("(c p) n -> p c n", p=128), 4096)
                    for h in range(2)]

        import os
        _no_sh = os.environ.get("K_P1_NO_SHARED", "0") == "1"
        _no_ex = os.environ.get("K_P1_NO_EXPERTS", "0") == "1"

        def gen_loads():
            for s in range(NST):
                if not _no_sh:
                    for hc in range(4):
                        yield half_srcs(wap["l0f_sw1T"][:, hc * 1024:(hc + 1) * 1024])
                        yield half_srcs(wap["l0f_sw2T"][:, hc * 1024:(hc + 1) * 1024])
                        yield half_srcs(wap["l0f_sw3T"][hc * 1024:(hc + 1) * 1024, :])
                if not _no_ex:
                    for e_ in range(8):
                        yield half_srcs(wap["l0f_w12T"][e_][:, 0:1024])
                        yield half_srcs(wap["l0f_w12T"][e_][:, 1024:2048])
                        yield half_srcs(wap["l0f_w3T"][e_])
        loads = gen_loads()
        for _ in range(NU):
            load_unit(next(loads))
        iu = 0

        store_toks = {}
        psA_free = 0
        psB_free = 0
        psC_free = 0
        tr_free = 0
        hb_free = 0
        s1_free = 0
        psr_free = 0

        for s in range(NST):
            gate_toks = []
            for ti in range(STT):
                xt = ctx.op("sync", lambda e, ti=ti, g=NTILES // NST * s + ti: e.dma_start(
                    out=x_st[:, ti * D:(ti + 1) * D], in_=x_src[g * 128:(g + 1) * 128, :]),
                    waits=[(st_s, store_toks[(s - 1, ti)])] if s else [],
                    inc=ld, inc_k=16)
                sc_t, h_t = emit_ln_tile(ctx, cm, x_st[:, ti * D:(ti + 1) * D], hb[:],
                                         sv[:, ti * 8:ti * 8 + 8], [(ld, xt)], se, ve)
                tr_t = emit_transpose_h(ctx, cm, hb[:], ps_tr,
                                        hT[:, ti * D:(ti + 1) * D], 8,
                                        [(ve, h_t), (ve, tr_free)], pe, ve)
                tr_free = tr_t
                def mmr(e, ti=ti):
                    r = None
                    for c in range(8):
                        r = e.matmul(ps_r[:], hT[:, ti * D + c * 128:ti * D + (c + 1) * 128],
                                     rw_sb[:, c * 8:(c + 1) * 8],
                                     start=(c == 0), stop=(c == 7))
                    return r
                t_mm = ctx.op("tensor", mmr,
                              waits=[(ve, tr_t), (loader.done, rw_tok), (ve, ebB_tok),
                                     (se, psr_free)],
                              inc=pe)
                t_sg = ctx.op("scalar", lambda e, ti=ti: e.activation(
                    sc_sb[:, ti * 8:(ti + 1) * 8], ps_r[:], AF.Sigmoid),
                    waits=[(pe, t_mm)], inc=se)
                psr_free = t_sg

                scc = sc_sb[:, ti * 8:(ti + 1) * 8]
                gts_ = gates[:, ti * 8:(ti + 1) * 8]
                gate_toks.append(ctx.chain("vector", [
                    lambda e, scc=scc: e.tensor_add(s8[:], scc, ebB[:]),
                    lambda e: e.tensor_reduce(tmp8[:, 0:1], s8[:], AX.X, ALU.max),
                    lambda e: e.tensor_scalar(tmp8[:, 8:16], s8[:], tmp8[:, 0:1],
                                              None, ALU.is_ge),
                    lambda e: e.scalar_tensor_tensor(tmp8[:, 8:16], tmp8[:, 8:16],
                                                     cm.negbig[:], s8[:], ALU.mult, ALU.add),
                    lambda e: e.tensor_reduce(tmp8[:, 1:2], tmp8[:, 8:16], AX.X, ALU.max),
                    lambda e: e.tensor_scalar(tmp8[:, 8:16], s8[:], tmp8[:, 1:2],
                                              None, ALU.is_ge),
                    lambda e, scc=scc, gts_=gts_: e.tensor_mul(gts_, scc, tmp8[:, 8:16]),
                ], waits=[(se, t_sg), (cm.ldc, cm.ld_tok)], inc=ve))

            if _dbg:
                ctx.op("sync", lambda e, s=s: e.dma_start(out=dbg_g[s], in_=gates[:]),
                       waits=[(ve, gate_toks[-1])], inc=st_s, inc_k=16)
                ctx.op("sync", lambda e, s=s: e.dma_start(out=dbg_s[s], in_=sc_sb[:]),
                       waits=[(ve, gate_toks[-1])], inc=st_s, inc_k=16)
            for hc in ([] if _no_sh else range(4)):
                u1, u2, u3 = iu, iu + 1, iu + 2
                for ti in range(STT):
                    def mm1(e, ti=ti, u1=u1):
                        r = None
                        for n in range(2):
                            for c in range(8):
                                r = e.matmul(psA[:, n * 512:(n + 1) * 512],
                                             hT[:, ti * D + c * 128:ti * D + (c + 1) * 128],
                                             unit_ap(u1, c, n * 512, (n + 1) * 512),
                                             start=(c == 0), stop=(c == 7))
                        return r
                    tA = ctx.op("tensor", mm1,
                                waits=[(loader.done, unit_fill[u1]), (ve, gate_toks[ti]),
                                       (se, psA_free), (ve, psC_free)],
                                inc=pe)
                    t_s1 = ctx.op("scalar", lambda e: e.activation(s1b[:], psA[:], AF.Silu),
                                  waits=[(pe, tA), (ve, s1_free)], inc=se)
                    psA_free = t_s1
                    def mm2(e, ti=ti, u2=u2):
                        r = None
                        for n in range(2):
                            for c in range(8):
                                r = e.matmul(psB[:, n * 512:(n + 1) * 512],
                                             hT[:, ti * D + c * 128:ti * D + (c + 1) * 128],
                                             unit_ap(u2, c, n * 512, (n + 1) * 512),
                                             start=(c == 0), stop=(c == 7))
                        return r
                    tB = ctx.op("tensor", mm2,
                                waits=[(loader.done, unit_fill[u2]), (ve, psB_free)],
                                inc=pe)
                    t_g = ctx.op("vector", lambda e: e.tensor_mul(gb[:], s1b[:], psB[:]),
                                 waits=[(pe, tB), (se, t_s1), (ve, hb_free)], inc=ve)
                    psB_free = t_g
                    s1_free = t_g
                    tr_t = emit_transpose_h(ctx, cm, gb[:], ps_tr, gT_sb[:], 8,
                                            [(ve, t_g), (ve, tr_free)], pe, ve)
                    tr_free = tr_t
                    hb_free = tr_t
                    def mm3(e, u3=u3):
                        r = None
                        for n in range(2):
                            for c in range(8):
                                r = e.matmul(psC[:, n * 512:(n + 1) * 512],
                                             gT_sb[:, c * 128:(c + 1) * 128],
                                             unit_ap(u3, c, n * 512, (n + 1) * 512),
                                             start=(c == 0), stop=(c == 7))
                        return r
                    tC = ctx.op("tensor", mm3,
                                waits=[(loader.done, unit_fill[u3]), (ve, tr_t), (ve, psC_free)],
                                inc=pe)
                    t_acc = ctx.op("vector", lambda e, ti=ti: e.tensor_add(
                        x_st[:, ti * D:(ti + 1) * D], x_st[:, ti * D:(ti + 1) * D], psC[:]),
                        waits=[(pe, tC)], inc=ve)
                    psC_free = t_acc
                    store_toks[("acc", s, ti)] = t_acc
                for u in (u1, u2, u3):
                    unit_free.append((ve, psC_free))
                    if True:
                        try:
                            load_unit(next(loads))
                        except StopIteration:
                            pass
                iu += 3
            for e_ in ([] if _no_ex else range(8)):
                ua, ub, uw = iu, iu + 1, iu + 2
                for ti in range(STT):
                    def mm1(e, ti=ti, ua=ua):
                        r = None
                        for n in range(2):
                            for c in range(8):
                                r = e.matmul(psA[:, n * 512:(n + 1) * 512],
                                             hT[:, ti * D + c * 128:ti * D + (c + 1) * 128],
                                             unit_ap(ua, c, n * 512, (n + 1) * 512),
                                             start=(c == 0), stop=(c == 7))
                        return r
                    tA = ctx.op("tensor", mm1,
                                waits=[(loader.done, unit_fill[ua]), (ve, gate_toks[ti]),
                                       (se, psA_free), (ve, psC_free)],
                                inc=pe)
                    t_s1 = ctx.op("scalar", lambda e: e.activation(s1b[:], psA[:], AF.Silu),
                                  waits=[(pe, tA), (ve, s1_free)], inc=se)
                    psA_free = t_s1
                    def mm2(e, ti=ti, ub=ub):
                        r = None
                        for n in range(2):
                            for c in range(8):
                                r = e.matmul(psB[:, n * 512:(n + 1) * 512],
                                             hT[:, ti * D + c * 128:ti * D + (c + 1) * 128],
                                             unit_ap(ub, c, n * 512, (n + 1) * 512),
                                             start=(c == 0), stop=(c == 7))
                        return r
                    tB = ctx.op("tensor", mm2,
                                waits=[(loader.done, unit_fill[ub]), (ve, psB_free)],
                                inc=pe)
                    gcol = gates[:, ti * 8 + e_:ti * 8 + e_ + 1]
                    t_g = ctx.op("vector", lambda e, gcol=gcol: e.scalar_tensor_tensor(
                        gb[:], psB[:], gcol, s1b[:], ALU.mult, ALU.mult),
                        waits=[(pe, tB), (se, t_s1), (ve, hb_free)], inc=ve)
                    psB_free = t_g
                    s1_free = t_g
                    tr_t = emit_transpose_h(ctx, cm, gb[:], ps_tr, gT_sb[:], 8,
                                            [(ve, t_g), (ve, tr_free)], pe, ve)
                    tr_free = tr_t
                    hb_free = tr_t
                    def mm3(e, uw=uw):
                        r = None
                        for n in range(2):
                            for c in range(8):
                                r = e.matmul(psC[:, n * 512:(n + 1) * 512],
                                             gT_sb[:, c * 128:(c + 1) * 128],
                                             unit_ap(uw, c, n * 512, (n + 1) * 512),
                                             start=(c == 0), stop=(c == 7))
                        return r
                    tC = ctx.op("tensor", mm3,
                                waits=[(loader.done, unit_fill[uw]), (ve, tr_t), (ve, psC_free)],
                                inc=pe)
                    t_acc = ctx.op("vector", lambda e, ti=ti: e.tensor_add(
                        x_st[:, ti * D:(ti + 1) * D], x_st[:, ti * D:(ti + 1) * D], psC[:]),
                        waits=[(pe, tC)], inc=ve)
                    psC_free = t_acc
                    store_toks[("acc", s, ti)] = t_acc
                for u in (ua, ub, uw):
                    unit_free.append((ve, psC_free))
                    try:
                        load_unit(next(loads))
                    except StopIteration:
                        pass
                iu += 3
            for ti in range(STT):
                store_toks[(s, ti)] = ctx.op(
                    "sync", lambda e, ti=ti, g=NTILES // NST * s + ti: e.dma_start(
                        out=x_dst[g * 128:(g + 1) * 128, :],
                        in_=x_st[:, ti * D:(ti + 1) * D]),
                    waits=[(ve, store_toks[("acc", s, ti)])], inc=st_s, inc_k=16)
        ctx.op("sync", lambda e: None, waits=[(st_s, st_s.v)])



def emit_l1_ffn(ctx, cm, wap, x_src, x_dst):
    nc = ctx.nc
    STT, NST = 8, 4
    UC = 8704
    NU = 3

    with contextlib.ExitStack() as es:
        x_st = es.enter_context(nc.sbuf_tensor("p3_x", [128, STT * D], F32))
        hT = es.enter_context(nc.sbuf_tensor("p3_hT", [128, STT * D], BF16))
        hb = es.enter_context(nc.sbuf_tensor("p3_hb", [128, D], BF16))
        s1h = es.enter_context(nc.sbuf_tensor("p3_s1", [128, 512], BF16))
        s1g = es.enter_context(nc.sbuf_tensor("p3_sg", [128, 512], BF16))
        s1T = es.enter_context(nc.sbuf_tensor("p3_s1T", [128, 512], BF16))
        sv = es.enter_context(nc.sbuf_tensor("p3_sv", [128, STT * 8], F32))
        eexp = es.enter_context(nc.sbuf_tensor("p3_ee", [128, 64], F32))
        itb = es.enter_context(nc.sbuf_tensor("p3_it", [128, 64], F32))
        mskb = es.enter_context(nc.sbuf_tensor("p3_mk", [128, 64], F32))
        cw_st = es.enter_context(nc.sbuf_tensor("p3_cw", [128, STT * 64], F32))
        cwb = es.enter_context(nc.sbuf_tensor("p3_cwb", [128, 64], BF16))
        cwT = es.enter_context(nc.sbuf_tensor("p3_cwT", [64, STT * 128], BF16))
        t8c = es.enter_context(nc.sbuf_tensor("p3_t8", [128, 8], F32))
        rw_sb = es.enter_context(nc.sbuf_tensor("p3_rw", [128, 512], BF16))
        rb_sb = es.enter_context(nc.sbuf_tensor("p3_rb", [1, 64], BF16))
        shb1 = es.enter_context(nc.sbuf_tensor("p3_shb1", [1, 4096], BF16))
        shb2 = es.enter_context(nc.sbuf_tensor("p3_shb2", [1, 2048], BF16))
        b2m = es.enter_context(nc.sbuf_tensor("p3_b2m", [64, 1024], BF16))
        wslot = es.enter_context(nc.sbuf_tensor("p3_w", [128, NU * UC], BF16))
        psA = es.enter_context(nc.psum_tensor("p3_psA", [128, 512], F32))
        psC = es.enter_context(nc.psum_tensor("p3_psC", [128, D], F32))
        ps_tr = es.enter_context(nc.psum_tensor("p3_pstr", [128, 1024], BF16))
        ps_r = es.enter_context(nc.psum_tensor("p3_psr", [128, 64], F32))
        ps_ct = es.enter_context(nc.psum_tensor("p3_psct", [128, 128], BF16))

        loader = CastLoader(ctx, es, "p3", 4096, 2)
        gld = loader.done
        pe = ctx.sem("p3_pe")
        ve = ctx.sem("p3_ve")
        se = ctx.sem("p3_se")
        ld = ctx.sem("p3_ld")
        st_s = ctx.sem("p3_st")

        loader.load(rw_sb[:], wap["l1f_rw"].rearrange("(c p) n -> p c n", p=128), 512)
        loader.load(rb_sb[:], wap["l1f_rb"], 64, nparts=1)
        loader.load(shb1[:], wap["l1f_sb1"].rearrange("l o n -> o l n"), 4096, nparts=1)
        loader.load(shb2[:], wap["l1f_sb2"].rearrange("l o n -> o l n"), 2048, nparts=1)
        tok_static = loader.load(b2m[:64, :], wap["l1f_b2m"], 1024, nparts=64)

        unit_fill = []
        unit_free = []

        def load_unit(srcs):
            i = len(unit_fill)
            rw_ = [unit_free[i - NU]] if i >= NU else []
            col0 = (i % NU) * UC
            off = 0
            t = None
            for j, (src, cols) in enumerate(srcs):
                np_ = src.shape[0] if len(src.shape) == 2 else 128
                t = loader.load(wslot[:np_, col0 + off:col0 + off + cols], src, cols,
                                ring_waits=rw_ if j == 0 else (),
                                nparts=np_)
                off += cols
            unit_fill.append(t)

        def uap(i, off, cols):
            col0 = (i % NU) * UC
            return wslot[:, col0 + off:col0 + off + cols]

        def gen_loads():
            for s in range(NST):
                for sh in range(2):
                    for hc in range(4):
                        sw1p = wap["l1f_sw1"][sh, :, hc * 512:(hc + 1) * 512]
                        sw2p = wap["l1f_sw2"][sh, hc * 512:(hc + 1) * 512, :]
                        yield [(sw1p.rearrange("(c p) n -> p c n", p=128), 4096),
                               (sw2p.rearrange("(c p) n -> p c n", p=128), 4096)]
                for pr in range(32):
                    w1p = wap["l1f_w1p"][pr]
                    yield [(w1p.rearrange("(c p) n -> p c n", p=128), 4096),
                           (wap["l1f_w2"][2 * pr].rearrange("(c p) n -> p c n", p=128), 2048),
                           (wap["l1f_w2"][2 * pr + 1].rearrange("(c p) n -> p c n", p=128), 2048),
                           (wap["l1f_b1p"][pr], 512)]
        loads = gen_loads()
        for _ in range(NU):
            load_unit(next(loads))
        iu = 0

        store_toks = {}
        psA_free = 0
        psC_free = 0
        tr_free = 0
        s1_free = 0
        s1g_free = 0
        psr_free = 0
        ct_free = 0

        for s in range(NST):
            cw_toks = []
            cwT_toks = []
            for ti in range(STT):
                xt = ctx.op("sync", lambda e, ti=ti, g=NTILES // NST * s + ti: e.dma_start(
                    out=x_st[:, ti * D:(ti + 1) * D], in_=x_src[g * 128:(g + 1) * 128, :]),
                    waits=[(st_s, store_toks[(s - 1, ti)])] if s else [],
                    inc=ld, inc_k=16)
                sc_t, h_t = emit_ln_tile(ctx, cm, x_st[:, ti * D:(ti + 1) * D], hb[:],
                                         sv[:, ti * 8:ti * 8 + 8], [(ld, xt)], se, ve)
                tr_t = emit_transpose_h(ctx, cm, hb[:], ps_tr,
                                        hT[:, ti * D:(ti + 1) * D], 8,
                                        [(ve, h_t), (ve, tr_free)], pe, ve)
                tr_free = tr_t
                def mmr(e, ti=ti):
                    r = None
                    for c in range(8):
                        r = e.matmul(ps_r[:], hT[:, ti * D + c * 128:ti * D + (c + 1) * 128],
                                     rw_sb[:, c * 64:(c + 1) * 64],
                                     start=(c == 0), stop=False)
                    return e.matmul(ps_r[:], cm.ones1[:], rb_sb[:],
                                    start=False, stop=True)
                t_mm = ctx.op("tensor", mmr,
                              waits=[(ve, tr_t), (gld, tok_static), (se, psr_free)],
                              inc=pe)
                t_nm = ctx.chain("vector", [
                    lambda e: e.tensor_reduce(t8c[:, 0:1], ps_r[:], AX.X, ALU.max),
                    lambda e: e.tensor_scalar_mul(t8c[:, 1:2], t8c[:, 0:1], -1.0),
                ], waits=[(pe, t_mm)], inc=ve)
                t_ex = ctx.op("scalar", lambda e: e.activation(
                    eexp[:], ps_r[:], AF.Exp, bias=t8c[:, 1:2]),
                    waits=[(ve, t_nm)], inc=se)
                psr_free = t_ex
                steps = [lambda e: e.tensor_copy(itb[:], eexp[:])]
                for k in range(8):
                    steps.append(lambda e: e.tensor_reduce(t8c[:, 2:3], itb[:], AX.X, ALU.max))
                    if k < 7:
                        steps.append(lambda e: e.tensor_scalar(mskb[:], itb[:], t8c[:, 2:3],
                                                               None, ALU.is_ge))
                        steps.append(lambda e: e.scalar_tensor_tensor(
                            itb[:], mskb[:], cm.negbig[:], itb[:], ALU.mult, ALU.add))
                steps.append(lambda e: e.tensor_scalar(mskb[:], eexp[:], t8c[:, 2:3],
                                                       None, ALU.is_ge))
                steps.append(lambda e: e.tensor_mul(itb[:], eexp[:], mskb[:]))
                steps.append(lambda e: e.tensor_reduce(t8c[:, 3:4], itb[:], AX.X, ALU.add))
                t_m8 = ctx.chain("vector", steps, waits=[(se, t_ex)], inc=ve)
                t_rs = ctx.chain("scalar", [
                    lambda e: e.activation(t8c[:, 4:5], t8c[:, 3:4], AF.Ln),
                    lambda e: e.activation(t8c[:, 4:5], t8c[:, 4:5], AF.Exp, scale=-1.0),
                ], waits=[(ve, t_m8)], inc=se)
                cwc = cw_st[:, ti * 64:(ti + 1) * 64]
                t_cw = ctx.chain("vector", [
                    lambda e, cwc=cwc: e.tensor_scalar(cwc, itb[:], t8c[:, 4:5], None, ALU.mult),
                    lambda e, cwc=cwc: e.tensor_copy(cwb[:], cwc),
                ], waits=[(se, t_rs), (ve, ct_free)], inc=ve)
                cw_toks.append(t_cw)
                t_ct = ctx.op("tensor", lambda e: e.transpose(ps_ct[0:64, 0:128], cwb[:], cm.ident[:]),
                              waits=[(ve, t_cw)], inc=pe)
                t_ce = ctx.op("vector", lambda e, ti=ti: e.tensor_copy(
                    cwT[:64, ti * 128:(ti + 1) * 128], ps_ct[:64, 0:128]),
                    waits=[(pe, t_ct)], inc=ve)
                ct_free = t_ce
                cwT_toks.append(t_ce)

            for sh in range(2):
                for hc in range(4):
                    u = iu
                    b1off = sh * 2048 + hc * 512
                    for ti in range(STT):
                        def mm1(e, ti=ti, u=u, b1off=b1off):
                            r = None
                            for c in range(8):
                                r = e.matmul(psA[:], hT[:, ti * D + c * 128:ti * D + (c + 1) * 128],
                                             uap(u, c * 512, 512),
                                             start=(c == 0), stop=False)
                            return e.matmul(psA[:], cm.ones1[:], shb1[:, b1off:b1off + 512],
                                            start=False, stop=True)
                        tA = ctx.op("tensor", mm1,
                                    waits=[(gld, unit_fill[u]), (se, psA_free), (ve, psC_free)],
                                    inc=pe)
                        t_s1 = ctx.op("scalar", lambda e: e.activation(s1h[:], psA[:], AF.Silu),
                                      waits=[(pe, tA), (ve, s1_free)], inc=se)
                        psA_free = t_s1
                        tr_t = emit_transpose_h(ctx, cm, s1h[:], ps_tr, s1T[:], 4,
                                                [(se, t_s1), (ve, tr_free)], pe, ve)
                        tr_free = tr_t
                        s1_free = tr_t
                        def mm2(e, ti=ti, u=u):
                            r = None
                            for n in range(2):
                                for c in range(4):
                                    r = e.matmul(psC[:, n * 512:(n + 1) * 512],
                                                 s1T[:, c * 128:(c + 1) * 128],
                                                 uap(u, 4096 + c * 1024 + n * 512, 512),
                                                 start=(c == 0), stop=(c == 3))
                            return r
                        tC = ctx.op("tensor", mm2, waits=[(ve, tr_t), (ve, psC_free)],
                                    inc=pe)
                        t_acc = ctx.op("vector", lambda e, ti=ti: e.tensor_add(
                            x_st[:, ti * D:(ti + 1) * D], x_st[:, ti * D:(ti + 1) * D], psC[:]),
                            waits=[(pe, tC)], inc=ve)
                        psC_free = t_acc
                        store_toks[("acc", s, ti)] = t_acc
                    unit_free.append((ve, psC_free))
                    try:
                        load_unit(next(loads))
                    except StopIteration:
                        pass
                    iu += 1
            for pr in range(32):
                u = iu
                for ti in range(STT):
                    def mm1(e, ti=ti, u=u):
                        r = None
                        for c in range(8):
                            r = e.matmul(psA[:], hT[:, ti * D + c * 128:ti * D + (c + 1) * 128],
                                         uap(u, c * 512, 512),
                                         start=(c == 0), stop=False)
                        return e.matmul(psA[:], cm.ones1[:], uap(u, 8192, 512)[0:1, :],
                                        start=False, stop=True)
                    tA = ctx.op("tensor", mm1,
                                waits=[(gld, unit_fill[u]), (se, psA_free), (ve, psC_free),
                                       (ve, cw_toks[ti])],
                                inc=pe)
                    t_s1 = ctx.op("scalar", lambda e: e.activation(s1h[:], psA[:], AF.Silu),
                                  waits=[(pe, tA), (ve, s1_free)], inc=se)
                    psA_free = t_s1
                    def scale(e, ti=ti, pr=pr):
                        e.tensor_scalar(s1g[:, 0:256], s1h[:, 0:256],
                                        cw_st[:, ti * 64 + 2 * pr:ti * 64 + 2 * pr + 1],
                                        None, ALU.mult)
                        return e.tensor_scalar(s1g[:, 256:512], s1h[:, 256:512],
                                               cw_st[:, ti * 64 + 2 * pr + 1:ti * 64 + 2 * pr + 2],
                                               None, ALU.mult)
                    t_g = ctx.op("vector", scale,
                                 waits=[(se, t_s1), (ve, s1g_free)], inc=ve)
                    s1_free = t_g
                    tr_t = emit_transpose_h(ctx, cm, s1g[:], ps_tr, s1T[:], 4,
                                            [(ve, t_g), (ve, tr_free)], pe, ve)
                    tr_free = tr_t
                    s1g_free = tr_t
                    def mm2(e, ti=ti, u=u):
                        r = None
                        for eh in range(2):
                            for n in range(2):
                                for c in range(2):
                                    r = e.matmul(psC[:, n * 512:(n + 1) * 512],
                                                 s1T[:, eh * 256 + c * 128:eh * 256 + (c + 1) * 128],
                                                 uap(u, 4096 + eh * 2048 + c * 1024 + n * 512, 512),
                                                 start=(eh == 0 and c == 0),
                                                 stop=(eh == 1 and c == 1))
                        return r
                    tC = ctx.op("tensor", mm2, waits=[(ve, tr_t), (ve, psC_free)],
                                inc=pe)
                    t_acc = ctx.op("vector", lambda e, ti=ti: e.tensor_add(
                        x_st[:, ti * D:(ti + 1) * D], x_st[:, ti * D:(ti + 1) * D], psC[:]),
                        waits=[(pe, tC)], inc=ve)
                    psC_free = t_acc
                    store_toks[("acc", s, ti)] = t_acc
                unit_free.append((ve, psC_free))
                try:
                    load_unit(next(loads))
                except StopIteration:
                    pass
                iu += 1
            for ti in range(STT):
                def mmb(e, ti=ti):
                    r = None
                    for n in range(2):
                        r = e.matmul(psC[:, n * 512:(n + 1) * 512], cm.ones1[:],
                                     shb2[:, n * 512:(n + 1) * 512],
                                     start=True, stop=False)
                        r = e.matmul(psC[:, n * 512:(n + 1) * 512], cm.ones1[:],
                                     shb2[:, 1024 + n * 512:1024 + (n + 1) * 512],
                                     start=False, stop=False)
                        r = e.matmul(psC[:, n * 512:(n + 1) * 512],
                                     cwT[:64, ti * 128:(ti + 1) * 128],
                                     b2m[:64, n * 512:(n + 1) * 512],
                                     start=False, stop=True)
                    return r
                t_mm = ctx.op("tensor", mmb,
                              waits=[(ve, cwT_toks[ti]), (ve, psC_free), (gld, tok_static)],
                              inc=pe)
                t_acc = ctx.op("vector", lambda e, ti=ti: e.tensor_add(
                    x_st[:, ti * D:(ti + 1) * D], x_st[:, ti * D:(ti + 1) * D], psC[:]),
                    waits=[(pe, t_mm)], inc=ve)
                psC_free = t_acc
                store_toks[(s, ti)] = ctx.op(
                    "sync", lambda e, ti=ti, g=NTILES // NST * s + ti: e.dma_start(
                        out=x_dst[g * 128:(g + 1) * 128, :],
                        in_=x_st[:, ti * D:(ti + 1) * D]),
                    waits=[(ve, t_acc)], inc=st_s, inc_k=16)
        ctx.op("sync", lambda e: None, waits=[(st_s, st_s.v)])



def emit_softmax_attention(ctx, cm, wap, cfg, x_src, x_dst):
    nc = ctx.nc
    nm = cfg["name"]
    scale = 0.125

    with contextlib.ExitStack() as es:
        qT = es.enter_context(nc.sbuf_tensor(nm + "_qT", [128, 8 * TS], BF16))
        kTb = es.enter_context(nc.sbuf_tensor(nm + "_kT", [128, 8 * TS], BF16))
        v_row = es.enter_context(nc.sbuf_tensor(nm + "_v", [128, 16 * 1040], BF16))
        hT = es.enter_context(nc.sbuf_tensor(nm + "_hT", [128, 16 * D], BF16))
        oT = hT
        wu = es.enter_context(nc.sbuf_tensor(nm + "_wu", [128, 8 * D], BF16))
        pbuf = es.enter_context(nc.sbuf_tensor(nm + "_pb", [128, 2048], BF16))
        xb = es.enter_context(nc.sbuf_tensor(nm + "_xb", [128, 2 * D], F32))
        hb = es.enter_context(nc.sbuf_tensor(nm + "_hb", [128, D], BF16))
        qtok = es.enter_context(nc.sbuf_tensor(nm + "_qt", [128, D], BF16))
        sqb = es.enter_context(nc.sbuf_tensor(nm + "_sq", [128, D], F32))
        sv = es.enter_context(nc.sbuf_tensor(nm + "_sv", [128, 16 * 8], F32))
        nrm = es.enter_context(nc.sbuf_tensor(nm + "_nrm", [128, 48], F32))
        nsm = es.enter_context(nc.sbuf_tensor(nm + "_nsm", [16, 288], F32))
        mrow = es.enter_context(nc.sbuf_tensor(nm + "_mr", [1, 64], F32))
        Mb = es.enter_context(nc.sbuf_tensor(nm + "_Mb", [128, 32], F32))
        den_sb = es.enter_context(nc.sbuf_tensor(nm + "_den", [1, 512], F32))
        rden_bf = es.enter_context(nc.sbuf_tensor(nm + "_rdb", [1, 512], BF16))
        rdenB = es.enter_context(nc.sbuf_tensor(nm + "_rB", [128, 512], F32))
        bias_sb = es.enter_context(nc.sbuf_tensor(nm + "_bia", [1, 4 * D], BF16))
        if cfg.get("affine") or cfg.get("save_h") is not None:
            hf = es.enter_context(nc.sbuf_tensor(nm + "_hf", [128, D], F32))
            affb = es.enter_context(nc.sbuf_tensor(nm + "_aff", [128, 2 * D], BF16))
            affr = es.enter_context(nc.sbuf_tensor(nm + "_afr", [1, 2 * D], BF16))
        if cfg.get("rope"):
            trg = es.enter_context(nc.sbuf_tensor(nm + "_trg", [128, 1024], F32))
        if cfg.get("toep"):
            expC = es.enter_context(nc.sbuf_tensor(nm + "_eC", [128, 4096], BF16))
            Cstg = es.enter_context(nc.sbuf_tensor(nm + "_Cs", [128, 1024], F32))
        pbig = es.enter_context(nc.psum_tensor(nm + "_pS", [128, 2048], F32))
        pmid = es.enter_context(nc.psum_tensor(nm + "_pM", [128, 1024], F32))
        pone = es.enter_context(nc.psum_tensor(nm + "_p1", [128, 512], F32))
        ptr = es.enter_context(nc.psum_tensor(nm + "_pT", [128, 1024], BF16))

        loader = CastLoader(ctx, es, nm, 1024, 2)
        gld = loader.done
        pe = ctx.sem(nm + "_pe")
        ve = ctx.sem(nm + "_ve")
        se = ctx.sem(nm + "_se")
        ld = ctx.sem(nm + "_ld")
        st_s = ctx.sem(nm + "_st")

        tok_static = 0
        if cfg.get("bias"):
            for j, bn in enumerate(cfg["bias"]):
                if bn is None:
                    continue
                tok_static = loader.load(bias_sb[:1, j * D:(j + 1) * D],
                                         wap[bn], D, nparts=1)
        aff_tok = 0
        pmid_free = 0
        if cfg.get("affine"):
            g_n, b_n = cfg["affine"]
            loader.load(affr[:1, 0:D], wap[g_n], D, nparts=1)
            t2 = loader.load(affr[:1, D:2 * D], wap[b_n], D, nparts=1)
            for j in range(2):
                t_b = ctx.op("tensor", lambda e, j=j: [
                    e.matmul(pmid[:, n2 * 512:(n2 + 1) * 512], cm.ones1[:],
                             affr[:1, j * D + n2 * 512:j * D + (n2 + 1) * 512],
                             start=True, stop=True) for n2 in range(2)][-1],
                    waits=[(gld, t2), (cm.ldc, cm.ld_tok), (ve, pmid_free)],
                    inc=pe)
                aff_tok = ctx.op("vector", lambda e, j=j: e.tensor_copy(
                    affb[:, j * D:(j + 1) * D], pmid[:]),
                    waits=[(pe, t_b)], inc=ve)
                pmid_free = aff_tok

        store_toks = {}
        ldrow = {}
        pbig_free = [0, 0]
        pbuf_free = [0, 0]
        ptr_free = 0
        pone_free = 0
        hb_free = 0
        qtok_free = 0
        eC_free = 0

        def load_w(name):
            t = None
            for hh in range(8):
                src = wap[name][hh * 128:(hh + 1) * 128, :]
                t = loader.load(wu[:, hh * D:(hh + 1) * D], src, 1024,
                                ring_waits=[(pe, load_w.free)] if hh == 0 else ())
            return t
        load_w.free = 0

        for r in range(R):
            tok0 = r * TS
            h_toks = []
            wtok_q = load_w(cfg["wq"])
            for ti in range(16):
                g = tok0 // 128 + ti
                slot = ti % 2
                xw = ctx.op("sync", lambda e, ti=ti, g=g, slot=slot: e.dma_start(
                    out=xb[:, slot * D:(slot + 1) * D],
                    in_=x_src[g * 128:(g + 1) * 128, :]),
                    waits=[(ve, ldrow.get(("xb", slot), 0))],
                    inc=ld, inc_k=16)
                x_ap = xb[:, slot * D:(slot + 1) * D]
                if cfg.get("affine") or cfg.get("save_h") is not None:
                    sc_t, hf_t = emit_ln_tile(ctx, cm, x_ap, hf[:],
                                              sv[:, ti * 8:ti * 8 + 8],
                                              [(ld, xw), (ve, hb_free),
                                               (st_s, store_toks.get(("h", r, ti - 1), 0))],
                                              se, ve)
                    ldrow[("xb", slot)] = hf_t
                    if cfg.get("save_h") is not None:
                        store_toks[("h", r, ti)] = ctx.op(
                            "sync", lambda e, g=g: e.dma_start(
                                out=cfg["save_h"][g * 128:(g + 1) * 128, :], in_=hf[:]),
                            waits=[(ve, hf_t)], inc=st_s, inc_k=16)
                    if cfg.get("affine"):
                        h_t = ctx.chain("vector", [
                            lambda e: e.tensor_mul(sqb[:], hf[:], affb[:, 0:D]),
                            lambda e: e.tensor_add(sqb[:], sqb[:], affb[:, D:2 * D]),
                            lambda e: e.tensor_copy(hb[:], sqb[:]),
                        ], waits=[(ve, hf_t)], inc=ve)
                    else:
                        h_t = ctx.op("vector", lambda e: e.tensor_copy(hb[:], hf[:]),
                                     waits=[(ve, hf_t)], inc=ve)

                else:
                    sc_t, h_t = emit_ln_tile(ctx, cm, x_ap, hb[:],
                                             sv[:, ti * 8:ti * 8 + 8],
                                             [(ld, xw), (ve, hb_free)], se, ve)
                    ldrow[("xb", slot)] = h_t
                tr_t = emit_transpose_h(ctx, cm, hb[:], ptr,
                                        hT[:, ti * D:(ti + 1) * D], 8,
                                        [(ve, h_t), (ve, ptr_free)], pe, ve)
                ptr_free = tr_t
                hb_free = tr_t
                h_toks.append(tr_t)
            t_vo = ctx.op("vector", lambda e: e.memset(
                v_row[:].rearrange("p (t h o) -> p t h o", t=16, h=16)[:, :, :, 64:65],
                1.0), inc=ve)

            nrm_base = {"q": 0, "k": 16}
            proj_toks = {}
            for pi, (pname, wname) in enumerate(
                    (("q", cfg["wq"]), ("k", cfg["wk"]), ("v", cfg["wv"]))):
                wtok = wtok_q if pi == 0 else load_w(wname)
                bias_j = {"q": 0, "k": 1, "v": 2}[pname] if cfg.get("bias") else None
                for ti in range(16):
                    def mmp(e, ti=ti, bias_j=bias_j):
                        rr = None
                        for n2 in range(2):
                            for c in range(8):
                                rr = e.matmul(pmid[:, n2 * 512:(n2 + 1) * 512],
                                              hT[:, ti * D + c * 128:ti * D + (c + 1) * 128],
                                              wu[:, c * D + n2 * 512:c * D + (n2 + 1) * 512],
                                              start=(c == 0),
                                              stop=(c == 7 and bias_j is None))
                            if bias_j is not None:
                                rr = e.matmul(pmid[:, n2 * 512:(n2 + 1) * 512], cm.ones1[:],
                                              bias_sb[:1, bias_j * D + n2 * 512:
                                                      bias_j * D + (n2 + 1) * 512],
                                              start=False, stop=True)
                        return rr
                    t_mm = ctx.op("tensor", mmp,
                                  waits=[(gld, wtok), (ve, h_toks[ti]), (ve, pmid_free),
                                         (gld, tok_static)],
                                  inc=pe)
                    if pname == "v":
                        dst = v_row[:, ti * 1040:(ti + 1) * 1040]
                        dst = dst.rearrange("p (h o) -> p h o", h=16)[:, :, 0:64]
                        t_ev = ctx.op("vector", lambda e, dst=dst: e.tensor_copy(
                            dst, pmid[:].rearrange("p (h o) -> p h o", h=16)),
                            waits=[(pe, t_mm), (ve, t_vo)], inc=ve)
                        pmid_free = t_ev
                        proj_toks[("v", ti)] = t_ev
                        continue
                    if cfg.get("rope"):
                        g = tok0 // 128 + ti
                        ctx.op("sync", lambda e, g=g: e.dma_start(
                            out=trg[:, 0:512],
                            in_=wap["c_rope_cos"][(g % 16) * 128:(g % 16 + 1) * 128, :]),
                            waits=[(ve, ldrow.get("trg", 0))], inc=ld, inc_k=16)
                        tt = ctx.op("sync", lambda e, g=g: e.dma_start(
                            out=trg[:, 512:1024],
                            in_=wap["c_rope_sin"][(g % 16) * 128:(g % 16 + 1) * 128, :]),
                            inc=ld, inc_k=16)
                        def ev(e):
                            pmid3 = pmid[:].rearrange("p (a b) -> p a b", b=2)
                            qt3 = qtok[:].rearrange("p (a b) -> p a b", b=2)
                            e.tensor_mul(qt3[:, :, 0:1], pmid3[:, :, 0:1],
                                         trg[:, 0:512].rearrange("p (a b) -> p a b", b=1))
                            return e.tensor_mul(qt3[:, :, 1:2], pmid3[:, :, 0:1],
                                                trg[:, 512:1024].rearrange("p (a b) -> p a b", b=1))
                        t_ev = ctx.op("vector", ev,
                                      waits=[(pe, t_mm), (ld, tt), (ve, qtok_free)],
                                      inc=ve)
                        ldrow["trg"] = t_ev
                    else:
                        t_ev = ctx.op("vector", lambda e: e.tensor_copy(qtok[:], pmid[:]),
                                      waits=[(pe, t_mm), (ve, qtok_free)], inc=ve)
                    pmid_free = t_ev
                    t_sq = ctx.op("vector", lambda e: e.tensor_mul(sqb[:], qtok[:], qtok[:]),
                                  waits=[(ve, t_ev)], inc=ve)
                    nb = nrm_base[pname]
                    if ti > 0:
                        upd = (lambda e, nb=nb: e.tensor_max(
                            nrm[:, nb:nb + 16], nrm[:, nb:nb + 16], nrm[:, 32:48]))
                    else:
                        upd = (lambda e, nb=nb: e.tensor_copy(
                            nrm[:, nb:nb + 16], nrm[:, 32:48]))
                    t_nr = ctx.chain("vector", [
                        lambda e: e.tensor_reduce(
                            nrm[:, 32:48], sqb[:].rearrange("p (h d) -> p h d", h=16),
                            AX.X, ALU.add),
                        upd,
                    ], waits=[(ve, t_sq)], inc=ve)
                    dstT = qT if pname == "q" else kTb
                    def pe_tr(e):
                        rr = None
                        for c in range(8):
                            rr = e.transpose(ptr[:, c * 128:(c + 1) * 128],
                                             qtok[:, c * 128:(c + 1) * 128], cm.ident[:])
                        return rr
                    t_tr = ctx.op("tensor", pe_tr,
                                  waits=[(ve, t_nr), (ve, ptr_free)], inc=pe)
                    dst = dstT[:].rearrange("p (c t) -> p c t", c=8)[:, :, ti * 128:(ti + 1) * 128]
                    t_te = ctx.op("vector", lambda e, dst=dst: e.tensor_copy(
                        dst, ptr[:].rearrange("p (c t) -> p c t", c=8)),
                        waits=[(pe, t_tr)], inc=ve)
                    ptr_free = t_te
                    qtok_free = t_te
                    proj_toks[(pname, ti)] = t_te
                load_w.free = pe.v

            t_trn = ctx.op("tensor", lambda e: [
                e.transpose(pmid[0:16, 0:128], nrm[:, 0:16], cm.identf[:]),
                e.transpose(pmid[0:16, 128:256], nrm[:, 16:32], cm.identf[:])][-1],
                waits=[(ve, ve.v), (ve, pmid_free)], inc=pe)
            t_nsm = ctx.chain("vector", [
                lambda e: e.tensor_copy(nsm[:16, 0:128], pmid[0:16, 0:128]),
                lambda e: e.tensor_copy(nsm[:16, 128:256], pmid[0:16, 128:256]),
                lambda e: e.tensor_reduce(nsm[:16, 256:257], nsm[:16, 0:128], AX.X, ALU.max),
                lambda e: e.tensor_reduce(nsm[:16, 257:258], nsm[:16, 128:256], AX.X, ALU.max),
            ], waits=[(pe, t_trn)], inc=ve)
            pmid_free = t_nsm
            t_lnm = ctx.chain("scalar", [
                lambda e: e.activation(nsm[:16, 258:259], nsm[:16, 256:257], AF.Ln),
                lambda e: e.activation(nsm[:16, 259:260], nsm[:16, 257:258], AF.Ln),
            ], waits=[(ve, t_nsm)], inc=se)
            t_m = ctx.op("vector", lambda e: e.tensor_add(
                nsm[:16, 260:261], nsm[:16, 258:259], nsm[:16, 259:260]),
                waits=[(se, t_lnm)], inc=ve)
            t_me = ctx.op("scalar", lambda e: e.activation(
                nsm[:16, 261:262], nsm[:16, 260:261], AF.Exp, scale=0.5),
                waits=[(ve, t_m)], inc=se)
            t_mt = ctx.op("tensor", lambda e: e.transpose(
                pone[0:1, 0:16], nsm[:16, 261:262], cm.identf[:16, :16]),
                waits=[(se, t_me), (ve, pone_free)], inc=pe)
            t_mr = ctx.op("vector", lambda e: e.tensor_copy(mrow[:1, 0:16], pone[0:1, 0:16]),
                          waits=[(pe, t_mt)], inc=ve)
            t_mrb = ctx.op("scalar", lambda e: e.copy(rden_bf[:1, 0:16], mrow[:1, 0:16]),
                           waits=[(ve, t_mr)], inc=se)
            t_mbb = ctx.op("tensor", lambda e: e.matmul(
                pone[:, 0:16], cm.ones1[:], rden_bf[:1, 0:16], start=True, stop=True),
                waits=[(se, t_mrb)], inc=pe)
            t_Mb = ctx.chain("vector", [
                lambda e: e.tensor_copy(Mb[:, 0:16], pone[:, 0:16]),
                lambda e: e.tensor_scalar_mul(Mb[:, 16:32], Mb[:, 0:16], -scale),
            ], waits=[(pe, t_mbb)], inc=ve)
            pone_free = t_Mb

            oT_toks = []
            for h in range(H):
                base = (h % 2) * 64
                cp = h // 2
                eC_tok = 0
                if cfg.get("toep"):
                    for hh in range(4):
                        t_l = ctx.op("sync", lambda e, h=h, hh=hh: e.dma_start(
                            out=Cstg[:],
                            in_=wap[cfg["toep"]][h, :, hh * 1024:(hh + 1) * 1024]),
                            waits=[(se, ldrow.get("Cs", 0))], inc=ld, inc_k=16)
                        t_e = ctx.op("scalar", lambda e, hh=hh: e.activation(
                            expC[:, hh * 1024:(hh + 1) * 1024], Cstg[:], AF.Exp),
                            waits=[(ld, t_l), (ve, eC_free)], inc=se)
                        ldrow["Cs"] = t_e
                        eC_tok = t_e
                mby = Mb[:, 16 + h:16 + h + 1]
                for qsh in range(2):
                    q0 = qsh * 1024
                    kc_hi = 16
                    if cfg.get("causal"):
                        kc_hi = min(16, (q0 + 1023) // 128 + 1)
                    ex_toks = {}
                    av_last = {}
                    av_started = {0: False, 1: False}

                    def emit_st_exp(kc, base=base, cp=cp, q0=q0, mby=mby):
                        def mmst(e, kc=kc, base=base, cp=cp, q0=q0):
                            rr = None
                            for qs in range(2):
                                rr = e.matmul(
                                    pbig[:, (kc % 2) * 1024 + qs * 512:
                                         (kc % 2) * 1024 + (qs + 1) * 512],
                                    kTb[base:base + 64,
                                        cp * TS + kc * 128:cp * TS + (kc + 1) * 128],
                                    qT[base:base + 64,
                                       cp * TS + q0 + qs * 512:cp * TS + q0 + (qs + 1) * 512],
                                    start=True, stop=True)
                            return rr
                        t_st = ctx.op("tensor", mmst,
                                      waits=[(ve, proj_toks[("k", 15)]),
                                             (ve, proj_toks[("q", 15)]),
                                             (se, pbig_free[kc % 2]),
                                             (ve, t_Mb)],
                                      inc=pe)
                        t_ex = ctx.op("scalar", lambda e, kc=kc, mby=mby: e.activation(
                            pbuf[:, (kc % 2) * 1024:(kc % 2 + 1) * 1024],
                            pbig[:, (kc % 2) * 1024:(kc % 2 + 1) * 1024],
                            AF.Exp, bias=mby, scale=scale),
                            waits=[(pe, t_st), (pe, pbuf_free[kc % 2]),
                                   (se, eC_tok)],
                            inc=se)
                        pbig_free[kc % 2] = t_ex
                        fixes = []
                        if cfg.get("toep"):
                            for qs in range(2):
                                n0 = (q0 + qs * 512) - kc * 128 + 1920
                                fixes.append(lambda e, kc=kc, qs=qs, n0=n0: e.tensor_mul(
                                    pbuf[:, (kc % 2) * 1024 + qs * 512:
                                         (kc % 2) * 1024 + (qs + 1) * 512],
                                    pbuf[:, (kc % 2) * 1024 + qs * 512:
                                         (kc % 2) * 1024 + (qs + 1) * 512],
                                    expC[:, n0:n0 + 512]))
                        if cfg.get("causal"):
                            for qs in range(2):
                                off = kc * 128 - (q0 + qs * 512)
                                if 0 <= off < 512:
                                    oi = off // 128
                                    fixes.append(lambda e, kc=kc, qs=qs, oi=oi: e.tensor_mul(
                                        pbuf[:, (kc % 2) * 1024 + qs * 512:
                                             (kc % 2) * 1024 + (qs + 1) * 512],
                                        pbuf[:, (kc % 2) * 1024 + qs * 512:
                                             (kc % 2) * 1024 + (qs + 1) * 512],
                                        cm.causal[:, oi * 512:(oi + 1) * 512]))
                        if fixes:
                            t_fix = ctx.op("vector",
                                           lambda e, fixes=fixes: [f(e) for f in fixes][-1],
                                           waits=[(se, t_ex)], inc=ve)
                            ex_toks[kc] = (ve, t_fix)
                        else:
                            ex_toks[kc] = (se, t_ex)

                    def emit_av(kc, h=h, q0=q0, kc_hi=kc_hi):
                        sem_, tok = ex_toks[kc]
                        for qs in range(2):
                            qb0 = q0 + qs * 512
                            if cfg.get("causal") and kc * 128 > qb0 + 511:
                                continue
                            kmax = min(kc_hi - 1, (qb0 + 511) // 128) \
                                if cfg.get("causal") else 15
                            t_av = ctx.op("tensor", lambda e, kc=kc, qs=qs, kmax=kmax, h=h:
                                          e.matmul(
                                              pmid[0:65, qs * 512:(qs + 1) * 512],
                                              v_row[:, kc * 1040 + h * 65:
                                                    kc * 1040 + (h + 1) * 65],
                                              pbuf[:, (kc % 2) * 1024 + qs * 512:
                                                   (kc % 2) * 1024 + (qs + 1) * 512],
                                              start=(kc == 0), stop=(kc == kmax)),
                                          waits=[(sem_, tok), (ve, pmid_free),
                                                 (ve, proj_toks[("v", 15)])],
                                          inc=pe)
                            av_last[qs] = t_av
                        pbuf_free[kc % 2] = pe.v

                    emit_st_exp(0)
                    for kc in range(1, kc_hi):
                        emit_st_exp(kc)
                        emit_av(kc - 1)
                    emit_av(kc_hi - 1)

                    for qs in range(2):
                        t_de = ctx.op("vector", lambda e, qs=qs: e.tensor_copy(
                            den_sb[:1, :], pmid[64:65, qs * 512:(qs + 1) * 512]),
                            waits=[(pe, av_last[qs]), (ve, pone_free)], inc=ve)
                        t_rd = ctx.chain("scalar", [
                            lambda e: e.activation(den_sb[:1, :], den_sb[:1, :], AF.Ln),
                            lambda e: e.activation(den_sb[:1, :], den_sb[:1, :],
                                                   AF.Exp, scale=-1.0),
                            lambda e: e.copy(rden_bf[:1, 0:512], den_sb[:1, :]),
                        ], waits=[(ve, t_de)], inc=se)
                        t_rb = ctx.op("tensor", lambda e: e.matmul(
                            pone[:, 0:512], cm.ones1[:], rden_bf[:1, 0:512],
                            start=True, stop=True),
                            waits=[(se, t_rd)], inc=pe)
                        t_rB = ctx.op("vector", lambda e: e.tensor_copy(
                            rdenB[:], pone[:, 0:512]),
                            waits=[(pe, t_rb)], inc=ve)
                        pone_free = t_rB
                        dst = oT[base:base + 64,
                                 cp * TS + q0 + qs * 512:cp * TS + q0 + (qs + 1) * 512]
                        t_o = ctx.op("vector", lambda e, dst=dst, qs=qs: e.tensor_mul(
                            dst, pmid[0:64, qs * 512:(qs + 1) * 512], rdenB[0:64, :]),
                            waits=[(ve, t_rB)], inc=ve)
                        oT_toks.append(t_o)
                        if qs == 1:
                            pmid_free = t_o
                eC_free = ve.v
            oT_ready = oT_toks[-1]

            if cfg.get("wo"):
                wtok_o = load_w(cfg["wo"])
                res_src = x_src if cfg.get("res_src", "x") == "x" else cfg["save_h"]
                bias_o = 3 if (cfg.get("bias") and cfg["bias"][3]) else None
                for ti in range(16):
                    g = tok0 // 128 + ti
                    slot = ti % 2
                    xw = ctx.op("sync", lambda e, g=g, slot=slot: e.dma_start(
                        out=xb[:, slot * D:(slot + 1) * D],
                        in_=res_src[g * 128:(g + 1) * 128, :]),
                        waits=[(ve, ldrow.get(("xb", slot), 0)),
                               (st_s, store_toks.get(("h", r, ti), 0))],
                        inc=ld, inc_k=16)
                    def mmo(e, ti=ti, bias_o=bias_o):
                        rr = None
                        for n2 in range(2):
                            for c in range(8):
                                rr = e.matmul(pmid[:, n2 * 512:(n2 + 1) * 512],
                                              oT[:, c * TS + ti * 128:c * TS + (ti + 1) * 128],
                                              wu[:, c * D + n2 * 512:c * D + (n2 + 1) * 512],
                                              start=(c == 0),
                                              stop=(c == 7 and bias_o is None))
                            if bias_o is not None:
                                rr = e.matmul(pmid[:, n2 * 512:(n2 + 1) * 512], cm.ones1[:],
                                              bias_sb[:1, bias_o * D + n2 * 512:
                                                      bias_o * D + (n2 + 1) * 512],
                                              start=False, stop=True)
                        return rr
                    t_mm = ctx.op("tensor", mmo,
                                  waits=[(gld, wtok_o), (ve, oT_ready), (ve, pmid_free)],
                                  inc=pe)
                    t_x = ctx.op("vector", lambda e, slot=slot: e.tensor_add(
                        xb[:, slot * D:(slot + 1) * D],
                        xb[:, slot * D:(slot + 1) * D], pmid[:]),
                        waits=[(pe, t_mm), (ld, xw)], inc=ve)
                    pmid_free = t_x
                    ldrow[("xb", slot)] = t_x
                    store_toks[(r, ti)] = ctx.op(
                        "sync", lambda e, g=g, slot=slot: e.dma_start(
                            out=x_dst[g * 128:(g + 1) * 128, :],
                            in_=xb[:, slot * D:(slot + 1) * D]),
                        waits=[(ve, t_x)], inc=st_s, inc_k=16)
                load_w.free = pe.v
            else:
                for ti in range(16):
                    g = tok0 // 128 + ti
                    slot = ti % 2
                    xw = ctx.op("sync", lambda e, g=g, slot=slot: e.dma_start(
                        out=xb[:, slot * D:(slot + 1) * D],
                        in_=x_src[g * 128:(g + 1) * 128, :]),
                        waits=[(ve, ldrow.get(("xb", slot), 0))],
                        inc=ld, inc_k=16)
                    def pe_tr(e, ti=ti):
                        rr = None
                        for c in range(8):
                            rr = e.transpose(ptr[:, c * 128:(c + 1) * 128],
                                             oT[:, c * TS + ti * 128:c * TS + (ti + 1) * 128],
                                             cm.ident[:])
                        return rr
                    t_tr = ctx.op("tensor", pe_tr,
                                  waits=[(ve, oT_ready), (ve, ptr_free)], inc=pe)
                    t_x = ctx.op("vector", lambda e, slot=slot: e.tensor_add(
                        xb[:, slot * D:(slot + 1) * D],
                        xb[:, slot * D:(slot + 1) * D], ptr[:]),
                        waits=[(pe, t_tr), (ld, xw)], inc=ve)
                    ptr_free = t_x
                    ldrow[("xb", slot)] = t_x
                    store_toks[(r, ti)] = ctx.op(
                        "sync", lambda e, g=g, slot=slot: e.dma_start(
                            out=x_dst[g * 128:(g + 1) * 128, :],
                            in_=xb[:, slot * D:(slot + 1) * D]),
                        waits=[(ve, t_x)], inc=st_s, inc_k=16)
        ctx.op("sync", lambda e: None, waits=[(st_s, st_s.v)])



def emit_l0_attn(ctx, cm, wap, x_src, x_dst):
    nc = ctx.nc
    nm = "p0"
    ctx.serial = True
    with contextlib.ExitStack() as es:
        qT = es.enter_context(nc.sbuf_tensor(nm + "_qT", [128, 8 * TS], BF16))
        k_row = es.enter_context(nc.sbuf_tensor(nm + "_k", [128, 16 * D], BF16))
        v_row = es.enter_context(nc.sbuf_tensor(nm + "_v", [128, 16 * D], BF16))
        hT = es.enter_context(nc.sbuf_tensor(nm + "_hT", [128, 16 * D], BF16))
        oT = hT
        wu = es.enter_context(nc.sbuf_tensor(nm + "_wu", [128, 8 * D], BF16))
        xb = es.enter_context(nc.sbuf_tensor(nm + "_xb", [128, 2 * D], F32))
        hb = es.enter_context(nc.sbuf_tensor(nm + "_hb", [128, D], BF16))
        qf32 = es.enter_context(nc.sbuf_tensor(nm + "_qf", [128, D], F32))
        qln = es.enter_context(nc.sbuf_tensor(nm + "_ql", [128, D], BF16))
        qtok = es.enter_context(nc.sbuf_tensor(nm + "_qt", [128, D], BF16))
        sqb = es.enter_context(nc.sbuf_tensor(nm + "_sq", [128, D], F32))
        sv = es.enter_context(nc.sbuf_tensor(nm + "_sv", [128, 16 * 8], F32))
        st16 = es.enter_context(nc.sbuf_tensor(nm + "_st16", [128, 64], F32))
        gB = es.enter_context(nc.sbuf_tensor(nm + "_gB", [128, 4 * D], BF16))
        gr = es.enter_context(nc.sbuf_tensor(nm + "_gr", [1, 4 * D], BF16))
        S_sb = es.enter_context(nc.sbuf_tensor(nm + "_S", [128, 64], BF16))
        kTc = es.enter_context(nc.sbuf_tensor(nm + "_kTc", [128, 128], BF16))
        ATm = es.enter_context(nc.sbuf_tensor(nm + "_ATm", [128, 128], BF16))
        obuf = es.enter_context(nc.sbuf_tensor(nm + "_ob", [128, 64], BF16))
        triu = es.enter_context(nc.sbuf_tensor(nm + "_tri", [128, 128], BF16))
        pmid = es.enter_context(nc.psum_tensor(nm + "_pM", [128, 1024], F32))
        ptr = es.enter_context(nc.psum_tensor(nm + "_pT", [128, 1024], BF16))
        psS = es.enter_context(nc.psum_tensor(nm + "_pS", [64, 64], F32))
        psA = es.enter_context(nc.psum_tensor(nm + "_pA", [128, 128], F32))
        psO = es.enter_context(nc.psum_tensor(nm + "_pO", [128, 64], F32))

        loader = CastLoader(ctx, es, nm, 1024, 2)
        gld = loader.done
        pe = ctx.sem(nm + "_pe")
        ve = ctx.sem(nm + "_ve")
        se = ctx.sem(nm + "_se")
        ld = ctx.sem(nm + "_ld")
        st_s = ctx.sem(nm + "_st")

        t_tri = ctx.op("sync", lambda e: e.dma_start(out=triu[:], in_=wap["c_triu"]),
                       inc=ld, inc_k=16)
        gtok = 0
        for j, bn in enumerate(("l0a_qg_full", "l0a_qb_full",
                                "l0a_kg_full", "l0a_kb_full")):
            gtok = loader.load(gr[:1, j * D:(j + 1) * D], wap[bn], D, nparts=1)
        pm_free = 0
        gB_tok = 0
        for j in range(4):
            t_b = ctx.op("tensor", lambda e, j=j: [
                e.matmul(pmid[:, n2 * 512:(n2 + 1) * 512], cm.ones1[:],
                         gr[:1, j * D + n2 * 512:j * D + (n2 + 1) * 512],
                         start=True, stop=True) for n2 in range(2)][-1],
                waits=[(gld, gtok), (cm.ldc, cm.ld_tok), (ve, pm_free)],
                inc=pe)
            gB_tok = ctx.op("vector", lambda e, j=j: e.tensor_copy(
                gB[:, j * D:(j + 1) * D], pmid[:]),
                waits=[(pe, t_b)], inc=ve)
            pm_free = gB_tok

        def load_w(name):
            t = None
            for hh in range(8):
                src = wap[name][hh * 128:(hh + 1) * 128, :]
                t = loader.load(wu[:, hh * D:(hh + 1) * D], src, 1024,
                                ring_waits=[(pe, load_w.free)] if hh == 0 else ())
            return t
        load_w.free = 0

        ptr_free = 0
        hb_free = 0
        qtok_free = 0
        qf_free = 0
        qln_free = 0
        ldrow = {}
        store_toks = {}
        st3 = {}

        for r in range(R):
            tok0 = r * TS
            h_toks = []
            wtok_q = load_w("l0a_wqT")
            for ti in range(16):
                g = tok0 // 128 + ti
                slot = ti % 2
                xw = ctx.op("sync", lambda e, g=g, slot=slot: e.dma_start(
                    out=xb[:, slot * D:(slot + 1) * D],
                    in_=x_src[g * 128:(g + 1) * 128, :]),
                    waits=[(ve, ldrow.get(("xb", slot), 0))],
                    inc=ld, inc_k=16)
                sc_t, h_t = emit_ln_tile(ctx, cm, xb[:, slot * D:(slot + 1) * D],
                                         hb[:], sv[:, ti * 8:ti * 8 + 8],
                                         [(ld, xw), (ve, hb_free)], se, ve)
                ldrow[("xb", slot)] = h_t
                tr_t = emit_transpose_h(ctx, cm, hb[:], ptr,
                                        hT[:, ti * D:(ti + 1) * D], 8,
                                        [(ve, h_t), (ve, ptr_free)], pe, ve)
                ptr_free = tr_t
                hb_free = tr_t
                h_toks.append(tr_t)

            proj_last = {}
            for pi, (pname, wname) in enumerate(
                    (("q", "l0a_wqT"), ("k", "l0a_wkT"), ("v", "l0a_wvT"))):
                wtok = wtok_q if pi == 0 else load_w(wname)
                for ti in range(16):
                    def mmp(e, ti=ti):
                        rr = None
                        for n2 in range(2):
                            for c in range(8):
                                rr = e.matmul(pmid[:, n2 * 512:(n2 + 1) * 512],
                                              hT[:, ti * D + c * 128:ti * D + (c + 1) * 128],
                                              wu[:, c * D + n2 * 512:c * D + (n2 + 1) * 512],
                                              start=(c == 0), stop=(c == 7))
                        return rr
                    t_mm = ctx.op("tensor", mmp,
                                  waits=[(gld, wtok), (ve, h_toks[ti]), (ve, pm_free)],
                                  inc=pe)
                    if pname == "v":
                        t_ev = ctx.op("vector", lambda e, ti=ti: e.tensor_copy(
                            v_row[:, ti * D:(ti + 1) * D], pmid[:]),
                            waits=[(pe, t_mm)], inc=ve)
                        pm_free = t_ev
                        proj_last["v"] = t_ev
                        continue
                    t_cp = ctx.op("vector", lambda e: e.tensor_copy(qf32[:], pmid[:]),
                                  waits=[(pe, t_mm), (ve, qf_free)], inc=ve)
                    pm_free = t_cp
                    t_st = ctx.chain("vector", [
                        lambda e: e.tensor_reduce(
                            st16[:, 0:16], qf32[:].rearrange("p (h d) -> p h d", h=16),
                            AX.X, ALU.add),
                        lambda e: e.tensor_scalar_mul(st16[:, 0:16], st16[:, 0:16], 1.0 / 64),
                        lambda e: e.tensor_mul(sqb[:], qf32[:], qf32[:]),
                        lambda e: e.tensor_reduce(
                            st16[:, 16:32], sqb[:].rearrange("p (h d) -> p h d", h=16),
                            AX.X, ALU.add),
                        lambda e: e.tensor_scalar_mul(st16[:, 16:32], st16[:, 16:32], 1.0 / 64),
                        lambda e: e.tensor_mul(st16[:, 32:48], st16[:, 0:16], st16[:, 0:16]),
                        lambda e: e.tensor_sub(st16[:, 16:32], st16[:, 16:32], st16[:, 32:48]),
                    ], waits=[(ve, t_cp)], inc=ve)
                    t_rs = ctx.chain("scalar", [
                        lambda e: e.activation(st16[:, 32:48], st16[:, 16:32], AF.Ln,
                                               bias=cm.eps[:, 0:1]),
                        lambda e: e.activation(st16[:, 32:48], st16[:, 32:48], AF.Exp,
                                               scale=-0.5),
                    ], waits=[(ve, t_st)], inc=se)
                    goff = 0 if pname == "q" else 2 * D
                    steps = []
                    for hh in range(16):
                        steps.append(lambda e, hh=hh: e.tensor_scalar(
                            qln[:, hh * 64:(hh + 1) * 64], qf32[:, hh * 64:(hh + 1) * 64],
                            st16[:, hh:hh + 1], st16[:, 32 + hh:33 + hh],
                            ALU.subtract, ALU.mult))
                    steps.append(lambda e, goff=goff: e.tensor_mul(
                        qln[:], qln[:], gB[:, goff:goff + D]))
                    if pname == "q":
                        steps.append(lambda e, goff=goff: e.tensor_add(
                            qtok[:], qln[:], gB[:, goff + D:goff + 2 * D]))
                    else:
                        steps.append(lambda e, ti=ti, goff=goff: e.tensor_add(
                            k_row[:, ti * D:(ti + 1) * D], qln[:],
                            gB[:, goff + D:goff + 2 * D]))
                    t_ap = ctx.chain("vector", steps,
                                     waits=[(se, t_rs), (ve, gB_tok),
                                            (ve, qln_free),
                                            (ve, qtok_free if pname == "q" else 0)],
                                     inc=ve)
                    qf_free = t_ap
                    qln_free = t_ap
                    if pname == "q":
                        def pe_tr(e):
                            rr = None
                            for c in range(8):
                                rr = e.transpose(ptr[:, c * 128:(c + 1) * 128],
                                                 qtok[:, c * 128:(c + 1) * 128],
                                                 cm.ident[:])
                            return rr
                        t_tr = ctx.op("tensor", pe_tr,
                                      waits=[(ve, t_ap), (ve, ptr_free)], inc=pe)
                        dst = qT[:].rearrange("p (c t) -> p c t", c=8)[
                            :, :, ti * 128:(ti + 1) * 128]
                        t_te = ctx.op("vector", lambda e, dst=dst: e.tensor_copy(
                            dst, ptr[:].rearrange("p (c t) -> p c t", c=8)),
                            waits=[(pe, t_tr)], inc=ve)
                        ptr_free = t_te
                        qtok_free = t_te
                        proj_last["q"] = t_te
                    else:
                        proj_last["k"] = t_ap
                load_w.free = pe.v

            oT_toks = []
            for h in range(H):
                base = (h % 2) * 64
                cp = h // 2
                s_upd = 0
                for i in range(16):
                    t_kt = ctx.op("tensor", lambda e, h=h, i=i: e.transpose(
                        ptr[0:64, 0:128], k_row[:, i * D + h * 64:i * D + (h + 1) * 64],
                        cm.ident[:]),
                        waits=[(ve, proj_last["k"]), (ve, st3.get("ptr", ptr_free))],
                        inc=pe)
                    t_kte = ctx.op("vector", lambda e, base=base: e.tensor_copy(
                        kTc[base:base + 64, :], ptr[0:64, 0:128]),
                        waits=[(pe, t_kt), (pe, st3.get("kTc", 0))], inc=ve)
                    st3["ptr"] = t_kte
                    t_at = ctx.op("tensor", lambda e, base=base, cp=cp, i=i: e.matmul(
                        psA[:], kTc[base:base + 64, :],
                        qT[base:base + 64, cp * TS + i * 128:cp * TS + (i + 1) * 128],
                        start=True, stop=True),
                        waits=[(ve, t_kte), (ve, proj_last["q"]),
                               (ve, st3.get("ATm_v", 0))],
                        inc=pe)
                    st3["kTc"] = t_at
                    t_am = ctx.op("vector", lambda e: e.tensor_mul(
                        ATm[:], psA[:], triu[:]),
                        waits=[(pe, t_at), (pe, st3.get("ATm_pe", 0)), (ld, t_tri)],
                        inc=ve)
                    st3["ATm_v"] = t_am
                    if i > 0:
                        t_se = ctx.op("vector", lambda e, base=base: e.tensor_copy(
                            S_sb[base:base + 64, :], psS[0:64, :]),
                            waits=[(pe, st3.get("S_pe", 0)), (pe, s_upd)], inc=ve)
                        st3["se_last"] = t_se
                    else:
                        t_se = 0
                    def mmo(e, h=h, i=i, base=base, cp=cp, first=(i == 0)):
                        rr = e.matmul(psO[:], ATm[:],
                                      v_row[:, i * D + h * 64:i * D + (h + 1) * 64],
                                      start=True, stop=first)
                        if not first:
                            rr = e.matmul(psO[:],
                                          qT[base:base + 64,
                                             cp * TS + i * 128:cp * TS + (i + 1) * 128],
                                          S_sb[base:base + 64, :], start=False, stop=True)
                        return rr
                    t_o = ctx.op("tensor", mmo,
                                 waits=[(ve, t_am), (ve, t_se), (ve, proj_last["v"]),
                                        (ve, st3.get("ob_v", 0))],
                                 inc=pe)
                    st3["ATm_pe"] = t_o
                    st3["S_pe"] = t_o
                    t_su = ctx.op("tensor", lambda e, h=h, i=i: e.matmul(
                        psS[0:64, :], k_row[:, i * D + h * 64:i * D + (h + 1) * 64],
                        v_row[:, i * D + h * 64:i * D + (h + 1) * 64],
                        start=(i == 0), stop=(i == 15)),
                        waits=[(ve, t_se if i else st3.get("se_last", 0))], inc=pe)
                    s_upd = t_su
                    t_oe = ctx.op("vector", lambda e: e.tensor_copy(obuf[:], psO[:]),
                                  waits=[(pe, t_o), (pe, st3.get("ob_pe", 0))], inc=ve)
                    st3["ob_v"] = t_oe
                    t_ot = ctx.op("tensor", lambda e: e.transpose(
                        ptr[0:64, 128:256], obuf[:, 0:64], cm.ident[:]),
                        waits=[(ve, t_oe), (ve, st3.get("ptr", 0))], inc=pe)
                    st3["ob_pe"] = t_ot
                    t_ote = ctx.op("vector", lambda e, base=base, cp=cp, i=i: e.tensor_copy(
                        oT[base:base + 64, cp * TS + i * 128:cp * TS + (i + 1) * 128],
                        ptr[0:64, 128:256]),
                        waits=[(pe, t_ot)], inc=ve)
                    st3["ptr"] = t_ote
                    oT_toks.append(t_ote)
            ptr_free = st3["ptr"]
            oT_ready = oT_toks[-1]

            wtok_o = load_w("l0a_woT")
            for ti in range(16):
                g = tok0 // 128 + ti
                slot = ti % 2
                xw = ctx.op("sync", lambda e, g=g, slot=slot: e.dma_start(
                    out=xb[:, slot * D:(slot + 1) * D],
                    in_=x_src[g * 128:(g + 1) * 128, :]),
                    waits=[(ve, ldrow.get(("xb", slot), 0))],
                    inc=ld, inc_k=16)
                def mmo4(e, ti=ti):
                    rr = None
                    for n2 in range(2):
                        for c in range(8):
                            rr = e.matmul(pmid[:, n2 * 512:(n2 + 1) * 512],
                                          oT[:, c * TS + ti * 128:c * TS + (ti + 1) * 128],
                                          wu[:, c * D + n2 * 512:c * D + (n2 + 1) * 512],
                                          start=(c == 0), stop=(c == 7))
                    return rr
                t_mm = ctx.op("tensor", mmo4,
                              waits=[(gld, wtok_o), (ve, oT_ready), (ve, pm_free)],
                              inc=pe)
                t_x = ctx.op("vector", lambda e, slot=slot: e.tensor_add(
                    xb[:, slot * D:(slot + 1) * D],
                    xb[:, slot * D:(slot + 1) * D], pmid[:]),
                    waits=[(pe, t_mm), (ld, xw)], inc=ve)
                pm_free = t_x
                ldrow[("xb", slot)] = t_x
                store_toks[(r, ti)] = ctx.op(
                    "sync", lambda e, g=g, slot=slot: e.dma_start(
                        out=x_dst[g * 128:(g + 1) * 128, :],
                        in_=xb[:, slot * D:(slot + 1) * D]),
                    waits=[(ve, t_x)], inc=st_s, inc_k=16)
            load_w.free = pe.v
        ctx.op("sync", lambda e: None, waits=[(st_s, st_s.v)])
    ctx.serial = False


def emit_l3_moe(ctx, cm, wap, a_src, x_src, x_dst, wait_sem, wait_tok):
    nc = ctx.nc
    nm = "p7b"
    NE = 8
    NSTM = 16
    with contextlib.ExitStack() as es:
        ew1_sb = es.enter_context(nc.sbuf_tensor(nm + "_w1", [128, 8 * 4096], BF16))
        ew2_sb = es.enter_context(nc.sbuf_tensor(nm + "_w2", [128, 32 * 1024], BF16))
        eb1_sb = es.enter_context(nc.sbuf_tensor(nm + "_b1", [128, 32], F32))
        eb2m = es.enter_context(nc.sbuf_tensor(nm + "_b2", [8, 1024], BF16))
        midF = es.enter_context(nc.sbuf_tensor(nm + "_mid", [128, 32 * 256], BF16))
        h2T = es.enter_context(nc.sbuf_tensor(nm + "_h2T", [128, 8 * 256], BF16))
        ab = es.enter_context(nc.sbuf_tensor(nm + "_ab", [128, 2 * D], F32))
        ac = es.enter_context(nc.sbuf_tensor(nm + "_ac", [128, 2 * D], F32))
        hf = es.enter_context(nc.sbuf_tensor(nm + "_hf", [128, D], F32))
        h2b = es.enter_context(nc.sbuf_tensor(nm + "_h2b", [128, D], BF16))
        affb = es.enter_context(nc.sbuf_tensor(nm + "_aff", [128, 2 * D], BF16))
        affr = es.enter_context(nc.sbuf_tensor(nm + "_afr", [1, 2 * D], BF16))
        sv = es.enter_context(nc.sbuf_tensor(nm + "_sv", [128, 16], F32))
        wr_sb = es.enter_context(nc.sbuf_tensor(nm + "_wr", [128, 64], BF16))
        rb_sb = es.enter_context(nc.sbuf_tensor(nm + "_rb", [1, 16], BF16))
        lgb = es.enter_context(nc.sbuf_tensor(nm + "_lg", [128, 8], F32))
        web = es.enter_context(nc.sbuf_tensor(nm + "_we", [128, 8], F32))
        t8 = es.enter_context(nc.sbuf_tensor(nm + "_t8", [128, 16], F32))
        w_st = es.enter_context(nc.sbuf_tensor(nm + "_wst", [128, NTILES * 8], F32))
        wbf = es.enter_context(nc.sbuf_tensor(nm + "_wbf", [128, 8], BF16))
        wT_st = es.enter_context(nc.sbuf_tensor(nm + "_wT", [8, NTILES * 128], BF16))
        psF = es.enter_context(nc.psum_tensor(nm + "_pF", [128, 256], F32))
        psO = es.enter_context(nc.psum_tensor(nm + "_pO", [128, D], F32))
        psO2 = es.enter_context(nc.psum_tensor(nm + "_pO2", [128, D], F32))
        ps_tr = es.enter_context(nc.psum_tensor(nm + "_pT", [128, 1024], BF16))
        ps_r = es.enter_context(nc.psum_tensor(nm + "_pR", [128, 8], F32))

        loader = CastLoader(ctx, es, nm, 2048, 1)
        gld = loader.done
        pe = ctx.sem(nm + "_pe")
        ve = ctx.sem(nm + "_ve")
        se = ctx.sem(nm + "_se")
        ld = ctx.sem(nm + "_ld")
        st_s = ctx.sem(nm + "_st")

        loader.load(wr_sb[:], wap["l3f_wr"].rearrange("(c p) n -> p c n", p=128), 64)
        loader.load(rb_sb[:1, 0:8], wap["l3f_br"], 8, nparts=1)
        loader.load(eb2m[:8, :], wap["l3f_eb2"], 1024, nparts=8)
        loader.load(affr[:1, 0:D], wap["l3f_g2"], D, nparts=1)
        tok_static = loader.load(affr[:1, D:2 * D], wap["l3f_b2"], D, nparts=1)
        pm_free = 0
        aff_toks = []
        for j in range(2):
            t_b = ctx.op("tensor", lambda e, j=j: [
                e.matmul(psO[:, n2 * 512:(n2 + 1) * 512], cm.ones1[:],
                         affr[:1, j * D + n2 * 512:j * D + (n2 + 1) * 512],
                         start=True, stop=True) for n2 in range(2)][-1],
                waits=[(gld, tok_static), (cm.ldc, cm.ld_tok), (ve, pm_free)],
                inc=pe)
            t_a = ctx.op("vector", lambda e, j=j: e.tensor_copy(
                affb[:, j * D:(j + 1) * D], psO[:]),
                waits=[(pe, t_b)], inc=ve)
            pm_free = t_a
            aff_toks.append(t_a)
        aff_tok = aff_toks[-1]

        psF_free = 0
        psO_free = [pm_free, 0]
        ptr_free = 0
        psr_free = 0
        h2b_free = 0
        ab_free = {}
        ac_free = {}
        w_toks = {}
        store_toks = {}

        for e_ in range(NE):
            t = None
            for dc in range(8):
                for hh in range(2):
                    src = wap["l3f_ew1"][e_, dc * 128:(dc + 1) * 128,
                                         hh * 2048:(hh + 1) * 2048]
                    t = loader.load(ew1_sb[:, dc * 4096 + hh * 2048:
                                           dc * 4096 + (hh + 1) * 2048],
                                    src, 2048,
                                    ring_waits=[(pe, w_toks.get(("w1f", e_ - 1), 0))]
                                    if dc == 0 and hh == 0 else ())
            tok_w1 = t
            for fc in range(32):
                src = wap["l3f_ew2"][e_, fc * 128:(fc + 1) * 128, :]
                t = loader.load(ew2_sb[:, fc * 1024:(fc + 1) * 1024], src, 1024,
                                ring_waits=[(pe, w_toks.get(("w2f", e_ - 1), 0))]
                                if fc == 0 else ())
            tok_w2 = t
            tok_b1 = loader.load(eb1_sb[:, 0:32], wap["l3f_eb1T"][e_], 32,
                                 ring_waits=[(se, w_toks.get(("b1f", e_ - 1), 0))])
            for s in range(NSTM):
                tis = (2 * s, 2 * s + 1)
                h2_toks = []
                for j, g in enumerate(tis):
                    aw = ctx.op("sync", lambda e, g=g, j=j: e.dma_start(
                        out=ab[:, j * D:(j + 1) * D],
                        in_=a_src[g * 128:(g + 1) * 128, :]),
                        waits=[(ve, ab_free.get(j, 0))] +
                              ([(wait_sem, wait_tok)] if e_ == 0 and s == 0 else []),
                        inc=ld, inc_k=16)
                    sc_t, hf_t = emit_ln_tile(ctx, cm, ab[:, j * D:(j + 1) * D],
                                              hf[:], sv[:, j * 8:j * 8 + 8],
                                              [(ld, aw)], se, ve)
                    t_h2 = ctx.chain("vector", [
                        lambda e: e.tensor_mul(hf[:], hf[:], affb[:, 0:D]),
                        lambda e: e.tensor_add(hf[:], hf[:], affb[:, D:2 * D]),
                        lambda e: e.tensor_copy(h2b[:], hf[:]),
                    ], waits=[(ve, hf_t), (ve, aff_tok), (ve, h2b_free)], inc=ve)
                    ab_free[j] = t_h2
                    def pe_tr(e, j=j):
                        rr = None
                        for c in range(8):
                            rr = e.transpose(ps_tr[:, c * 128:(c + 1) * 128],
                                             h2b[:, c * 128:(c + 1) * 128], cm.ident[:])
                        return rr
                    t_tr = ctx.op("tensor", pe_tr,
                                  waits=[(ve, t_h2), (ve, ptr_free)], inc=pe)
                    dst = h2T[:].rearrange("p (c t) -> p c t", c=8)[:, :, j * 128:(j + 1) * 128]
                    t_te = ctx.op("vector", lambda e, dst=dst: e.tensor_copy(
                        dst, ps_tr[:].rearrange("p (c t) -> p c t", c=8)),
                        waits=[(pe, t_tr)], inc=ve)
                    ptr_free = t_te
                    h2b_free = t_te
                    h2_toks.append(t_te)
                    if e_ == 0:
                        def mmr(e, j=j):
                            rr = None
                            for c in range(8):
                                rr = e.matmul(ps_r[:],
                                              h2T[:, c * 256 + j * 128:c * 256 + (j + 1) * 128],
                                              wr_sb[:, c * 8:(c + 1) * 8],
                                              start=(c == 0), stop=False)
                            return e.matmul(ps_r[:], cm.ones1[:], rb_sb[:1, 0:8],
                                            start=False, stop=True)
                        t_mm = ctx.op("tensor", mmr,
                                      waits=[(ve, t_te), (se, psr_free)], inc=pe)
                        t_lg = ctx.chain("vector", [
                            lambda e: e.tensor_copy(lgb[:], ps_r[:]),
                            lambda e: e.tensor_reduce(t8[:, 0:1], lgb[:], AX.X, ALU.max),
                            lambda e: e.tensor_scalar(t8[:, 8:16], lgb[:], t8[:, 0:1],
                                                      None, ALU.is_ge),
                            lambda e: e.scalar_tensor_tensor(t8[:, 8:16], t8[:, 8:16],
                                                             cm.negbig[:], lgb[:],
                                                             ALU.mult, ALU.add),
                            lambda e: e.tensor_reduce(t8[:, 1:2], t8[:, 8:16], AX.X, ALU.max),
                            lambda e: e.tensor_scalar(t8[:, 8:16], lgb[:], t8[:, 1:2],
                                                      None, ALU.is_ge),
                            lambda e: e.tensor_scalar_mul(t8[:, 2:3], t8[:, 0:1], -1.0),
                        ], waits=[(pe, t_mm), (cm.ldc, cm.ld_tok)], inc=ve)
                        t_we = ctx.op("scalar", lambda e: e.activation(
                            web[:], ps_r[:], AF.Exp, bias=t8[:, 2:3]),
                            waits=[(ve, t_lg)], inc=se)
                        psr_free = t_we
                        t_w = ctx.chain("vector", [
                            lambda e: e.tensor_mul(web[:], web[:], t8[:, 8:16]),
                            lambda e: e.tensor_reduce(t8[:, 3:4], web[:], AX.X, ALU.add),
                        ], waits=[(se, t_we)], inc=ve)
                        t_rs = ctx.chain("scalar", [
                            lambda e: e.activation(t8[:, 4:5], t8[:, 3:4], AF.Ln),
                            lambda e: e.activation(t8[:, 4:5], t8[:, 4:5], AF.Exp,
                                                   scale=-1.0),
                        ], waits=[(ve, t_w)], inc=se)
                        wcol = w_st[:, g * 8:(g + 1) * 8]
                        t_cw = ctx.chain("vector", [
                            lambda e, wcol=wcol: e.tensor_scalar(
                                wcol, web[:], t8[:, 4:5], None, ALU.mult),
                            lambda e, wcol=wcol: e.tensor_copy(wbf[:], wcol),
                        ], waits=[(se, t_rs)], inc=ve)
                        t_wt = ctx.op("tensor", lambda e: e.transpose(
                            ps_tr[0:8, 0:128], wbf[:], cm.ident[:]),
                            waits=[(ve, t_cw), (ve, ptr_free)], inc=pe)
                        t_wte = ctx.op("vector", lambda e, g=g: e.tensor_copy(
                            wT_st[:8, g * 128:(g + 1) * 128], ps_tr[:8, 0:128]),
                            waits=[(pe, t_wt)], inc=ve)
                        ptr_free = t_wte
                        w_toks[("w", g)] = t_wte
                for fc in range(32):
                    def mm1(e, fc=fc):
                        rr = None
                        for dc in range(8):
                            rr = e.matmul(psF[:],
                                          ew1_sb[:, dc * 4096 + fc * 128:
                                                 dc * 4096 + fc * 128 + 128],
                                          h2T[:, dc * 256:(dc + 1) * 256],
                                          start=(dc == 0), stop=(dc == 7))
                        return rr
                    tA = ctx.op("tensor", mm1,
                                waits=[(gld, tok_w1), (ve, h2_toks[-1]),
                                       (se, psF_free)],
                                inc=pe)
                    t_g = ctx.op("scalar", lambda e, fc=fc: e.activation(
                        midF[:, fc * 256:(fc + 1) * 256], psF[:], AF.Gelu,
                        bias=eb1_sb[:, fc:fc + 1]),
                        waits=[(pe, tA), (gld, tok_b1), (pe, w_toks.get("mf_last", 0))],
                        inc=se)
                    psF_free = t_g
                for j, g in enumerate(tis):
                    ps = psO if j == 0 else psO2
                    if e_ == 0:
                        aw = ctx.op("sync", lambda e, g=g, j=j: e.dma_start(
                            out=ac[:, j * D:(j + 1) * D],
                            in_=x_src[g * 128:(g + 1) * 128, :]),
                            waits=[(st_s, ac_free.get(j, 0))], inc=ld, inc_k=16)
                        t_in = ctx.op("vector", lambda e, j=j: e.tensor_add(
                            ac[:, j * D:(j + 1) * D], ac[:, j * D:(j + 1) * D],
                            ab[:, j * D:(j + 1) * D]),
                            waits=[(ld, aw)], inc=ve)
                        ab_free[j] = t_in
                    else:
                        aw = ctx.op("sync", lambda e, g=g, j=j: e.dma_start(
                            out=ac[:, j * D:(j + 1) * D],
                            in_=x_dst[g * 128:(g + 1) * 128, :]),
                            waits=[(st_s, ac_free.get(j, 0)),
                                   (st_s, store_toks.get((e_ - 1, g), 0))],
                            inc=ld, inc_k=16)
                        t_in = None
                    def mm2(e, j=j, ps=ps, g=g, e_=e_):
                        rr = None
                        for n2 in range(2):
                            for fc in range(32):
                                rr = e.matmul(ps[:, n2 * 512:(n2 + 1) * 512],
                                              midF[:, fc * 256 + j * 128:
                                                   fc * 256 + j * 128 + 128],
                                              ew2_sb[:, fc * 1024 + n2 * 512:
                                                     fc * 1024 + (n2 + 1) * 512],
                                              start=(fc == 0),
                                              stop=(fc == 31 and e_ != NE - 1))
                            if e_ == NE - 1:
                                rr = e.matmul(ps[:, n2 * 512:(n2 + 1) * 512],
                                              wT_st[:8, g * 128:(g + 1) * 128],
                                              eb2m[:8, n2 * 512:(n2 + 1) * 512],
                                              start=False, stop=True)
                        return rr
                    t_mm = ctx.op("tensor", mm2,
                                  waits=[(gld, tok_w2), (se, psF_free),
                                         (ve, psO_free[j]),
                                         (ve, w_toks.get(("w", g), 0))],
                                  inc=pe)
                    wcol = w_st[:, g * 8 + e_:g * 8 + e_ + 1]
                    waits_acc = [(pe, t_mm), (ld, aw)]
                    if t_in is not None:
                        waits_acc.append((ve, t_in))
                    t_acc = ctx.op("vector", lambda e, j=j, ps=ps, wcol=wcol:
                                   e.scalar_tensor_tensor(
                                       ac[:, j * D:(j + 1) * D], ps[:], wcol,
                                       ac[:, j * D:(j + 1) * D], ALU.mult, ALU.add),
                                   waits=waits_acc, inc=ve)
                    psO_free[j] = t_acc
                    store_toks[(e_, g)] = ctx.op(
                        "sync", lambda e, g=g, j=j: e.dma_start(
                            out=x_dst[g * 128:(g + 1) * 128, :],
                            in_=ac[:, j * D:(j + 1) * D]),
                        waits=[(ve, t_acc)], inc=st_s, inc_k=16)
                    ac_free[j] = store_toks[(e_, g)]
                w_toks["mf_last"] = pe.v
            w_toks[("w1f", e_)] = pe.v
            w_toks[("w2f", e_)] = pe.v
            w_toks[("b1f", e_)] = se.v
        ctx.op("sync", lambda e: None, waits=[(st_s, st_s.v)])


def emit_l3_ffn(ctx, cm, wap, x_src, x_dst):
    nc = ctx.nc
    hbuf = nc.dram_tensor("p7_hbuf", [NT, D], F32, kind="Internal").ap()
    abuf = nc.dram_tensor("p7_abuf", [NT, D], F32, kind="Internal").ap()
    emit_softmax_attention(ctx, cm, wap, dict(
        name="p7a", wq="l3f_wq", wk="l3f_wk", wv="l3f_wv", wo="l3f_wo",
        bias=("l3f_bq", "l3f_bk", "l3f_bv", "l3f_bo"), causal=True,
        affine=("l3f_g1", "l3f_b1"), save_h=hbuf, res_src="h"),
        x_src, abuf)
    st_a = ctx.sem("p7a_st")
    emit_l3_moe(ctx, cm, wap, abuf, x_src, x_dst, st_a, st_a.v)


def emit_l3_attn(ctx, cm, wap, x_src, x_dst):
    emit_softmax_attention(ctx, cm, wap, dict(
        name="p6", wq="l3a_wq", wk="l3a_wk", wv="l3a_wv", wo=None), x_src, x_dst)


def emit_l2_attn(ctx, cm, wap, x_src, x_dst):
    emit_softmax_attention(ctx, cm, wap, dict(
        name="p4", wq="l2a_wqT", wk="l2a_wkT", wv="l2a_wvT", wo="l2a_woT",
        rope=True), x_src, x_dst)


def emit_l1_attn(ctx, cm, wap, x_src, x_dst):
    emit_softmax_attention(ctx, cm, wap, dict(
        name="p2", wq="l1a_wq", wk="l1a_wk", wv="l1a_wv", wo="l1a_wo",
        bias=("l1a_bq", "l1a_bk", "l1a_bv", "l1a_bo"), toep="l1a_toep"),
        x_src, x_dst)



PHASE_EMITTERS = {}


def _register():
    PHASE_EMITTERS["l2_ffn"] = (emit_l2_ffn,
                                ["l2f_rwT", "l2f_rb", "l2f_ewT", "l2f_eb"])
    PHASE_EMITTERS["l0_attn"] = (emit_l0_attn,
                                 ["l0a_wqT", "l0a_wkT", "l0a_wvT", "l0a_woT",
                                  "l0a_qg_full", "l0a_qb_full", "l0a_kg_full",
                                  "l0a_kb_full"])
    PHASE_EMITTERS["l3_attn"] = (emit_l3_attn, ["l3a_wq", "l3a_wk", "l3a_wv"])
    PHASE_EMITTERS["l3_ffn"] = (emit_l3_ffn,
                                ["l3f_g1", "l3f_b1", "l3f_g2", "l3f_b2",
                                 "l3f_wq", "l3f_bq", "l3f_wk", "l3f_bk",
                                 "l3f_wv", "l3f_bv", "l3f_wo", "l3f_bo",
                                 "l3f_wr", "l3f_br", "l3f_ew1", "l3f_eb1T",
                                 "l3f_ew2", "l3f_eb2"])
    PHASE_EMITTERS["l2_attn"] = (emit_l2_attn,
                                 ["l2a_wqT", "l2a_wkT", "l2a_wvT", "l2a_woT"])
    PHASE_EMITTERS["l1_attn"] = (emit_l1_attn,
                                 ["l1a_wq", "l1a_wk", "l1a_wv", "l1a_wo",
                                  "l1a_bq", "l1a_bk", "l1a_bv", "l1a_bo", "l1a_toep"])
    PHASE_EMITTERS["l1_ffn"] = (emit_l1_ffn,
                                ["l1f_rw", "l1f_rb", "l1f_w1p", "l1f_b1p",
                                 "l1f_w2", "l1f_b2m", "l1f_sw1", "l1f_sb1",
                                 "l1f_sw2", "l1f_sb2"])
    PHASE_EMITTERS["l0_ffn"] = (emit_l0_ffn,
                                ["l0f_w12T", "l0f_w3T", "l0f_sw1T", "l0f_sw2T",
                                 "l0f_sw3T", "l0f_rwT", "l0f_eb"])


_register()

COMMON_NAMES = ["c_ident", "c_identf", "c_eps", "c_negbig", "c_ones1", "c_ones128",
                "c_triu", "c_causal", "c_rope_cos", "c_rope_sin"]

PHASE_ORDER = ["l0_attn", "l0_ffn", "l1_attn", "l1_ffn",
               "l2_attn", "l2_ffn", "l3_attn", "l3_ffn"]


def build_program(phases, Wshapes):
    nc = bass.Bass("TRN2", target_bir_lowering=False)
    es = contextlib.ExitStack()
    nc._es = es
    ctx = Ctx(nc, es)

    x_in = nc.dram_tensor("x_in", [NT, D], F32, kind="ExternalInput").ap()
    x_out = nc.dram_tensor("x_out", [NT, D], F32, kind="ExternalOutput").ap()
    if len(phases) > 1:
        xres = nc.dram_tensor("xres", [NT, D], F32, kind="Internal").ap()
    names = set(COMMON_NAMES)
    for ph in phases:
        names.update(PHASE_EMITTERS[ph][1])
    wap = {}
    for n in sorted(names):
        sh, dt = Wshapes[n]
        wap[n] = nc.dram_tensor(n, list(sh), dt, kind="ExternalInput").ap()

    cm = Common(ctx, es, wap)
    for k, ph in enumerate(phases):
        src = x_in if k == 0 else xres
        dst = x_out if k == len(phases) - 1 else xres
        PHASE_EMITTERS[ph][0](ctx, cm, wap, src, dst)
        if k != len(phases) - 1:
            ctx.barrier(f"ph{k}")

    with nc.Block() as block:
        for st in STREAMS:
            ops = ctx.streams[st]
            getattr(block, st)(lambda eng, ops=ops: [f(eng) for f in ops])
    es.close()
    return nc, sorted(names)


def _np_dt(dt):
    return F32 if dt == np.float32 else BF16


def run_phases(phase_list, x, W, fused=False):
    Wshapes = {k: (v.shape, F32 if v.dtype == np.float32 else BF16)
               for k, v in W.items()}
    x = np.ascontiguousarray(x, dtype=np.float32)
    groups = [phase_list] if fused else [[p] for p in phase_list]
    for grp in groups:
        nc, names = build_program(grp, Wshapes)
        in_maps = []
        for c in range(NCORES):
            m = {n: W[n] for n in names}
            m["x_in"] = x[2 * c:2 * c + 2].reshape(NT, D)
            in_maps.append(m)
        res = run_bass_kernel_spmd(nc, in_maps, core_ids=list(range(NCORES)))
        x = np.stack([r["x_out"].reshape(2, TS, D) for r in res.results])
        x = x.reshape(16, TS, D)
    return x


def kernel(**inputs):
    W = stage_weights(inputs)
    x = np.asarray(inputs["L0_attn_x"], dtype=np.float32)
    x = run_phases(PHASE_ORDER, x, W, fused=True)
    return x
